# version 03
# baseline (speedup 1.0000x reference)
import math
import numpy as np
from contextlib import ExitStack

import concourse.bass as bass
import concourse.mybir as mybir
from concourse.bass_utils import run_bass_kernel_spmd
from concourse.bass_core_rust import IndirectOffsetOnAxis

F32 = mybir.dt.float32
BF16 = mybir.dt.bfloat16
I32 = mybir.dt.int32
AF = mybir.ActivationFunctionType
ALU = mybir.AluOpType
AX = mybir.AxisListType

B, T, D = 8, 2048, 1024
NDT = D // 128

ENGINES = ("tensor", "vector", "scalar", "gpsimd", "sync")
QUEUES = {"qsync": "sync", "qgp": "gpsimd", "qscal": "scalar"}
NSLOT = 3
QSLOTS = [q + "#" + str(i) for q in QUEUES for i in range(NSLOT)]


class Sched:

    def __init__(self, nc):
        self.nc = nc
        self.block = None
        self.count = {e: 0 for e in ENGINES}
        self.dma_count = {q: 0 for q in QUEUES}
        self.slot_count = {s: 0 for s in QSLOTS}
        self.sem = {}
        self.last_writer = {}
        self.readers = {}
        self.vc = {e: {} for e in ENGINES}
        self.snaps = {c: [] for c in list(ENGINES) + QSLOTS}
        self.n_waits = 0
        self.n_ops = 0

    def setup(self, es, block):
        for name in list(ENGINES) + QSLOTS:
            self.sem[name] = es.enter_context(self.nc.semaphore("sem_" + name))
        self.block = block

    def _snap_at(self, clk, val):
        lst = self.snaps[clk]
        lo, hi = 0, len(lst)
        while lo < hi:
            mid = (lo + hi) // 2
            if lst[mid][0] <= val:
                lo = mid + 1
            else:
                hi = mid
        return lst[lo - 1][1] if lo > 0 else {}

    def _collect_wait(self, eng, clk, val, waits):
        if self.vc[eng].get(clk, 0) >= val:
            return
        waits[clk] = max(waits.get(clk, 0), val)

    def _apply_wait(self, eng, clk, val):
        snap = self._snap_at(clk, val)
        vc = self.vc[eng]
        for k, v in snap.items():
            if vc.get(k, 0) < v:
                vc[k] = v
        if vc.get(clk, 0) < val:
            vc[clk] = val

    @staticmethod
    def _kconf(a, b):
        if a[0] != b[0]:
            return False
        for i in (1, 2):
            ra = a[i] if len(a) > i else None
            rb = b[i] if len(b) > i else None
            if ra is None or rb is None:
                continue
            if not (isinstance(ra, tuple) and isinstance(rb, tuple)):
                return True
            if ra[1] <= rb[0] or rb[1] <= ra[0]:
                return False
        return True

    def _conflicts(self, a):
        out = []
        for k in self.last_writer:
            if self._kconf(k, a):
                out.append(k)
        for k in self.readers:
            if k not in self.last_writer and self._kconf(k, a):
                out.append(k)
        return out

    def op(self, eng, fn, reads=(), writes=(), queue=None):
        assert eng in ENGINES
        if PARANOID:
            self.barrier_all()
        clk = queue if queue is not None else eng
        waits = {}
        def edge(src_clk, src_val):
            if src_clk == eng == "tensor":
                return
            self._collect_wait(eng, src_clk, src_val, waits)
        for r in reads:
            for k in self._conflicts(r):
                w = self.last_writer.get(k)
                if w is not None:
                    edge(w[0], w[1])
        for wkey in writes:
            for k in self._conflicts(wkey):
                w = self.last_writer.get(k)
                if w is not None:
                    edge(w[0], w[1])
                for r in self.readers.get(k, ()):
                    edge(r[0], r[1])
        if queue is not None:
            slot = self.dma_count[queue] % NSLOT
            clk = queue + "#" + str(slot)
            if self.slot_count[clk]:
                self._collect_wait(eng, clk, 16 * self.slot_count[clk], waits)
            self.dma_count[queue] += 1
            self.slot_count[clk] += 1
            myval = 16 * self.slot_count[clk]
            inc = 16
        else:
            self.count[eng] += 1
            myval = self.count[eng]
            inc = 1
        for ck, cv in waits.items():
            self._apply_wait(eng, ck, cv)
        self.n_waits += len(waits)
        self.n_ops += 1
        sem = self.sem

        def body(engobj):
            for ck, cv in waits.items():
                engobj.wait_ge(sem[ck], cv)
            last = fn(engobj)
            last.then_inc(sem[clk], inc)

        getattr(self.block, eng)(body)

        vc_now = dict(self.vc[eng])
        vc_now[clk] = myval
        if queue is not None:
            vc_now[eng] = self.count[eng]
        self.snaps[clk].append((myval, vc_now))
        for r in reads:
            self.readers.setdefault(r, []).append((clk, myval))
        for wkey in writes:
            self.last_writer[wkey] = (clk, myval)
            self.readers[wkey] = []
        return myval

    def raw_wait(self, eng, clk, val):
        if self.vc[eng].get(clk, 0) >= val:
            return
        sem = self.sem

        def body(engobj):
            engobj.wait_ge(sem[clk], val)

        getattr(self.block, eng)(body)
        self._apply_wait(eng, clk, val)

    def barrier_all(self):
        for e in ENGINES:
            for s in QSLOTS:
                if self.slot_count[s]:
                    self.raw_wait(e, s, 16 * self.slot_count[s])
            for e2 in ENGINES:
                if e2 != e and self.count[e2]:
                    self.raw_wait(e, e2, self.count[e2])

    def final_wait(self):
        for s in QSLOTS:
            tot = 16 * self.slot_count[s]
            if tot:
                self.raw_wait("gpsimd" if not s.startswith("qgp") else "sync",
                              s, tot)


class ConstMgr:
    def __init__(self):
        self.scalars = {}

    def scalar(self, v):
        v = float(np.float32(v))
        if v not in self.scalars:
            self.scalars[v] = len(self.scalars)
        return self.scalars[v]

    def col(self, ctx, v):
        i = self.scalar(v)
        assert i < CPOOL_N
        return ctx.t["cpool"][:, i:i + 1]

    def pool(self, n):
        vals = np.zeros((128, n), np.float32)
        for v, i in self.scalars.items():
            vals[:, i] = v
        return vals


CPOOL_N = 32
DBG_DUMP = False
ATTN0_STOP = 99
ATTN2_BARRIER = False
PARANOID = False
FFN3_STOP = 99
FFN1_STOP = 99


class Ctx:
    def __init__(self, nc, es, sched, consts, weights):
        self.nc = nc
        self.es = es
        self.s = sched
        self.c = consts
        self.w = weights
        self.sbuf_bytes = 0
        self.t = {}
        self.dram = {}
        self.psum = {}

    def sb(self, name, shape, dtype=F32):
        t = self.es.enter_context(self.nc.sbuf_tensor(name, list(shape), dtype))
        self.sbuf_bytes += int(np.prod(shape[1:])) * (2 if dtype == BF16 else 4)
        self.t[name] = t
        return t

    def ps(self, name, cols=512, dtype=F32):
        t = self.es.enter_context(self.nc.psum_tensor([128, cols], dtype))
        self.psum[name] = t
        return t

    def dscr(self, name, shape, dtype=F32):
        t = self.nc.dram_tensor(name, list(shape), dtype, kind="Internal")
        self.dram[name] = t
        return t



def h_dma(ctx, dst, src, dkeys, skeys=(), queue="qsync"):
    eng = QUEUES[queue]
    ctx.s.op(eng, lambda e: e.dma_start(out=dst, in_=src),
             reads=tuple(skeys), writes=tuple(dkeys), queue=queue)


def h_dmaT(ctx, dst, src, dkeys, skeys=(), queue="qscal"):
    eng = QUEUES[queue]
    ctx.s.op(eng, lambda e: e.dma_start_transpose(dst, src),
             reads=tuple(skeys), writes=tuple(dkeys), queue=queue)


def h_mm(ctx, pairs, psum_ap, pkey, rkeys, start=True, stop=True):
    n = len(pairs)

    def fn(e):
        last = None
        for i, (l, r) in enumerate(pairs):
            last = e.matmul(psum_ap, l, r,
                            start=(start and i == 0), stop=(stop and i == n - 1))
        return last

    ctx.s.op("tensor", fn, reads=tuple(rkeys), writes=(pkey,))


def h_tr(ctx, out_psum, in_ap, pkey, rkeys, ident=None):
    ident = ident if ident is not None else ctx.t["ident_bf"][:, :]
    ctx.s.op("tensor", lambda e: e.transpose(out_psum, in_ap, ident),
             reads=tuple(rkeys), writes=(pkey,))


def h_act(ctx, out, in_, func, okeys, rkeys, bias=0.0, scale=1.0,
          accum_out=None):
    def fn(e):
        return e.activation(out, in_, func, bias=bias, scale=scale,
                            accum_out=accum_out)

    ctx.s.op("scalar", fn, reads=tuple(rkeys), writes=tuple(okeys))


def h_v(ctx, fn, okeys, rkeys, eng="vector"):
    ctx.s.op(eng, fn, reads=tuple(rkeys), writes=tuple(okeys))


def xcols(k, c0=0, n=T):
    return slice(k * T + c0, k * T + c0 + n)



def build_norm(ctx, xsrc, xkey, out_bf, outkey, kind="ln", eps=1e-5, CH=512,
               ndt=NDT):
    ones = ctx.t["ones_f32"]
    sq = ctx.t["wbuf_f32"]
    rows = ctx.t["rowpool"]
    p_st = ctx.psum["p_stat"]
    p_bc = ctx.psum["p_bc"]
    inv_n = 1.0 / (128 * ndt)
    for c0 in range(0, T, CH):
        for k in range(ndt):
            h_act(ctx, sq[:, k * CH:(k + 1) * CH], xsrc[:, xcols(k, c0, CH)],
                  AF.Square, [("wbuf_f32", (k * CH, (k + 1) * CH))], [(xkey, (k * T + c0, k * T + c0 + CH))])
        if kind == "ln":
            h_mm(ctx, [(ones[:, 0:1], xsrc[:, xcols(k, c0, CH)])
                       for k in range(ndt)],
                 p_st[0:1, 0:CH], ("p_stat", (0, 512)), [(xkey,)])
        h_mm(ctx, [(ones[:, 0:1], sq[:, k * CH:(k + 1) * CH]) for k in range(ndt)],
             p_st[0:1, CH:2 * CH], ("p_stat", (CH, 2 * CH)),
             [("wbuf_f32", (k * CH, (k + 1) * CH)) for k in range(ndt)])
        mrow = rows[0:1, 0:CH]
        vrow = rows[0:1, CH:2 * CH]
        rrow = rows[0:1, 2 * CH:3 * CH]
        nrow = rows[0:1, 3 * CH:4 * CH]
        if kind == "ln":
            h_v(ctx, lambda e: e.tensor_scalar_mul(mrow, p_st[0:1, 0:CH], inv_n),
                [("rows", (0, 512))], [("p_stat", (0, 512))])
            h_act(ctx, vrow, mrow, AF.Square, [("rows", (512, 1024))], [("rows", (0, 512))])
            h_v(ctx, lambda e: e.scalar_tensor_tensor(
                vrow, p_st[0:1, CH:2 * CH], inv_n, vrow, ALU.mult, ALU.subtract),
                [("rows", (512, 1024))], [("p_stat", (CH, 2 * CH)), ("rows", (512, 1024))])
        else:
            h_v(ctx, lambda e: e.tensor_scalar_mul(vrow, p_st[0:1, CH:2 * CH],
                                                   inv_n),
                [("rows", (512, 1024))], [("p_stat", (CH, 2 * CH))])
        h_act(ctx, rrow, vrow, AF.Sqrt, [("rows", (1024, 1536))], [("rows", (512, 1024))],
              bias=ctx.c.col(ctx, eps)[0:1, 0:1])
        h_v(ctx, lambda e: e.reciprocal(rrow, rrow), [("rows", (1024, 1536))], [("rows", (1024, 1536))])
        h_mm(ctx, [(ones[0:1, :], rrow)], p_bc[:, 0:CH], ("p_bc", (0, 256)),
             [("rows", (1024, 1536))])
        if kind == "ln":
            h_v(ctx, lambda e: e.tensor_mul(nrow, mrow, rrow), [("rows", (1536, 2048))],
                [("rows", (0, 512)), ("rows", (1024, 1536))])
            h_mm(ctx, [(ones[0:1, :], nrow)], p_bc[:, CH:2 * CH], ("p_bc", (256, 512)),
                 [("rows", (1536, 2048))])
            for k in range(ndt):
                def f1(e, k=k, c0=c0):
                    return e.tensor_tensor(sq[:, k * CH:(k + 1) * CH],
                                           xsrc[:, xcols(k, c0, CH)],
                                           p_bc[:, 0:CH], ALU.mult)

                h_v(ctx, f1, [("wbuf_f32", (k * CH, (k + 1) * CH))],
                    [(xkey, (k * T + c0, k * T + c0 + CH)), ("p_bc", (0, 256))])

                def f2(e, k=k, c0=c0):
                    return e.tensor_tensor(out_bf[:, xcols(k, c0, CH)],
                                           sq[:, k * CH:(k + 1) * CH],
                                           p_bc[:, CH:2 * CH], ALU.subtract)

                h_v(ctx, f2, [(outkey, (k * T + c0, k * T + c0 + CH))],
                    [("wbuf_f32", (k * CH, (k + 1) * CH)), ("p_bc", (256, 512))])
        else:
            for k in range(ndt):
                def f1(e, k=k, c0=c0):
                    return e.tensor_tensor(out_bf[:, xcols(k, c0, CH)],
                                           xsrc[:, xcols(k, c0, CH)],
                                           p_bc[:, 0:CH], ALU.mult)

                h_v(ctx, f1, [(outkey, (k * T + c0, k * T + c0 + CH))],
                    [(xkey, (k * T + c0, k * T + c0 + CH)), ("p_bc", (0, 256))])



def build_l0ffn_prep(ctx):
    w1t = ctx.w["L0_ffn_W1T"]
    w2t = ctx.w["L0_ffn_W2T"]
    w1qt = ctx.dscr("w1qt", (D, 4 * D), BF16).ap()
    w2qt = ctx.dscr("w2qt", (4 * D, D), BF16).ap()
    wl = ctx.t["wbuf_f32"]
    acc = ctx.t["scr_a"]
    rows = ctx.t["rowpool"]
    p_a = ctx.psum["p_stat"]
    p_t = ctx.psum["p_bc"]
    qt = ctx.t["bigb_bf"]
    for (wsrc, nrow_, ncol, qdst, tag) in (
            (w1t, D, 4 * D, w1qt, "q1"), (w2t, 4 * D, D, w2qt, "q2")):
        ntile = nrow_ // 128
        for i in range(ntile):
            h_dma(ctx, wl[:, 0:ncol], wsrc[128 * i:128 * (i + 1), :],
                  [("wbuf_f32",)], queue="qsync")

            def fred(e, i=i, ncol=ncol):
                return e.tensor_reduce(acc[:, i:i + 1], wl[:, 0:ncol], AX.X,
                                       ALU.add, apply_absolute_value=True)

            h_v(ctx, fred, [("scr_a", (i, i + 1))], [("wbuf_f32",)])

        def ftot(e, ntile=ntile):
            return e.tensor_reduce(acc[:, 64:65], acc[:, 0:ntile], AX.X, ALU.add)

        h_v(ctx, ftot, [("scr_a", (64, 65))],
            [("scr_a", (i, i + 1)) for i in range(ntile)])
        h_mm(ctx, [(ctx.t["ones_f32"][:, 0:1], acc[:, 64:65])], p_a[0:1, 0:1],
             ("p_stat", (0, 512)), [("scr_a", (64, 65))])

        def fthr(e, n=nrow_ * ncol):
            return e.tensor_scalar(rows[0:1, 0:1], p_a[0:1, 0:1], 0.5 / n,
                                   0.5e-5, ALU.mult, ALU.add)

        h_v(ctx, fthr, [("rows", (0, 1))], [("p_stat", (0, 512))])
        h_mm(ctx, [(ctx.t["ones_f32"][0:1, :], rows[0:1, 0:1])], p_t[:, 0:1],
             ("p_bc", (0, 512)), [("rows", (0, 1))])
        cols = ctx.t["cols_sb"]
        h_v(ctx, lambda e: e.tensor_copy(cols[:, 0:1], p_t[:, 0:1]),
            [("cols_sb", (0, 1))], [("p_bc", (0, 256))])
        h_v(ctx, lambda e: e.tensor_scalar_mul(cols[:, 1:2], cols[:, 0:1],
                                               -1.0),
            [("cols_sb", (1, 2))], [("cols_sb", (0, 1))])
        for i in range(ntile):
            h_dma(ctx, wl[:, 0:ncol], wsrc[128 * i:128 * (i + 1), :],
                  [("wbuf_f32",)], queue="qsync")

            def fle(e, ncol=ncol):
                return e.tensor_scalar(qt[:, 4096:4096 + ncol], wl[:, 0:ncol],
                                       cols[:, 1:2], None, ALU.is_le)

            h_v(ctx, fle, [("bigb_bf", (4096, 8192))],
                [("wbuf_f32",), ("cols_sb", (1, 2))])

            def fge(e, ncol=ncol):
                return e.tensor_scalar(wl[:, 0:ncol], wl[:, 0:ncol],
                                       cols[:, 0:1], None, ALU.is_ge)

            h_v(ctx, fge, [("wbuf_f32",)], [("wbuf_f32",), ("cols_sb", (0, 1))])

            def fsub(e, ncol=ncol):
                return e.tensor_tensor(qt[:, 0:ncol], wl[:, 0:ncol],
                                       qt[:, 4096:4096 + ncol], ALU.subtract)

            h_v(ctx, fsub, [("bigb_bf", (0, 4096))],
                [("wbuf_f32",), ("bigb_bf", (4096, 8192))])
            h_dma(ctx, qdst[128 * i:128 * (i + 1), :], qt[:, 0:ncol],
                  [(tag,)], [("bigb_bf", (0, 4096))], queue="qsync")


def build_ffn0(ctx):
    x = ctx.t["x_res"]
    xb = ctx.t["xln_bf"]
    build_norm(ctx, x, "x_res", xb, "xln_bf", kind="ln")
    w1qt = ctx.dram["w1qt"].ap()
    w2qt = ctx.dram["w2qt"].ap()
    b1 = ctx.t["b1_sb"]
    hbuf = ctx.t["bigb_bf"]
    wband = [ctx.t["wband0"], ctx.t["wband1"]]
    p_h = [ctx.psum["p_a"], ctx.psum["p_b"]]
    p_o = [ctx.psum["p_c"], ctx.psum["p_d"]]
    CH = 512
    for c0 in range(0, T, CH):
        for fc in range(8):
            wb = wband[fc % 2]
            wbk = ("wband0",) if fc % 2 == 0 else ("wband1",)
            h_dma(ctx, wb[:, :].rearrange("p (g f) -> p g f", g=NDT),
                  w1qt[:, 512 * fc:512 * (fc + 1)].rearrange(
                      "(g p) f -> p g f", p=128),
                  [wbk], [("q1",)], queue="qsync")
            for fs in range(4):
                fi = 4 * fc + fs
                pp = p_h[fs % 2]
                ppk = ("p_a",) if fs % 2 == 0 else ("p_b",)
                pairs = [(wb[:, 512 * g + 128 * fs:512 * g + 128 * (fs + 1)],
                          xb[:, xcols(g, c0, CH)]) for g in range(NDT)]
                h_mm(ctx, pairs, pp[:, 0:CH], ppk,
                     [wbk] + [("xln_bf", (g * T + c0, g * T + c0 + CH))
                              for g in range(NDT)])
                h_act(ctx, hbuf[:, CH * fi:CH * (fi + 1)], pp[:, 0:CH], AF.Relu,
                      [("bigb_bf", (CH * fi, CH * (fi + 1)))], [ppk, ("b1_sb",)],
                      bias=b1[:, fi:fi + 1])
        for dd in range(NDT):
            wb = wband[dd % 2]
            wbk = ("wband0",) if dd % 2 == 0 else ("wband1",)
            h_dma(ctx, wb[:, :].rearrange("p (g c) -> p g c", g=32),
                  w2qt[:, 128 * dd:128 * (dd + 1)].rearrange(
                      "(g p) c -> p g c", p=128),
                  [wbk], [("q2",)], queue="qsync")
            po = p_o[dd % 2]
            pok = ("p_c",) if dd % 2 == 0 else ("p_d",)
            pairs = [(wb[:, 128 * fi:128 * (fi + 1)],
                      hbuf[:, CH * fi:CH * (fi + 1)]) for fi in range(32)]
            h_mm(ctx, pairs, po[:, 0:CH], pok,
                 [wbk] + [("bigb_bf", (CH * fi, CH * (fi + 1))) for fi in range(32)])

            def facc(e, dd=dd, c0=c0, po=po):
                return e.scalar_tensor_tensor(
                    x[:, xcols(dd, c0, CH)], po[:, 0:CH], 0.5,
                    x[:, xcols(dd, c0, CH)], ALU.mult, ALU.add)

            xk = ("x_res", (dd * T + c0, dd * T + c0 + CH))
            h_v(ctx, facc, [xk], [pok, xk])
    b2h = ctx.t["b2h_sb"]
    for dd in range(NDT):
        def fb2(e, dd=dd):
            return e.tensor_scalar(x[:, xcols(dd)], x[:, xcols(dd)],
                                   b2h[:, dd:dd + 1], None, ALU.add)

        h_v(ctx, fb2, [("x_res", (dd * T, (dd + 1) * T))],
            [("x_res", (dd * T, (dd + 1) * T)), ("b2h_sb",)])




def load_band(ctx, dst_sb, dkey, w_ap, c0, c1, nrows=D, queue="qgp", skeys=()):
    g = nrows // 128
    h_dma(ctx, dst_sb.rearrange("p (g c) -> p g c", g=g),
          w_ap[:, c0:c1].rearrange("(g p) c -> p g c", p=128),
          [dkey], skeys, queue=queue)


def build_attn0(ctx):
    x = ctx.t["x_res"]
    xb = ctx.t["xln_bf"]
    build_norm(ctx, x, "x_res", xb, "xln_bf", kind="ln")
    H, DK = 16, 64
    bigb = ctx.t["bigb_bf"]
    qt = ctx.t["qtile_bf"]
    sm = ctx.t["smalltok"]
    wrow = ctx.t["wbuf_f32"]
    msb = ctx.t["msb_bf"]
    sel = ctx.t["sel16"]
    wb0, wb1 = ctx.t["wband0"], ctx.t["wband1"]
    p_a, p_b, p_c, p_d = (ctx.psum[n] for n in ("p_a", "p_b", "p_c", "p_d"))
    p_trb = ctx.psum["p_d"].bitcast(BF16)
    scr = ctx.t["scr_a"]
    cols = ctx.t["cols_sb"]
    onesf = ctx.t["ones_f32"]
    idf = ctx.t["ident_f32"]
    idb = ctx.t["ident_bf"]

    def KEY(name, c0, c1, p0=0, p1=128):
        return (name, (c0, c1), (p0, p1))

    if ATTN0_STOP <= 0:
        return
    h_dma(ctx, ctx.t["rowpool"][:, 0:512], ctx.w["btf1_t_in"][:, :],
          [("rows", (0, 512))])
    h_dma(ctx, ctx.t["rowpool"][:, 512:1024], ctx.w["wtf2_t_in"][:, :],
          [("rows", (512, 1024))])
    load_band(ctx, wb0[:, 0:384], ("wband0",), ctx.w["L0_attn_Wbfs"], 0, 48)
    if ATTN0_STOP <= 0.5:
        return
    for tt in range(16):
        c0 = tt * 128
        pairs = [(xb[:, xcols(g, c0, 128)], wb0[:, 48 * g:48 * (g + 1)])
                 for g in range(NDT)]
        h_mm(ctx, pairs, p_a[:, 0:48], ("p_a",),
             [("wband0",)] + [("xln_bf", (g * T + c0, g * T + c0 + 128))
                              for g in range(NDT)])
        h_act(ctx, sm[:, 16 * tt:16 * (tt + 1)], p_a[:, 0:16], AF.Sigmoid,
              [KEY("smalltok", 16 * tt, 16 * tt + 16)], [("p_a",)])
        for j, (bt, prow) in enumerate(
                ((ctx.t["bfd_t"], 0), (ctx.t["bsd_t"], 32))):
            def fadd(e, j=j, bt=bt):
                return e.tensor_tensor(scr[:, 16 * j:16 * (j + 1)],
                                       p_a[:, 16 * (j + 1):16 * (j + 2)],
                                       bt[:, :], ALU.add)
            h_v(ctx, fadd, [KEY("scr_a", 16 * j, 16 * j + 16)],
                [("p_a",), (("bfd_t",) if j == 0 else ("bsd_t",))])
            h_act(ctx, scr[:, 32 + 16 * j:48 + 16 * j],
                  scr[:, 16 * j:16 * (j + 1)], AF.Sigmoid,
                  [KEY("scr_a", 32 + 16 * j, 48 + 16 * j)],
                  [KEY("scr_a", 16 * j, 16 * j + 16)])
            def ftr(e, j=j):
                return e.transpose(p_b[0:16, 0:128],
                                   scr[:, 32 + 16 * j:48 + 16 * j], idf[:, :])
            ctx.s.op("tensor", ftr,
                     reads=(KEY("scr_a", 32 + 16 * j, 48 + 16 * j),
                            ("ident_f32",)),
                     writes=(("p_b",),))
            def fcp(e, tt=tt, prow=prow):
                return e.tensor_copy(wrow[prow:prow + 16, 128 * tt:128 * (tt + 1)],
                                     p_b[0:16, 0:128])
            h_v(ctx, fcp, [KEY("wbuf_f32", 128 * tt, 128 * (tt + 1), prow, prow + 16)],
                [("p_b",)])

    if ATTN0_STOP <= 1:
        return
    for j, prow in ((0, 0), (1, 32)):
        h_act(ctx, wrow[64:80, 0:2048], wrow[prow:prow + 16, 0:2048], AF.Ln,
              [KEY("wbuf_f32", 0, 2048, 64, 80)],
              [KEY("wbuf_f32", 0, 2048, prow, prow + 16), ("cpool",)],
              bias=ctx.c.col(ctx, 1e-6)[0:16, 0:1])
        def fscan(e):
            return e.tensor_tensor_scan(wrow[64:80, 0:2048], wrow[64:80, 0:2048],
                                        wrow[64:80, 0:2048], 0.0,
                                        ALU.add, ALU.bypass)
        h_v(ctx, fscan, [KEY("wbuf_f32", 0, 2048, 64, 80)],
            [KEY("wbuf_f32", 0, 2048, 64, 80)])
        def fsub(e):
            return e.tensor_scalar(wrow[64:80, 0:2048], wrow[64:80, 0:2048],
                                   wrow[64:80, 2047:2048], None, ALU.subtract)
        h_v(ctx, fsub, [KEY("wbuf_f32", 0, 2048, 64, 80)],
            [KEY("wbuf_f32", 0, 2048, 64, 80)])
        h_act(ctx, wrow[prow:prow + 16, 2048:4096], wrow[64:80, 0:2048], AF.Exp,
              [KEY("wbuf_f32", 2048, 4096, prow, prow + 16)],
              [KEY("wbuf_f32", 0, 2048, 64, 80)], scale=-1.0)
        def fred(e, prow=prow):
            return e.tensor_reduce(cols[0:16, 4:5],
                                   wrow[prow:prow + 16, 2048:4096], AX.X, ALU.add)
        h_v(ctx, fred, [KEY("cols_sb", 4, 5, 0, 16)],
            [KEY("wbuf_f32", 2048, 4096, prow, prow + 16)])
        def faddeps(e, j=j):
            return e.tensor_scalar(cols[0:16, 4:5], cols[0:16, 4:5],
                                   1e-6, None, ALU.add)
        h_v(ctx, faddeps, [KEY("cols_sb", 4, 5, 0, 16)],
            [KEY("cols_sb", 4, 5, 0, 16)])
        def frec(e, j=j):
            return e.reciprocal(cols[0:16, 2 + j:3 + j], cols[0:16, 4:5])
        h_v(ctx, frec, [KEY("cols_sb", 2 + j, 3 + j, 0, 16)],
            [KEY("cols_sb", 4, 5, 0, 16)])

    if ATTN0_STOP <= 2:
        return
    for j, prow, dst0 in ((0, 0, 256), (1, 32, 512)):
        for tt in range(16):
            def ftr(e, prow=prow, tt=tt):
                return e.transpose(p_b[0:128, 0:16],
                                   wrow[prow:prow + 16,
                                        2048 + 128 * tt:2048 + 128 * (tt + 1)],
                                   idf[prow:prow + 16, prow:prow + 16])
            ctx.s.op("tensor", ftr,
                     reads=(KEY("wbuf_f32", 2048 + 128 * tt, 2048 + 128 * (tt + 1),
                                prow, prow + 16), ("ident_f32",)),
                     writes=(("p_b",),))
            def fcp(e, dst0=dst0, tt=tt):
                return e.tensor_copy(sm[:, dst0 + 16 * tt:dst0 + 16 * (tt + 1)],
                                     p_b[0:128, 0:16])
            h_v(ctx, fcp, [KEY("smalltok", dst0 + 16 * tt, dst0 + 16 * tt + 16)],
                [("p_b",)])

    if ATTN0_STOP <= 3:
        return
    def proj_chunk_loop(wname, post):
        for c in range(2):
            wb = wb0 if c == 0 else wb1
            wbk = ("wband0",) if c == 0 else ("wband1",)
            load_band(ctx, wb[:, :], wbk, ctx.w[wname], 512 * c, 512 * (c + 1))
            for tt in range(16):
                c0 = tt * 128
                pp = p_a if tt % 2 == 0 else p_b
                pk = ("p_a",) if tt % 2 == 0 else ("p_b",)
                pairs = [(xb[:, xcols(g, c0, 128)],
                          wb[:, 512 * g:512 * (g + 1)]) for g in range(NDT)]
                h_mm(ctx, pairs, pp[:, 0:512], pk,
                     [wbk] + [("xln_bf", (g * T + c0, g * T + c0 + 128))
                              for g in range(NDT)])
                post(tt, c, pp, pk)

    def norm_scale(tt, c, pp, pk, extra_mul_slice, dst_ap, dstkey,
                   extra_key=None):
        h_act(ctx, scr[:, 0:512], pp[:, 0:512], AF.Square,
              [KEY("scr_a", 0, 512)], [pk])
        def fr(e):
            return e.tensor_reduce(scr[:, 512:520],
                                   scr[:, 0:512].rearrange("p (h d) -> p h d", d=64),
                                   AX.X, ALU.add)
        h_v(ctx, fr, [KEY("scr_a", 512, 520)], [KEY("scr_a", 0, 512)])
        h_v(ctx, lambda e: e.tensor_scalar_max(scr[:, 512:520], scr[:, 512:520],
                                               1e-24),
            [KEY("scr_a", 512, 520)], [KEY("scr_a", 512, 520)])
        h_act(ctx, scr[:, 520:528], scr[:, 512:520], AF.Sqrt,
              [KEY("scr_a", 520, 528)], [KEY("scr_a", 512, 520)])
        h_v(ctx, lambda e: e.reciprocal(scr[:, 512:520], scr[:, 520:528]),
            [KEY("scr_a", 512, 520)], [KEY("scr_a", 520, 528)])
        if extra_mul_slice is not None:
            def fm(e):
                return e.tensor_tensor(scr[:, 512:520], scr[:, 512:520],
                                       extra_mul_slice, ALU.mult)
            h_v(ctx, fm, [KEY("scr_a", 512, 520)],
                [KEY("scr_a", 512, 520), extra_key])
        def fs(e):
            a, b = bass.broadcast_tensor_aps(
                pp[:, 0:512].rearrange("p (h d) -> p h d", d=64),
                scr[:, 512:520].rearrange("p (h o) -> p h o", o=1))
            return e.tensor_tensor(dst_ap.rearrange("p (h d) -> p h d", d=64),
                                   a, b, ALU.mult)
        h_v(ctx, fs, [dstkey], [pk, KEY("scr_a", 512, 520)])

    def k_post(tt, c, pp, pk):
        norm_scale(tt, c, pp, pk, sm[:, 16 * tt + 8 * c:16 * tt + 8 * (c + 1)],
                   bigb[:, 1024 * tt + 512 * c:1024 * tt + 512 * (c + 1)],
                   KEY("bigb_bf", 1024 * tt + 512 * c, 1024 * tt + 512 * (c + 1)),
                   extra_key=KEY("smalltok", 16 * tt + 8 * c,
                                 16 * tt + 8 * (c + 1)))

    proj_chunk_loop("L0_attn_Wk", k_post)

    if ATTN0_STOP <= 4:
        return
    h_dma(ctx, ctx.t["wtf1_bf"][0:64, 0:32], ctx.w["L0_attn_Wtf1"][:, :],
          [("wtf1_bf",)], queue="qgp")
    for tt in range(16):
        for h in range(H):
            def ftr(e, tt=tt, h=h):
                return e.transpose(p_trb[0:64, 0:128],
                                   bigb[:, 1024 * tt + 64 * h:1024 * tt + 64 * (h + 1)],
                                   idb[:, :])
            ctx.s.op("tensor", ftr,
                     reads=(KEY("bigb_bf", 1024 * tt + 64 * h,
                                1024 * tt + 64 * (h + 1)), ("ident_bf",)),
                     writes=(("p_d",),))
            def fcp(e):
                return e.tensor_copy(qt[0:64, 2048:2176], p_trb[0:64, 0:128])
            h_v(ctx, fcp, [KEY("qtile_bf", 2048, 2176, 0, 64)], [("p_d",)])
            h_mm(ctx, [(qt[0:64, 2048:2176], ctx.t["wtf1_bf"][0:64, 0:32])],
                 p_c[:, 32 * h:32 * (h + 1)], ("p_c", (32 * h, 32 * h + 32)),
                 [KEY("qtile_bf", 2048, 2176, 0, 64), ("wtf1_bf",)])
        def fa(e):
            return e.tensor_tensor(scr[:, 0:512], p_c[:, 0:512],
                                   ctx.t["rowpool"][:, 0:512], ALU.add)
        h_v(ctx, fa, [KEY("scr_a", 0, 512)],
            [("p_c",), ("rows", (0, 512))])
        h_act(ctx, scr[:, 512:1024], scr[:, 0:512], AF.Silu,
              [KEY("scr_a", 512, 1024)], [KEY("scr_a", 0, 512)])
        def fm2(e):
            return e.tensor_tensor(scr[:, 512:1024], scr[:, 512:1024],
                                   ctx.t["rowpool"][:, 512:1024], ALU.mult)
        h_v(ctx, fm2, [KEY("scr_a", 512, 1024)],
            [KEY("scr_a", 512, 1024), ("rows", (512, 1024))])
        def fr2(e):
            return e.tensor_reduce(scr[:, 1024:1040],
                                   scr[:, 512:1024].rearrange(
                                       "p (h j) -> p h j", j=32), AX.X, ALU.add)
        h_v(ctx, fr2, [KEY("scr_a", 1024, 1040)], [KEY("scr_a", 512, 1024)])
        h_act(ctx, scr[:, 1040:1056], scr[:, 1024:1040], AF.Sigmoid,
              [KEY("scr_a", 1040, 1056)], [KEY("scr_a", 1024, 1040), ("btf2_t",)],
              bias=ctx.t["btf2_t"][:, 0:1])
        h_v(ctx, lambda e: e.tensor_scalar_min(scr[:, 1040:1056],
                                               scr[:, 1040:1056], 0.99),
            [KEY("scr_a", 1040, 1056)], [KEY("scr_a", 1040, 1056)])
        h_v(ctx, lambda e: e.tensor_scalar_max(scr[:, 1040:1056],
                                               scr[:, 1040:1056], 0.01),
            [KEY("scr_a", 1040, 1056)], [KEY("scr_a", 1040, 1056)])
        for j, coff, scl in ((0, 2048, 0.3), (1, 0, -0.3)):
            def fal(e, scl=scl, j=j):
                return e.tensor_scalar(scr[:, 1056 + 16 * j:1072 + 16 * j],
                                       scr[:, 1040:1056], scl, 0.5,
                                       ALU.mult, ALU.add)
            h_v(ctx, fal, [KEY("scr_a", 1056 + 16 * j, 1072 + 16 * j)],
                [KEY("scr_a", 1040, 1056)])
            def ftr2(e, j=j):
                return e.transpose(p_b[0:16, 0:128],
                                   scr[:, 1056 + 16 * j:1072 + 16 * j], idf[:, :])
            ctx.s.op("tensor", ftr2,
                     reads=(KEY("scr_a", 1056 + 16 * j, 1072 + 16 * j),
                            ("ident_f32",)),
                     writes=(("p_b",),))
            def fcp2(e, coff=coff, tt=tt):
                return e.tensor_copy(wrow[0:16,
                                          coff + 128 * tt:coff + 128 * (tt + 1)],
                                     p_b[0:16, 0:128])
            h_v(ctx, fcp2,
                [KEY("wbuf_f32", coff + 128 * tt, coff + 128 * (tt + 1), 0, 16)],
                [("p_b",)])

    if ATTN0_STOP <= 5:
        return
    for c in range(2):
        wb = wb0 if c == 0 else wb1
        wbk = ("wband0",) if c == 0 else ("wband1",)
        load_band(ctx, wb[:, :], wbk, ctx.w["L0_attn_Wv"], 512 * c, 512 * (c + 1))
        for tt in range(16):
            c0 = tt * 128
            pp = p_a if tt % 2 == 0 else p_b
            pk = ("p_a",) if tt % 2 == 0 else ("p_b",)
            pairs = [(xb[:, xcols(g, c0, 128)],
                      wb[:, 512 * g:512 * (g + 1)]) for g in range(NDT)]
            h_mm(ctx, pairs, pp[:, 0:512], pk,
                 [wbk] + [("xln_bf", (g * T + c0, g * T + c0 + 128))
                          for g in range(NDT)])
            def fvs(e, tt=tt, c=c, pp=pp):
                a, b = bass.broadcast_tensor_aps(
                    pp[:, 0:512].rearrange("p (h d) -> p h d", d=64),
                    sm[:, 16 * tt + 8 * c:16 * tt + 8 * (c + 1)].rearrange(
                        "p (h o) -> p h o", o=1))
                return e.tensor_tensor(
                    qt[:, 512:1024].rearrange("p (h d) -> p h d", d=64),
                    a, b, ALU.mult)
            h_v(ctx, fvs, [KEY("qtile_bf", 512, 1024)],
                [pk, KEY("smalltok", 16 * tt + 8 * c, 16 * tt + 8 * (c + 1))])
            for j, src0, dst0 in ((0, 256, 1024), (1, 512, 1536)):
                def fcv(e, tt=tt, c=c, j=j, src0=src0):
                    return e.tensor_copy(
                        qt[:, 2176 + 8 * j:2184 + 8 * j],
                        sm[:, src0 + 16 * tt + 8 * c:src0 + 16 * tt + 8 * (c + 1)])
                h_v(ctx, fcv, [KEY("qtile_bf", 2176 + 8 * j, 2184 + 8 * j)],
                    [KEY("smalltok", src0 + 16 * tt + 8 * c,
                         src0 + 16 * tt + 8 * (c + 1))])
                def fkf(e, tt=tt, c=c, j=j, dst0=dst0):
                    a, b = bass.broadcast_tensor_aps(
                        bigb[:, 1024 * tt + 512 * c:1024 * tt + 512 * (c + 1)]
                        .rearrange("p (h d) -> p h d", d=64),
                        qt[:, 2176 + 8 * j:2184 + 8 * j].rearrange(
                            "p (h o) -> p h o", o=1))
                    return e.tensor_tensor(
                        qt[:, dst0:dst0 + 512].rearrange("p (h d) -> p h d", d=64),
                        a, b, ALU.mult)
                h_v(ctx, fkf, [KEY("qtile_bf", dst0, dst0 + 512)],
                    [KEY("bigb_bf", 1024 * tt + 512 * c,
                         1024 * tt + 512 * (c + 1)),
                     KEY("qtile_bf", 2176 + 8 * j, 2184 + 8 * j)])
            for j in range(4):
                for typ, src0, pm, pmk in ((0, 1024, p_c, ("p_c",)),
                                           (1, 1536, p_d, ("p_d",))):
                    def fmm(e, j=j, src0=src0, pm=pm, tt=tt):
                        return e.matmul(pm[:, 128 * j:128 * (j + 1)],
                                        qt[:, src0 + 128 * j:src0 + 128 * (j + 1)],
                                        qt[:, 512 + 128 * j:512 + 128 * (j + 1)],
                                        start=(tt == 0), stop=(tt == 15))
                    ctx.s.op("tensor", fmm,
                             reads=(KEY("qtile_bf", src0 + 128 * j,
                                        src0 + 128 * (j + 1)),
                                    KEY("qtile_bf", 512 + 128 * j,
                                        512 + 128 * (j + 1))),
                             writes=(pmk + ((128 * j, 128 * (j + 1)),),))
        for hh in range(8):
            h = 8 * c + hh
            j, lane = hh // 2, hh % 2
            for typ, pm, pmk, d0 in ((0, p_c, "p_c", 0), (1, p_d, "p_d", 512)):
                def fsel(e, h=h, typ=typ):
                    return e.matmul(ctx.psum["p_bc"][0:64, 0:1],
                                    sel[:, 128 * h:128 * h + 64],
                                    cols[0:16, 2 + typ:3 + typ],
                                    start=True, stop=True)
                ctx.s.op("tensor", fsel,
                         reads=(("sel16",), KEY("cols_sb", 2 + typ, 3 + typ, 0, 16)),
                         writes=(("p_bc", (0, 1)),))
                h_v(ctx, lambda e: e.tensor_copy(cols[0:64, 5:6],
                                                 ctx.psum["p_bc"][0:64, 0:1]),
                    [KEY("cols_sb", 5, 6, 0, 64)], [("p_bc", (0, 1))])
                def fev(e, j=j, lane=lane, h=h, pm=pm, d0=d0):
                    return e.tensor_scalar(
                        msb[64 * lane:64 * (lane + 1),
                            d0 + 64 * (h // 2):d0 + 64 * (h // 2) + 64],
                        pm[64 * lane:64 * (lane + 1), 128 * j + 64 * lane:
                           128 * j + 64 * lane + 64],
                        cols[0:64, 5:6], None, ALU.mult)
                h_v(ctx, fev,
                    [KEY("msb_bf", d0 + 64 * (h // 2), d0 + 64 * (h // 2) + 64,
                         64 * lane, 64 * (lane + 1))],
                    [(pmk, (128 * j, 128 * (j + 1))),
                     KEY("cols_sb", 5, 6, 0, 64)])

    if ATTN0_STOP <= 6:
        return
    def q_post(tt, c, pp, pk):
        norm_scale(tt, c, pp, pk, None,
                   qt[:, 0:512], KEY("qtile_bf", 0, 512))
        for hh in range(8):
            h = 8 * c + hh
            pr, lane = h // 2, h % 2
            def ftr(e, hh=hh):
                return e.transpose(p_trb[0:64, 0:128],
                                   qt[:, 64 * hh:64 * (hh + 1)], idb[:, :])
            ctx.s.op("tensor", ftr,
                     reads=(KEY("qtile_bf", 64 * hh, 64 * (hh + 1)),
                            ("ident_bf",)),
                     writes=(("p_d",),))
            def fcp(e, pr=pr, lane=lane, tt=tt):
                return e.tensor_copy(
                    bigb[64 * lane:64 * (lane + 1),
                         2048 * pr + 128 * tt:2048 * pr + 128 * (tt + 1)],
                    p_trb[0:64, 0:128])
            h_v(ctx, fcp,
                [KEY("bigb_bf", 2048 * pr + 128 * tt, 2048 * pr + 128 * (tt + 1),
                     64 * lane, 64 * (lane + 1))],
                [("p_d",)])

    proj_chunk_loop("L0_attn_Wq", q_post)

    if ATTN0_STOP <= 7:
        return
    for h in range(H):
        pr, lane = h // 2, h % 2
        for cc in range(4):
            qslice = bigb[64 * lane:64 * (lane + 1),
                          2048 * pr + 512 * cc:2048 * pr + 512 * (cc + 1)]
            qkey = KEY("bigb_bf", 2048 * pr + 512 * cc, 2048 * pr + 512 * (cc + 1),
                       64 * lane, 64 * (lane + 1))
            for typ, d0, pp, pk in ((0, 0, p_a, "p_a"), (1, 512, p_b, "p_b")):
                def fo(e, h=h, lane=lane, d0=d0, pp=pp, qslice=qslice):
                    return e.matmul(pp[0:64, 0:512],
                                    msb[64 * lane:64 * (lane + 1),
                                        d0 + 64 * (h // 2):d0 + 64 * (h // 2) + 64],
                                    qslice, start=True, stop=True)
                ctx.s.op("tensor", fo,
                         reads=(KEY("msb_bf", d0 + 64 * (h // 2),
                                    d0 + 64 * (h // 2) + 64,
                                    64 * lane, 64 * (lane + 1)), qkey),
                         writes=((pk, (0, 512)),))
            for typ, coff, pp, pk in ((0, 2048, p_c, "p_c"), (1, 0, p_d, "p_d")):
                def fab(e, h=h, coff=coff, pp=pp, cc=cc):
                    return e.matmul(pp[0:64, 0:512], sel[:, 128 * h:128 * h + 64],
                                    wrow[0:16, coff + 512 * cc:coff + 512 * (cc + 1)],
                                    start=True, stop=True)
                ctx.s.op("tensor", fab,
                         reads=(("sel16",),
                                KEY("wbuf_f32", coff + 512 * cc, coff + 512 * (cc + 1),
                                    0, 16)),
                         writes=((pk, (0, 512)),))
            h_v(ctx, lambda e: e.tensor_copy(scr[0:64, 0:512], p_a[0:64, 0:512]),
                [KEY("scr_a", 0, 512, 0, 64)], [("p_a", (0, 512))])
            h_v(ctx, lambda e: e.tensor_tensor(scr[0:64, 512:1024],
                                               scr[0:64, 0:512],
                                               p_c[0:64, 0:512], ALU.mult),
                [KEY("scr_a", 512, 1024, 0, 64)],
                [KEY("scr_a", 0, 512, 0, 64), ("p_c", (0, 512))])
            h_v(ctx, lambda e: e.tensor_copy(scr[0:64, 0:512], p_b[0:64, 0:512]),
                [KEY("scr_a", 0, 512, 0, 64)], [("p_b", (0, 512))])
            h_v(ctx, lambda e: e.tensor_tensor(scr[0:64, 1024:1536],
                                               scr[0:64, 0:512],
                                               p_d[0:64, 0:512], ALU.mult),
                [KEY("scr_a", 1024, 1536, 0, 64)],
                [KEY("scr_a", 0, 512, 0, 64), ("p_d", (0, 512))])
            def fsum(e, h=h, cc=cc, pr=pr, lane=lane):
                return e.tensor_add(
                    xb[64 * lane:64 * (lane + 1),
                       pr * T + 512 * cc:pr * T + 512 * (cc + 1)],
                    scr[0:64, 512:1024], scr[0:64, 1024:1536])
            h_v(ctx, fsum,
                [("xln_bf", (pr * T + 512 * cc, pr * T + 512 * (cc + 1)),
                  (64 * lane, 64 * (lane + 1)))],
                [KEY("scr_a", 512, 1024, 0, 64), KEY("scr_a", 1024, 1536, 0, 64)])

    if ATTN0_STOP <= 8:
        return
    for dd in range(NDT):
        wb = wb0 if dd % 2 == 0 else wb1
        wbk = ("wband0",) if dd % 2 == 0 else ("wband1",)
        load_band(ctx, wb[:, 0:1024], wbk, ctx.w["L0_attn_Wo"],
                  128 * dd, 128 * (dd + 1))
        for cc in range(4):
            pp = p_c if cc % 2 == 0 else p_d
            pk = ("p_c",) if cc % 2 == 0 else ("p_d",)
            pairs = [(wb[:, 128 * g:128 * (g + 1)],
                      xb[:, xcols(g, 512 * cc, 512)]) for g in range(NDT)]
            h_mm(ctx, pairs, pp[:, 0:512], pk,
                 [wbk] + [("xln_bf", (g * T + 512 * cc, g * T + 512 * (cc + 1)))
                          for g in range(NDT)])
            def facc(e, dd=dd, cc=cc, pp=pp):
                return e.tensor_add(x[:, xcols(dd, 512 * cc, 512)],
                                    pp[:, 0:512], x[:, xcols(dd, 512 * cc, 512)])
            xk = ("x_res", (dd * T + 512 * cc, dd * T + 512 * (cc + 1)))
            h_v(ctx, facc, [xk], [pk, xk])




def build_attn2(ctx):
    x = ctx.t["x_res"]
    xb = ctx.t["xln_bf"]
    build_norm(ctx, x, "x_res", xb, "xln_bf", kind="ln")
    bigb = ctx.t["bigb_bf"]
    pa_, pb_ = ctx.t["pool_bf_a"], ctx.t["pool_bf_b"]
    qt = ctx.t["qtile_bf"]
    msb = ctx.t["msb_bf"]
    cmask = ctx.t["cmask_bf"]
    rows = ctx.t["rowpool"]
    scr = ctx.t["scr_a"]
    onesf = ctx.t["ones_f32"]
    onesb = ctx.t["ones_bf"]
    wb0, wb1 = ctx.t["wband0"], ctx.t["wband1"]
    p_a, p_b, p_c, p_d = (ctx.psum[n] for n in ("p_a", "p_b", "p_c", "p_d"))
    p_st = ctx.psum["p_stat"]
    p_bc = ctx.psum["p_bc"]

    def KEY(name, c0, c1, p0=0, p1=128):
        return (name, (c0, c1), (p0, p1))

    LT0 = 14336
    for gi, anm in enumerate(("L2_attn_aqT", "L2_attn_akT", "L2_attn_avT")):
        rb = 32 * gi
        load_band(ctx, wb0[:, 0:128], ("wband0",), ctx.w[anm], 0, 16)
        for cc in range(4):
            h_mm(ctx,
                 [(wb0[:, 16 * g:16 * (g + 1)], xb[:, xcols(g, 512 * cc, 512)])
                  for g in range(NDT)],
                 p_a[rb:rb + 16, 0:512], ("p_a",),
                 [("wband0",)] + [("xln_bf", (g * T + 512 * cc,
                                              g * T + 512 * (cc + 1)))
                                  for g in range(NDT)])
            def fev(e, rb=rb, cc=cc):
                return e.mul(bigb[rb:rb + 16, LT0 + 512 * cc:
                                  LT0 + 512 * (cc + 1)],
                             p_a[rb:rb + 16, 0:512], 1.0 / 16.0)
            ctx.s.op("scalar", fev,
                     reads=(("p_a",),),
                     writes=(KEY("bigb_bf", LT0 + 512 * cc,
                                 LT0 + 512 * (cc + 1), rb, rb + 16),))

    for pr in range(8):
        wq = wb0 if pr % 2 == 0 else wb1
        wqk = ("wband0",) if pr % 2 == 0 else ("wband1",)
        load_band(ctx, wq[:, 0:1024], wqk, ctx.w["L2_attn_wqT"],
                  128 * pr, 128 * (pr + 1))
        load_band(ctx, wq[:, 1024:2048], wqk, ctx.w["L2_attn_wkT"],
                  128 * pr, 128 * (pr + 1))
        load_band(ctx, wq[:, 2048:3072], wqk, ctx.w["L2_attn_wvT"],
                  128 * pr, 128 * (pr + 1))
        for gi in range(3):
            def mkdma(gi=gi, pr=pr):
                h_dma(ctx, msb[32 * gi:32 * gi + 16, 0:128],
                      ctx.w["L2_attn_b3"][:, gi, 128 * pr:128 * (pr + 1)],
                      [KEY("msb_bf", 0, 128, 32 * gi, 32 * gi + 16)],
                      queue="qgp")
            mkdma()
        for dst, dkey, woff, rb in (
                (pa_, "pool_bf_a", 0, 0),
                (pb_, "pool_bf_b", 1024, 32)):
            for cc in range(4):
                pairs = [(wq[:, woff + 128 * g:woff + 128 * (g + 1)],
                          xb[:, xcols(g, 512 * cc, 512)]) for g in range(NDT)]

                def fl(e, rb=rb, cc=cc):
                    return e.matmul(p_c[:, 0:512],
                                    msb[rb:rb + 16, 0:128],
                                    bigb[rb:rb + 16,
                                         LT0 + 512 * cc:LT0 + 512 * (cc + 1)],
                                    start=False, stop=True)
                h_mm(ctx, pairs, p_c[:, 0:512], ("p_c",),
                     [wqk] + [("xln_bf", (g * T + 512 * cc,
                                          g * T + 512 * (cc + 1)))
                              for g in range(NDT)], stop=False)
                ctx.s.op("tensor", fl,
                         reads=(KEY("msb_bf", 0, 128, rb, rb + 16),
                                KEY("bigb_bf", LT0 + 512 * cc,
                                    LT0 + 512 * (cc + 1), rb, rb + 16)),
                         writes=(("p_c",),))
                def fev(e, dst=dst, cc=cc):
                    return e.tensor_copy(dst[:, 512 * cc:512 * (cc + 1)],
                                         p_c[:, 0:512])
                h_v(ctx, fev, [(dkey, (512 * cc, 512 * (cc + 1)))], [("p_c",)])
        for tt in range(16):
            pairs = [(xb[:, xcols(g, 128 * tt, 128)],
                      wq[:, 2048 + 128 * g:2048 + 128 * (g + 1)])
                     for g in range(NDT)]
            def flv(e, tt=tt):
                return e.matmul(p_c[:, 0:128],
                                bigb[64:80, LT0 + 128 * tt:
                                     LT0 + 128 * (tt + 1)],
                                msb[64:80, 0:128], start=False, stop=True)
            h_mm(ctx, pairs, p_c[:, 0:128], ("p_c",),
                 [wqk] + [("xln_bf", (g * T + 128 * tt, g * T + 128 * (tt + 1)))
                          for g in range(NDT)], stop=False)
            ctx.s.op("tensor", flv,
                     reads=(KEY("msb_bf", 0, 128, 64, 80),
                            KEY("bigb_bf", LT0 + 128 * tt,
                                LT0 + 128 * (tt + 1), 64, 80)),
                     writes=(("p_c",),))
            def fevv(e, tt=tt):
                return e.tensor_copy(qt[:, 128 * tt:128 * (tt + 1)],
                                     p_c[:, 0:128])
            h_v(ctx, fevv, [("qtile_bf", (128 * tt, 128 * (tt + 1)))],
                [("p_c",)])
        for lane in (0, 64):
            hb = lane
            for qc in range(4):
                nkt = 4 * qc + 4
                for kt in range(nkt):
                    pp = p_a if kt % 2 == 0 else p_b
                    pk = ("p_a",) if kt % 2 == 0 else ("p_b",)
                    def fs(e, lane=lane, qc=qc, kt=kt, pp=pp):
                        return e.matmul(
                            pp[:, 0:512],
                            pb_[lane:lane + 64, 128 * kt:128 * (kt + 1)],
                            pa_[lane:lane + 64, 512 * qc:512 * (qc + 1)],
                            start=True, stop=True)
                    ctx.s.op("tensor", fs,
                             reads=(("pool_bf_b", (128 * kt, 128 * (kt + 1)),
                                     (lane, lane + 64)),
                                    ("pool_bf_a", (512 * qc, 512 * (qc + 1)),
                                     (lane, lane + 64))),
                             writes=(pk,))
                    def fe(e, pp=pp):
                        return e.activation(ctx.t["ptile_bf"][:, 0:512],
                                            pp[:, 0:512], AF.Exp, scale=0.125)
                    ctx.s.op("scalar", fe, reads=(pk,),
                             writes=(("ptile_bf",),))
                    m = kt - 4 * qc
                    if m >= 0:
                        def fmask(e, m=m):
                            return e.tensor_tensor(
                                ctx.t["ptile_bf"][:, 0:512],
                                ctx.t["ptile_bf"][:, 0:512],
                                cmask[:, 384 - 128 * m:896 - 128 * m], ALU.mult)
                        h_v(ctx, fmask, [("ptile_bf",)],
                            [("ptile_bf",), ("cmask_bf",)])
                    def fnum(e, lane=lane, kt=kt, qc=qc):
                        return e.matmul(p_c[lane:lane + 64, 0:512],
                                        qt[:, 128 * kt + lane:128 * kt + lane + 64],
                                        ctx.t["ptile_bf"][:, 0:512],
                                        start=(kt == 0), stop=(kt == nkt - 1))
                    ctx.s.op("tensor", fnum,
                             reads=(("qtile_bf", (128 * kt + lane,
                                                  128 * kt + lane + 64)),
                                    ("ptile_bf",)),
                             writes=(("p_c", (0, 512), (lane, lane + 64)),))
                    dpo = 0 if lane == 0 else 512
                    def fden(e, kt=kt, dpo=dpo):
                        return e.matmul(p_st[0:1, dpo:dpo + 512],
                                        onesb[:, 0:1],
                                        ctx.t["ptile_bf"][:, 0:512],
                                        start=(kt == 0), stop=(kt == nkt - 1))
                    ctx.s.op("tensor", fden,
                             reads=(("ones_bf",), ("ptile_bf",)),
                             writes=(("p_stat", (dpo, dpo + 512)),))
                dpo = 0 if lane == 0 else 512
                def frc(e, dpo=dpo):
                    return e.reciprocal(rows[0:1, 0:512], p_st[0:1, dpo:dpo + 512])
                h_v(ctx, frc, [("rows", (0, 512), (0, 1))],
                    [("p_stat", (dpo, dpo + 512))])
                h_mm(ctx, [(onesf[0:1, 0:64], rows[0:1, 0:512])],
                     p_bc[0:64, 0:512], ("p_bc",), [("rows", (0, 512), (0, 1))])
                def fcp(e, lane=lane):
                    return e.tensor_copy(scr[0:64, 0:512], p_c[lane:lane + 64, 0:512])
                h_v(ctx, fcp, [KEY("scr_a", 0, 512, 0, 64)],
                    [("p_c", (0, 512), (lane, lane + 64))])
                def fo(e, pr=pr, lane=lane, qc=qc):
                    return e.tensor_tensor(
                        bigb[lane:lane + 64,
                             2048 * pr + 512 * qc:2048 * pr + 512 * (qc + 1)],
                        scr[0:64, 0:512], p_bc[0:64, 0:512], ALU.mult)
                h_v(ctx, fo,
                    [KEY("bigb_bf", 2048 * pr + 512 * qc,
                         2048 * pr + 512 * (qc + 1), lane, lane + 64)],
                    [KEY("scr_a", 0, 512, 0, 64), ("p_bc",)])

    if ATTN2_BARRIER:
        ctx.s.barrier_all()
    for cc in range(4):
        load_band(ctx, wb0[:, 0:128], ("wband0",), ctx.w["L2_attn_aoT"], 0, 16)
        pairs = [(wb0[:, 16 * g:16 * (g + 1)],
                  bigb[:, g * T + 512 * cc:g * T + 512 * (cc + 1)])
                 for g in range(NDT)]
        h_mm(ctx, pairs, p_a[0:16, 0:512], ("p_a",),
             [("wband0",)] + [("bigb_bf", (g * T + 512 * cc,
                                           g * T + 512 * (cc + 1)))
                              for g in range(NDT)])
        def fevo(e, cc=cc):
            return e.mul(msb[0:16, 512:1024], p_a[0:16, 0:512], 1.0 / 16.0)
        ctx.s.op("scalar", fevo, reads=(("p_a",),),
                 writes=(KEY("msb_bf", 512, 1024, 0, 16),))
        for dd in range(NDT):
            wb = wb1
            load_band(ctx, wb[:, 0:1024], ("wband1",), ctx.w["L2_attn_woT"],
                      128 * dd, 128 * (dd + 1))
            h_dma(ctx, msb[0:16, 384:512],
                  ctx.w["L2_attn_boT"][:, 128 * dd:128 * (dd + 1)],
                  [KEY("msb_bf", 384, 512, 0, 16)], queue="qgp")
            pairs = [(wb[:, 128 * g:128 * (g + 1)],
                      bigb[:, g * T + 512 * cc:g * T + 512 * (cc + 1)])
                     for g in range(NDT)]
            h_mm(ctx, pairs, p_d[:, 0:512], ("p_d",),
                 [("wband1",)] + [("bigb_bf", (g * T + 512 * cc,
                                               g * T + 512 * (cc + 1)))
                                  for g in range(NDT)], stop=False)
            def flo(e, cc=cc):
                return e.matmul(p_d[:, 0:512], msb[0:16, 384:512],
                                msb[0:16, 512:1024], start=False, stop=True)
            ctx.s.op("tensor", flo,
                     reads=(KEY("msb_bf", 384, 512, 0, 16),
                            KEY("msb_bf", 512, 1024, 0, 16)),
                     writes=(("p_d",),))
            def facc(e, dd=dd, cc=cc):
                return e.tensor_add(x[:, xcols(dd, 512 * cc, 512)],
                                    p_d[:, 0:512], x[:, xcols(dd, 512 * cc, 512)])
            xk = ("x_res", (dd * T + 512 * cc, dd * T + 512 * (cc + 1)))
            h_v(ctx, facc, [xk], [("p_d",), xk])




def build_ffn2(ctx):
    x = ctx.t["x_res"]
    xb = ctx.t["xln_bf"]
    build_norm(ctx, x, "x_res", xb, "xln_bf", kind="ln")
    bigb = ctx.t["bigb_bf"]
    sm = ctx.t["smalltok"]
    scr = ctx.t["scr_a"]
    selb = ctx.t["sel16b"]
    wrows = ctx.t["pool_bf_a"]
    wb0, wb1 = ctx.t["wband0"], ctx.t["wband1"]
    idf = ctx.t["ident_f32"]
    p_a, p_b, p_c, p_d = (ctx.psum[n] for n in ("p_a", "p_b", "p_c", "p_d"))
    p_bc = ctx.psum["p_bc"]

    def KEY(name, c0, c1, p0=0, p1=128):
        return (name, (c0, c1), (p0, p1))

    load_band(ctx, wb0[:, 0:128], ("wband0",), ctx.w["L2_ffn_selT"], 0, 16)
    for tt in range(16):
        c0 = 128 * tt
        h_mm(ctx, [(xb[:, xcols(g, c0, 128)], wb0[:, 16 * g:16 * (g + 1)])
                   for g in range(NDT)],
             p_a[:, 0:16], ("p_a",),
             [("wband0",)] + [("xln_bf", (g * T + c0, g * T + c0 + 128))
                              for g in range(NDT)])
        h_act(ctx, sm[:, 16 * tt:16 * (tt + 1)], p_a[:, 0:16], AF.Sigmoid,
              [KEY("smalltok", 16 * tt, 16 * tt + 16)], [("p_a",)])
        P0, R0, M0 = 256, 272, 288
        def fcpy(e, tt=tt):
            return e.tensor_copy(scr[:, P0:P0 + 16], sm[:, 16 * tt:16 * (tt + 1)])
        h_v(ctx, fcpy, [KEY("scr_a", P0, P0 + 16)],
            [KEY("smalltok", 16 * tt, 16 * tt + 16)])
        for rnd in range(4):
            def fmx(e):
                return e.tensor_reduce(scr[:, R0:R0 + 1], scr[:, P0:P0 + 16],
                                       AX.X, ALU.max)
            h_v(ctx, fmx, [KEY("scr_a", R0, R0 + 1)], [KEY("scr_a", P0, P0 + 16)])
            if rnd < 3:
                def fmsk(e):
                    return e.tensor_scalar(scr[:, M0:M0 + 16], scr[:, P0:P0 + 16],
                                           scr[:, R0:R0 + 1], None, ALU.is_ge)
                h_v(ctx, fmsk, [KEY("scr_a", M0, M0 + 16)],
                    [KEY("scr_a", P0, P0 + 16), KEY("scr_a", R0, R0 + 1)])
                def fsub(e):
                    return e.scalar_tensor_tensor(scr[:, P0:P0 + 16],
                                                  scr[:, M0:M0 + 16], -1e30,
                                                  scr[:, P0:P0 + 16],
                                                  ALU.mult, ALU.add)
                h_v(ctx, fsub, [KEY("scr_a", P0, P0 + 16)],
                    [KEY("scr_a", M0, M0 + 16), KEY("scr_a", P0, P0 + 16)])
        def fm4(e, tt=tt):
            return e.tensor_scalar(scr[:, M0:M0 + 16], sm[:, 16 * tt:16 * (tt + 1)],
                                   scr[:, R0:R0 + 1], None, ALU.is_ge)
        h_v(ctx, fm4, [KEY("scr_a", M0, M0 + 16)],
            [KEY("smalltok", 16 * tt, 16 * tt + 16), KEY("scr_a", R0, R0 + 1)])
        def fwr(e, tt=tt):
            return e.tensor_tensor(scr[:, P0:P0 + 16], sm[:, 16 * tt:16 * (tt + 1)],
                                   scr[:, M0:M0 + 16], ALU.mult)
        h_v(ctx, fwr, [KEY("scr_a", P0, P0 + 16)],
            [KEY("smalltok", 16 * tt, 16 * tt + 16), KEY("scr_a", M0, M0 + 16)])
        def fsm(e):
            return e.tensor_reduce(scr[:, R0:R0 + 1], scr[:, P0:P0 + 16],
                                   AX.X, ALU.add)
        h_v(ctx, fsm, [KEY("scr_a", R0, R0 + 1)], [KEY("scr_a", P0, P0 + 16)])
        h_v(ctx, lambda e: e.tensor_scalar_max(scr[:, R0:R0 + 1],
                                               scr[:, R0:R0 + 1], 1e-9),
            [KEY("scr_a", R0, R0 + 1)], [KEY("scr_a", R0, R0 + 1)])
        h_v(ctx, lambda e: e.reciprocal(scr[:, R0 + 1:R0 + 2], scr[:, R0:R0 + 1]),
            [KEY("scr_a", R0 + 1, R0 + 2)], [KEY("scr_a", R0, R0 + 1)])
        def fwg(e):
            return e.tensor_tensor(scr[:, P0:P0 + 16], scr[:, P0:P0 + 16],
                                   scr[:, R0 + 1:R0 + 2].broadcast(1, 16)
                                   if False else scr[:, P0:P0 + 16], ALU.mult)
        def fwg2(e):
            return e.tensor_scalar(scr[:, P0:P0 + 16], scr[:, P0:P0 + 16],
                                   scr[:, R0 + 1:R0 + 2], None, ALU.mult)
        h_v(ctx, fwg2, [KEY("scr_a", P0, P0 + 16)],
            [KEY("scr_a", P0, P0 + 16), KEY("scr_a", R0 + 1, R0 + 2)])
        def fwg3(e):
            return e.tensor_scalar(scr[:, P0:P0 + 16], scr[:, P0:P0 + 16],
                                   ctx.t["rsh_t"][:, 0:1], None, ALU.mult)
        h_v(ctx, fwg3, [KEY("scr_a", P0, P0 + 16)],
            [KEY("scr_a", P0, P0 + 16), ("rsh_t",)])
        def ftr(e):
            return e.transpose(p_b[0:16, 0:128], scr[:, P0:P0 + 16], idf[:, :])
        ctx.s.op("tensor", ftr, reads=(KEY("scr_a", P0, P0 + 16), ("ident_f32",)),
                 writes=(("p_b",),))
        def fcpr(e, tt=tt):
            return e.tensor_copy(wrows[0:16, 128 * tt:128 * (tt + 1)],
                                 p_b[0:16, 0:128])
        h_v(ctx, fcpr, [KEY("pool_bf_a", 128 * tt, 128 * (tt + 1), 0, 16)],
            [("p_b",)])

    for cc in range(4):
        t0 = 512 * cc
        for e_ in range(16):
            wb = wb0 if e_ % 2 == 0 else wb1
            wbk = ("wband0",) if e_ % 2 == 0 else ("wband1",)
            h_dma(ctx, wb[:, 0:2048].rearrange("p (g c) -> p g c", g=NDT),
                  ctx.w["L2_ffn_keys"][e_].rearrange("(g p) c -> p g c", p=128),
                  [wbk], queue="qgp")
            h_mm(ctx, [(selb[:, 128 * e_:128 * (e_ + 1)],
                        wrows[0:16, t0:t0 + 512])],
                 p_bc[:, 0:512], ("p_bc",),
                 [("sel16b",), KEY("pool_bf_a", t0, t0 + 512, 0, 16)])
            for fj in range(2):
                pp = p_a if fj == 0 else p_b
                pk = ("p_a",) if fj == 0 else ("p_b",)
                h_mm(ctx, [(wb[:, 256 * g + 128 * fj:256 * g + 128 * (fj + 1)],
                            xb[:, xcols(g, t0, 512)]) for g in range(NDT)],
                     pp[:, 0:512], pk,
                     [wbk] + [("xln_bf", (g * T + t0, g * T + t0 + 512))
                              for g in range(NDT)])
                def fre(e, pp=pp, fj=fj):
                    return e.activation(scr[:, 512 * fj:512 * (fj + 1)],
                                        pp[:, 0:512], AF.Relu)
                ctx.s.op("scalar", fre, reads=(pk,),
                         writes=(KEY("scr_a", 512 * fj, 512 * (fj + 1)),))
                def fwm(e, e_=e_, fj=fj):
                    return e.tensor_tensor(
                        bigb[:, 1024 * e_ + 512 * fj:1024 * e_ + 512 * (fj + 1)],
                        scr[:, 512 * fj:512 * (fj + 1)], p_bc[:, 0:512],
                        ALU.mult)
                h_v(ctx, fwm,
                    [KEY("bigb_bf", 1024 * e_ + 512 * fj,
                         1024 * e_ + 512 * (fj + 1))],
                    [KEY("scr_a", 512 * fj, 512 * (fj + 1)), ("p_bc",)])
        for dd in range(NDT):
            wb = wb0 if dd % 2 == 0 else wb1
            wbk = ("wband0",) if dd % 2 == 0 else ("wband1",)
            h_dma(ctx, wb[:, 0:4096].rearrange("p (a g c) -> p a g c", a=16, g=2),
                  ctx.w["L2_ffn_values"][:, :, 128 * dd:128 * (dd + 1)]
                  .rearrange("a (g p) c -> p a g c", p=128),
                  [wbk], queue="qgp")
            pp = p_c if dd % 2 == 0 else p_d
            pk = ("p_c",) if dd % 2 == 0 else ("p_d",)
            pairs = []
            rk = [wbk]
            for e_ in range(16):
                for fj in range(2):
                    pairs.append((wb[:, 256 * e_ + 128 * fj:
                                     256 * e_ + 128 * (fj + 1)],
                                  bigb[:, 1024 * e_ + 512 * fj:
                                       1024 * e_ + 512 * (fj + 1)]))
                    rk.append(KEY("bigb_bf", 1024 * e_ + 512 * fj,
                                  1024 * e_ + 512 * (fj + 1)))
            h_mm(ctx, pairs, pp[:, 0:512], pk, rk)
            def facc(e, dd=dd, t0=t0, pp=pp):
                return e.tensor_add(x[:, xcols(dd, t0, 512)], pp[:, 0:512],
                                    x[:, xcols(dd, t0, 512)])
            xk = ("x_res", (dd * T + t0, dd * T + t0 + 512))
            h_v(ctx, facc, [xk], [pk, xk])




def build_ffn3(ctx, first):
    x = ctx.t["x_res"]
    xb = ctx.t["xln_bf"]
    build_norm(ctx, x, "x_res", xb, "xln_bf", kind="ln")
    bigb = ctx.t["bigb_bf"]
    qt = ctx.t["qtile_bf"]
    pba, pbb = ctx.t["pool_bf_a"], ctx.t["pool_bf_b"]
    acc = ctx.t["rowpool"]
    wb0, wb1 = ctx.t["wband0"], ctx.t["wband1"]
    p_a, p_b, p_c, p_d = (ctx.psum[n] for n in ("p_a", "p_b", "p_c", "p_d"))
    p_st, p_bc = ctx.psum["p_stat"], ctx.psum["p_bc"]

    def KEY(name, c0, c1, p0=0, p1=128):
        return (name, (c0, c1), (p0, p1))

    if first:
        wcbf = ctx.dscr("wcbf", (16 * D, D), BF16).ap()
        for i in range(8):
            h_dma(ctx, wcbf[2048 * i:2048 * (i + 1), :],
                  ctx.w["L3_ffn_cprojT"][2048 * i:2048 * (i + 1), :],
                  [("wcbf", (2048 * i, 2048 * (i + 1)))], queue="qgp")
    else:
        wcbf = ctx.dram["wcbf"].ap()

    CP = [(p_st, 0, "p_stat"), (p_st, 512, "p_stat"), (p_bc, 0, "p_bc"),
          (p_bc, 512, "p_bc"), (p_a, 0, "p_a"), (p_b, 0, "p_b"),
          (p_c, 0, "p_c"), (p_d, 0, "p_d")]
    ZS = [(p_st, 0, "p_stat"), (p_st, 512, "p_stat"), (p_bc, 0, "p_bc"),
          (p_bc, 512, "p_bc")]

    CHT = 256
    for tc in range(T // CHT):
        t0 = tc * CHT
        for h in range(4):
            half = h % 2
            wbd2 = wb0 if half == 0 else wb1
            wbd2k = ("wband0",) if half == 0 else ("wband1",)
            h_dma(ctx, wbd2[:, 0:4096].rearrange("p (g c) -> p g c", g=8),
                  ctx.w["L3_ffn_bd2"][h].rearrange("g p c -> p g c"),
                  [wbd2k], queue="qgp")
            h_dma(ctx, qt[:, 0:1024].rearrange("p (g c) -> p g c", g=8),
                  ctx.w["L3_ffn_bd1"][h].rearrange("g p c -> p g c"),
                  [KEY("qtile_bf", 0, 1024)], queue="qgp")
            for rg in range(8):
                h_mm(ctx, [(qt[:, 128 * rg:128 * (rg + 1)],
                            xb[:, xcols(rg, t0, CHT)])],
                     p_a[:, 0:CHT], ("p_a", (0, 256)),
                     [KEY("qtile_bf", 0, 1024),
                      ("xln_bf", (rg * T + t0, rg * T + t0 + CHT))])
                h_v(ctx, lambda e: e.tensor_copy(qt[:, 1024:1024 + CHT],
                                                 p_a[:, 0:CHT]),
                    [KEY("qtile_bf", 1024, 1024 + CHT)], [("p_a", (0, 256))])
                for m in range(4):
                    pz, zo, zk = ZS[m]
                    def fz(e, m=m, rg=rg, pz=pz, zo=zo, wbd2=wbd2):
                        return e.matmul(pz[:, zo:zo + CHT],
                                        wbd2[:, 512 * rg + 128 * m:
                                             512 * rg + 128 * (m + 1)],
                                        qt[:, 1024:1024 + CHT],
                                        start=True, stop=True)
                    ctx.s.op("tensor", fz,
                             reads=(wbd2k, KEY("qtile_bf", 1024, 1024 + CHT)),
                             writes=((zk, (zo, zo + CHT)),))
                for mb, pz, zk in ((0, p_st, "p_stat"), (1, p_bc, "p_bc")):
                    def fg(e, rg=rg, pz=pz, half=half, mb=mb):
                        return e.activation(
                            bigb[:, 8192 * half + 1024 * rg + 512 * mb:
                                 8192 * half + 1024 * rg + 512 * (mb + 1)]
                            .rearrange("p (a c) -> p a c", a=2),
                            pz[:, 0:1024].rearrange("p (a c) -> p a c", a=2)
                            [:, :, 0:CHT],
                            AF.Gelu)
                    ctx.s.op("scalar", fg,
                             reads=((zk, (0, 768)),),
                             writes=(KEY("bigb_bf",
                                         8192 * half + 1024 * rg + 512 * mb,
                                         8192 * half + 1024 * rg + 512 * (mb + 1)),))
            for b2 in range(16):
                fi0 = h * 32 + 2 * b2
                wc = pba if b2 % 2 == 0 else pbb
                wck = ("pool_bf_a",) if b2 % 2 == 0 else ("pool_bf_b",)
                h_dma(ctx, wc[:, 0:2048].rearrange("p (a c) -> p a c", a=2),
                      wcbf[128 * fi0:128 * (fi0 + 2), :]
                      .rearrange("(a p) c -> p a c", p=128),
                      [wck], [("wcbf",)],
                      queue="qsync" if b2 % 2 == 0 else "qscal")
                for a in range(2):
                    li = 2 * b2 + a
                    hsl = 8192 * half + 256 * li
                    for dd in range(8):
                        pt, off, ptk = CP[dd]
                        def fc(e, pt=pt, off=off, dd=dd, wc=wc, a=a, hsl=hsl,
                               li=li):
                            return e.matmul(
                                pt[:, off:off + CHT],
                                wc[:, 1024 * a + 128 * dd:
                                   1024 * a + 128 * (dd + 1)],
                                bigb[:, hsl:hsl + 256],
                                start=(li == 0), stop=(li == 31),
                                skip_group_check=True)
                        ctx.s.op("tensor", fc,
                                 reads=(wck, KEY("bigb_bf", hsl, hsl + 256)),
                                 writes=((ptk, (off, off + CHT)),))
            for dd in range(8):
                pt, off, ptk = CP[dd]
                if h == 0:
                    def fe0(e, pt=pt, off=off, dd=dd):
                        return e.tensor_copy(acc[:, 256 * dd:256 * (dd + 1)],
                                             pt[:, off:off + CHT])
                    h_v(ctx, fe0, [("rows", (256 * dd, 256 * (dd + 1)))],
                        [(ptk, (off, off + CHT))])
                else:
                    def fe1(e, pt=pt, off=off, dd=dd):
                        return e.tensor_add(acc[:, 256 * dd:256 * (dd + 1)],
                                            pt[:, off:off + CHT],
                                            acc[:, 256 * dd:256 * (dd + 1)])
                    h_v(ctx, fe1, [("rows", (256 * dd, 256 * (dd + 1)))],
                        [(ptk, (off, off + CHT)),
                         ("rows", (256 * dd, 256 * (dd + 1)))])
        for dd in range(8):
            def fac(e, dd=dd, t0=t0):
                return e.scalar_tensor_tensor(
                    x[:, xcols(dd, t0, CHT)],
                    acc[:, 256 * dd:256 * (dd + 1)], 0.5,
                    x[:, xcols(dd, t0, CHT)], ALU.mult, ALU.add)
            xk = ("x_res", (dd * T + t0, dd * T + t0 + CHT))
            h_v(ctx, fac, [xk], [("rows", (256 * dd, 256 * (dd + 1))), xk])




def build_attn3(ctx):
    x = ctx.t["x_res"]
    xb = ctx.t["xln_bf"]
    build_norm(ctx, x, "x_res", xb, "xln_bf", kind="ln")
    bigb = ctx.t["bigb_bf"]
    qt = ctx.t["qtile_bf"]
    qtf = ctx.t["qtile_bf"].bitcast(F32)
    pba, pbb = ctx.t["pool_bf_a"], ctx.t["pool_bf_b"]
    wrow = ctx.t["wbuf_f32"]
    scr = ctx.t["scr_a"]
    sm = ctx.t["smalltok"]
    wb0, wb1 = ctx.t["wband0"], ctx.t["wband1"]
    At = ctx.t["a3_sb"]
    b4 = ctx.t["bias4_sb"]
    cwt = ctx.t["convw_sb"]
    carry = ctx.t["carry_sb"]
    p_a, p_b, p_c, p_d = (ctx.psum[n] for n in ("p_a", "p_b", "p_c", "p_d"))
    p_st, p_bc = ctx.psum["p_stat"], ctx.psum["p_bc"]
    idf = ctx.t["ident_f32"]

    def KEY(name, c0, c1, p0=0, p1=128):
        return (name, (c0, c1), (p0, p1))

    xc32 = ctx.dscr("xc32", (D, T), F32).ap()
    ybf = ctx.dscr("ybf", (D, T), BF16).ap()
    bcr = ctx.dscr("bcrows", (32, T), F32).ap()

    for k in range(NDT):
        wb = wb0 if k % 2 == 0 else wb1
        wbk = ("wband0",) if k % 2 == 0 else ("wband1",)
        load_band(ctx, wb[:, 0:1024], wbk, ctx.w["L3_attn_Win"],
                  128 * k, 128 * (k + 1))
        for cc in range(4):
            pp = p_a if cc % 2 == 0 else p_b
            pk = ("p_a",) if cc % 2 == 0 else ("p_b",)
            h_mm(ctx, [(wb[:, 128 * g:128 * (g + 1)],
                        xb[:, xcols(g, 512 * cc, 512)]) for g in range(NDT)],
                 pp[:, 0:512], pk,
                 [wbk] + [("xln_bf", (g * T + 512 * cc, g * T + 512 * (cc + 1)))
                          for g in range(NDT)])
            def fev(e, cc=cc, pp=pp, k=k):
                return e.tensor_scalar(wrow[:, 512 * cc:512 * (cc + 1)],
                                       pp[:, 0:512], b4[:, k:k + 1], None,
                                       ALU.add)
            h_v(ctx, fev, [KEY("wbuf_f32", 512 * cc, 512 * (cc + 1))],
                [pk, ("bias4_sb",)])
        XI, XO = 0, 2048
        def c1(e, k=k):
            return e.tensor_scalar(wrow[:, XO:XO + 2048], wrow[:, XI:XI + 2048],
                                   cwt[:, 3 * k + 1:3 * k + 2], None, ALU.mult)
        h_v(ctx, c1, [KEY("wbuf_f32", XO, XO + 2048)],
            [KEY("wbuf_f32", XI, XI + 2048), ("convw_sb",)])
        def c0(e, k=k):
            return e.scalar_tensor_tensor(wrow[:, XO + 1:XO + 2048],
                                          wrow[:, XI:XI + 2047],
                                          cwt[:, 3 * k:3 * k + 1],
                                          wrow[:, XO + 1:XO + 2048],
                                          ALU.mult, ALU.add)
        h_v(ctx, c0, [KEY("wbuf_f32", XO, XO + 2048)],
            [KEY("wbuf_f32", XI, XI + 2048), ("convw_sb",),
             KEY("wbuf_f32", XO, XO + 2048)])
        def c2(e, k=k):
            return e.scalar_tensor_tensor(wrow[:, XO:XO + 2047],
                                          wrow[:, XI + 1:XI + 2048],
                                          cwt[:, 3 * k + 2:3 * k + 3],
                                          wrow[:, XO:XO + 2047],
                                          ALU.mult, ALU.add)
        h_v(ctx, c2, [KEY("wbuf_f32", XO, XO + 2048)],
            [KEY("wbuf_f32", XI, XI + 2048), ("convw_sb",),
             KEY("wbuf_f32", XO, XO + 2048)])
        def cb(e, k=k):
            return e.tensor_scalar(wrow[:, XO:XO + 2048], wrow[:, XO:XO + 2048],
                                   cwt[:, 24 + k:25 + k], None, ALU.add)
        h_v(ctx, cb, [KEY("wbuf_f32", XO, XO + 2048)],
            [KEY("wbuf_f32", XO, XO + 2048), ("convw_sb",)])
        h_act(ctx, wrow[:, XI:XI + 2048], wrow[:, XO:XO + 2048], AF.Silu,
              [KEY("wbuf_f32", XI, XI + 2048)], [KEY("wbuf_f32", XO, XO + 2048)])
        def fbf(e, k=k):
            return e.tensor_copy(bigb[:, k * T:(k + 1) * T],
                                 wrow[:, XI:XI + 2048])
        h_v(ctx, fbf, [("bigb_bf", (k * T, (k + 1) * T))],
            [KEY("wbuf_f32", XI, XI + 2048)])
        h_dma(ctx, xc32[128 * k:128 * (k + 1), :], wrow[:, XI:XI + 2048],
              [("xc32", (128 * k, 128 * (k + 1)))],
              [KEY("wbuf_f32", XI, XI + 2048)], queue="qsync")

    load_band(ctx, wb0[:, 0:256], ("wband0",), ctx.w["L3_attn_Wbc"], 0, 32)
    for tt in range(16):
        c0_ = 128 * tt
        h_mm(ctx, [(bigb[:, g * T + c0_:g * T + c0_ + 128],
                    wb0[:, 32 * g:32 * (g + 1)]) for g in range(NDT)],
             p_a[:, 0:32], ("p_a",),
             [("wband0",)] + [("bigb_bf", (g * T + c0_, g * T + c0_ + 128))
                              for g in range(NDT)])
        def fb(e):
            return e.tensor_tensor(scr[:, 0:32], p_a[:, 0:32],
                                   ctx.t["bbc_sb"][:, :], ALU.add)
        h_v(ctx, fb, [KEY("scr_a", 0, 32)], [("p_a",), ("bbc_sb",)])
        def ftr(e):
            return e.transpose(p_b[0:32, 0:128], scr[:, 0:32], idf[:, :])
        ctx.s.op("tensor", ftr, reads=(KEY("scr_a", 0, 32), ("ident_f32",)),
                 writes=(("p_b",),))
        def fcp(e, tt=tt):
            return e.tensor_copy(wrow[0:32, 128 * tt:128 * (tt + 1)],
                                 p_b[0:32, 0:128])
        h_v(ctx, fcp, [KEY("wbuf_f32", 128 * tt, 128 * (tt + 1), 0, 32)],
            [("p_b",)])
    h_dma(ctx, bcr[:, :], wrow[0:32, 0:2048], [("bcrows",)],
          [KEY("wbuf_f32", 0, 2048, 0, 32)], queue="qsync")

    HB = 1024
    for k in range(NDT):
        for hf in range(2):
            t0 = HB * hf
            wb = wb0 if k % 2 == 0 else wb1
            wbk = ("wband0",) if k % 2 == 0 else ("wband1",)
            if hf == 0:
                load_band(ctx, wb[:, 0:1024], wbk, ctx.w["L3_attn_Wdt"],
                          128 * k, 128 * (k + 1))
            for cc in range(2):
                pp = p_a if cc % 2 == 0 else p_b
                pk = ("p_a",) if cc % 2 == 0 else ("p_b",)
                h_mm(ctx, [(wb[:, 128 * g:128 * (g + 1)],
                            bigb[:, g * T + t0 + 512 * cc:
                                 g * T + t0 + 512 * (cc + 1)])
                           for g in range(NDT)],
                     pp[:, 0:512], pk,
                     [wbk] + [("bigb_bf", (g * T + t0 + 512 * cc,
                                           g * T + t0 + 512 * (cc + 1)))
                              for g in range(NDT)])
                def fr(e, cc=cc, pp=pp, k=k):
                    return e.tensor_scalar(wrow[:, 512 * cc:512 * (cc + 1)],
                                           pp[:, 0:512], b4[:, 16 + k:17 + k],
                                           0.0, ALU.add, ALU.max)
                h_v(ctx, fr, [KEY("wbuf_f32", 512 * cc, 512 * (cc + 1))],
                    [pk, ("bias4_sb",)])
                def fa(e, cc=cc, pp=pp, k=k):
                    return e.activation(scr[:, 512 * cc:512 * (cc + 1)],
                                        pp[:, 0:512], AF.Abs,
                                        bias=b4[:, 16 + k:17 + k])
                ctx.s.op("scalar", fa, reads=(pk, ("bias4_sb",)),
                         writes=(KEY("scr_a", 512 * cc, 512 * (cc + 1)),))
                h_act(ctx, scr[:, 512 * cc:512 * (cc + 1)],
                      scr[:, 512 * cc:512 * (cc + 1)], AF.Exp,
                      [KEY("scr_a", 512 * cc, 512 * (cc + 1))],
                      [KEY("scr_a", 512 * cc, 512 * (cc + 1))], scale=-1.0)
                h_act(ctx, scr[:, 512 * cc:512 * (cc + 1)],
                      scr[:, 512 * cc:512 * (cc + 1)], AF.Ln,
                      [KEY("scr_a", 512 * cc, 512 * (cc + 1))],
                      [KEY("scr_a", 512 * cc, 512 * (cc + 1)), ("cpool",)],
                      bias=ctx.c.col(ctx, 1.0))
                def fadd(e, cc=cc):
                    return e.tensor_add(wrow[:, 512 * cc:512 * (cc + 1)],
                                        wrow[:, 512 * cc:512 * (cc + 1)],
                                        scr[:, 512 * cc:512 * (cc + 1)])
                h_v(ctx, fadd, [KEY("wbuf_f32", 512 * cc, 512 * (cc + 1))],
                    [KEY("wbuf_f32", 512 * cc, 512 * (cc + 1)),
                     KEY("scr_a", 512 * cc, 512 * (cc + 1))])
            h_dma(ctx, scr[:, 0:HB], xc32[128 * k:128 * (k + 1), t0:t0 + HB],
                  [KEY("scr_a", 0, HB)], [("xc32", (128 * k, 128 * (k + 1)))],
                  queue="qscal")
            def fdbx(e):
                return e.tensor_mul(wrow[:, 1024:2048], wrow[:, 0:1024],
                                    scr[:, 0:HB])
            h_v(ctx, fdbx, [KEY("wbuf_f32", 1024, 2048)],
                [KEY("wbuf_f32", 0, 1024), KEY("scr_a", 0, HB)])
            h_v(ctx, lambda e: e.memset(wrow[:, 2048:3072], 0.0),
                [KEY("wbuf_f32", 2048, 3072)], [])
            for n in range(16):
                h_dma(ctx, wrow[:, 3072:4096],
                      bass.broadcast_tensor_aps(
                          wrow[:, 3072:4096], bcr[n:n + 1, t0:t0 + HB])[1],
                      [KEY("wbuf_f32", 3072, 4096)], [("bcrows",)],
                      queue="qgp")
                h_dma(ctx, qtf[:, 0:HB],
                      bass.broadcast_tensor_aps(
                          qtf[:, 0:HB], bcr[16 + n:17 + n, t0:t0 + HB])[1],
                      [("qtile_bf", (0, 2048))], [("bcrows",)],
                      queue="qsync")
                def fda(e, k=k, n=n):
                    return e.activation(scr[:, 0:HB], wrow[:, 0:1024], AF.Exp,
                                        scale=At[:, 16 * k + n:16 * k + n + 1])
                ctx.s.op("scalar", fda,
                         reads=(KEY("wbuf_f32", 0, 1024), ("a3_sb",)),
                         writes=(KEY("scr_a", 0, HB),))
                def fu(e):
                    return e.tensor_mul(sm[:, 0:HB], wrow[:, 1024:2048],
                                        wrow[:, 3072:4096])
                h_v(ctx, fu, [("smalltok",)],
                    [KEY("wbuf_f32", 1024, 2048), KEY("wbuf_f32", 3072, 4096)],
                    eng="gpsimd")
                def fs(e, k=k, n=n, hf=hf):
                    init = carry[:, n:n + 1] if hf == 1 else 0.0
                    return e.tensor_tensor_scan(sm[:, 0:HB], scr[:, 0:HB],
                                                sm[:, 0:HB], init,
                                                ALU.mult, ALU.add)
                h_v(ctx, fs, [("smalltok",)],
                    [KEY("scr_a", 0, HB), ("smalltok",), ("carry_sb",)])
                if hf == 0:
                    def fcar(e, n=n):
                        return e.tensor_copy(carry[:, n:n + 1],
                                             sm[:, HB - 1:HB])
                    h_v(ctx, fcar, [("carry_sb",)], [("smalltok",)])
                def ft1(e):
                    return e.tensor_mul(scr[:, 0:HB], sm[:, 0:HB], qtf[:, 0:HB])
                h_v(ctx, ft1, [KEY("scr_a", 0, HB)],
                    [("smalltok",), ("qtile_bf", (0, 2048))])
                def fy(e):
                    return e.tensor_add(wrow[:, 2048:3072], wrow[:, 2048:3072],
                                        scr[:, 0:HB])
                h_v(ctx, fy, [KEY("wbuf_f32", 2048, 3072)],
                    [KEY("wbuf_f32", 2048, 3072), KEY("scr_a", 0, HB)])
            h_dma(ctx, ybf[128 * k:128 * (k + 1), t0:t0 + HB],
                  wrow[:, 2048:3072],
                  [("ybf", (128 * k, 128 * (k + 1)))],
                  [KEY("wbuf_f32", 2048, 3072)], queue="qgp")

    for cc in range(4):
        t0 = 512 * cc
        for k in range(NDT):
            wb = wb0 if k % 2 == 0 else wb1
            wbk = ("wband0",) if k % 2 == 0 else ("wband1",)
            load_band(ctx, wb[:, 0:1024], wbk, ctx.w["L3_attn_Wg"],
                      128 * k, 128 * (k + 1))
            pp = p_a if k % 2 == 0 else p_b
            pk = ("p_a",) if k % 2 == 0 else ("p_b",)
            h_mm(ctx, [(wb[:, 128 * g:128 * (g + 1)],
                        xb[:, xcols(g, t0, 512)]) for g in range(NDT)],
                 pp[:, 0:512], pk,
                 [wbk] + [("xln_bf", (g * T + t0, g * T + t0 + 512))
                          for g in range(NDT)])
            def fz1(e, pp=pp, k=k):
                return e.activation(scr[:, 0:512], pp[:, 0:512], AF.Silu,
                                    bias=b4[:, 8 + k:9 + k])
            ctx.s.op("scalar", fz1, reads=(pk, ("bias4_sb",)),
                     writes=(KEY("scr_a", 0, 512),))
            h_dma(ctx, qt[:, 0:512], ybf[128 * k:128 * (k + 1), t0:t0 + 512],
                  [("qtile_bf", (0, 2048))], [("ybf", (128 * k, 128 * (k + 1)))],
                  queue="qscal")
            dst = pba if k < 4 else pbb
            dk = "pool_bf_a" if k < 4 else "pool_bf_b"
            def fg(e, k=k, dst=dst):
                return e.tensor_mul(dst[:, 512 * (k % 4):512 * (k % 4 + 1)],
                                    scr[:, 0:512], qt[:, 0:512])
            h_v(ctx, fg, [(dk, (512 * (k % 4), 512 * (k % 4 + 1)))],
                [KEY("scr_a", 0, 512), ("qtile_bf", (0, 2048))])
        for dd in range(NDT):
            wb = wb0 if dd % 2 == 0 else wb1
            wbk = ("wband0",) if dd % 2 == 0 else ("wband1",)
            load_band(ctx, wb[:, 0:1024], wbk, ctx.w["L3_attn_Wout"],
                      128 * dd, 128 * (dd + 1))
            pp = p_c if dd % 2 == 0 else p_d
            pk = ("p_c",) if dd % 2 == 0 else ("p_d",)
            pairs = []
            rks = [wbk]
            for k in range(NDT):
                srcb = pba if k < 4 else pbb
                pairs.append((wb[:, 128 * k:128 * (k + 1)],
                              srcb[:, 512 * (k % 4):512 * (k % 4 + 1)]))
                rks.append(("pool_bf_a" if k < 4 else "pool_bf_b",
                            (512 * (k % 4), 512 * (k % 4 + 1))))
            h_mm(ctx, pairs, pp[:, 0:512], pk, rks)
            def fo(e, dd=dd, t0=t0, pp=pp):
                return e.scalar_tensor_tensor(
                    x[:, xcols(dd, t0, 512)], pp[:, 0:512],
                    b4[:, 24 + dd:25 + dd], x[:, xcols(dd, t0, 512)],
                    ALU.add, ALU.add)
            xk = ("x_res", (dd * T + t0, dd * T + t0 + 512))
            h_v(ctx, fo, [xk], [pk, ("bias4_sb",), xk])




def build_attn1(ctx):
    x = ctx.t["x_res"]
    xb = ctx.t["xln_bf"]
    bigb = ctx.t["bigb_bf"]
    qt = ctx.t["qtile_bf"]
    pa_, pb_ = ctx.t["pool_bf_a"], ctx.t["pool_bf_b"]
    wrow = ctx.t["wbuf_f32"]
    scr = ctx.t["scr_a"]
    sm = ctx.t["smalltok"]
    msb = ctx.t["msb_bf"]
    cmask = ctx.t["cmask_bf"]
    wb0, wb1 = ctx.t["wband0"], ctx.t["wband1"]
    onesf, onesb = ctx.t["ones_f32"], ctx.t["ones_bf"]
    idf, idb = ctx.t["ident_f32"], ctx.t["ident_bf"]
    tri_bf = ctx.t["tri_bf"]
    mc = ctx.t["m1_sb"]
    p_a, p_b, p_c, p_d = (ctx.psum[n] for n in ("p_a", "p_b", "p_c", "p_d"))
    p_st, p_bc = ctx.psum["p_stat"], ctx.psum["p_bc"]

    def KEY(name, c0, c1, p0=0, p1=128):
        return (name, (c0, c1), (p0, p1))

    def fA1(e):
        return e.activation(scr[0:32, 0:1], mc[0:32, 32:33], AF.Exp)
    ctx.s.op("scalar", fA1, reads=(("m1_sb",),),
             writes=(KEY("scr_a", 0, 1, 0, 32),))
    def fA2(e):
        return e.tensor_scalar_mul(mc[0:32, 33:34], scr[0:32, 0:1], -1.0)
    h_v(ctx, fA2, [("m1_sb",)], [KEY("scr_a", 0, 1, 0, 32), ("m1_sb",)])

    xsave = ctx.dscr("xsave", (128, NDT * T), F32).ap()
    zscr = ctx.dscr("zscr", (2048, T), BF16).ap()
    bcscr = ctx.dscr("bcscr", (128, T), BF16).ap()
    ydram = ctx.dscr("ydram", (2048, T), BF16).ap()
    xsdram = ctx.dscr("xsdram", (2048, T), BF16).ap()
    etot_d = ctx.dscr("etot_d", (16, 32), F32).ap()

    h_dma(ctx, xsave[:, :], x[:, :], [("xsave",)], [("x_res",)], queue="qsync")
    build_norm(ctx, x, "x_res", xb, "xln_bf", kind="ln")
    for k in range(NDT):
        def fcp(e, k=k):
            return e.tensor_copy(x[:, xcols(k)], xb[:, xcols(k)])
        h_v(ctx, fcp, [("x_res", (k * T, (k + 1) * T))],
            [("xln_bf", (k * T, (k + 1) * T))])

    build_norm(ctx, x, "x_res", xb, "xln_bf", kind="rms")
    cw = ctx.t["cw1_sb"]
    for of in range(34):
        wb = wb0 if of % 2 == 0 else wb1
        wbk = ("wband0",) if of % 2 == 0 else ("wband1",)
        c0w, c1w = (128 * of, 128 * (of + 1)) if of < 33 else (4224, 4256)
        load_band(ctx, wb[:, 0:8 * (c1w - c0w)], wbk,
                  ctx.w["L1_attn_in_proj_w"], c0w, c1w)
        if of < 33:
            nw = c1w - c0w
            for cc in range(4):
                pp = p_a if cc % 2 == 0 else p_b
                pk = ("p_a",) if cc % 2 == 0 else ("p_b",)
                h_mm(ctx, [(wb[:, nw * g:nw * (g + 1)],
                            xb[:, xcols(g, 512 * cc, 512)]) for g in range(NDT)],
                     pp[:, 0:512], pk,
                     [wbk] + [("xln_bf", (g * T + 512 * cc,
                                          g * T + 512 * (cc + 1)))
                              for g in range(NDT)])
                if of < 16:
                    def fz(e, pp=pp):
                        return e.tensor_copy(qt[:, 0:512], pp[:, 0:512])
                    h_v(ctx, fz, [("qtile_bf", (0, 512))], [pk])
                    h_dma(ctx, zscr[128 * of:128 * (of + 1),
                                    512 * cc:512 * (cc + 1)], qt[:, 0:512],
                          [("zscr", (128 * of, 128 * (of + 1)))],
                          [("qtile_bf", (0, 512))], queue="qgp")
                else:
                    def fxc(e, cc=cc, pp=pp):
                        return e.tensor_copy(wrow[:, 512 * cc:512 * (cc + 1)],
                                             pp[:, 0:512])
                    h_v(ctx, fxc, [KEY("wbuf_f32", 512 * cc, 512 * (cc + 1))],
                        [pk])
            if of >= 16:
                ci = of - 16
                def c3(e, ci=ci):
                    return e.tensor_scalar(wrow[:, 2048:4096], wrow[:, 0:2048],
                                           cw[:, 4 * ci + 3:4 * ci + 4], None,
                                           ALU.mult)
                h_v(ctx, c3, [KEY("wbuf_f32", 2048, 4096)],
                    [KEY("wbuf_f32", 0, 2048), ("cw1_sb",)])
                for j, sh in ((2, 1), (1, 2), (0, 3)):
                    def cj(e, ci=ci, j=j, sh=sh):
                        return e.scalar_tensor_tensor(
                            wrow[:, 2048 + sh:4096], wrow[:, 0:2048 - sh],
                            cw[:, 4 * ci + j:4 * ci + j + 1],
                            wrow[:, 2048 + sh:4096], ALU.mult, ALU.add)
                    h_v(ctx, cj, [KEY("wbuf_f32", 2048, 4096)],
                        [KEY("wbuf_f32", 0, 2048), ("cw1_sb",),
                         KEY("wbuf_f32", 2048, 4096)])
                def cb(e, ci=ci):
                    return e.tensor_scalar(wrow[:, 2048:4096],
                                           wrow[:, 2048:4096],
                                           cw[:, 68 + ci:69 + ci], None, ALU.add)
                h_v(ctx, cb, [KEY("wbuf_f32", 2048, 4096)],
                    [KEY("wbuf_f32", 2048, 4096), ("cw1_sb",)])
                if of < 32:
                    h_act(ctx, qt[:, 0:2048], wrow[:, 2048:4096], AF.Silu,
                          [("qtile_bf", (0, 2048))],
                          [KEY("wbuf_f32", 2048, 4096)])
                    h_dma(ctx, xsdram[128 * (of - 16):128 * (of - 15), :],
                          qt[:, 0:2048],
                          [("xsdram", (128 * (of - 16), 128 * (of - 15)))],
                          [("qtile_bf", (0, 2048))], queue="qgp")
                else:
                    h_act(ctx, qt[:, 0:2048], wrow[:, 2048:4096], AF.Silu,
                          [("qtile_bf", (0, 2048))],
                          [KEY("wbuf_f32", 2048, 4096)])
                    h_dma(ctx, bcscr[:, :], qt[:, 0:2048], [("bcscr",)],
                          [("qtile_bf", (0, 2048))], queue="qgp")
                    h_dma(ctx, pa_[0:64, 0:2048], bcscr[0:64, :],
                          [KEY("pool_bf_a", 0, 2048, 0, 64)], [("bcscr",)],
                          queue="qsync")
                    h_dma(ctx, pb_[0:64, 0:2048], bcscr[64:128, :],
                          [KEY("pool_bf_b", 0, 2048, 0, 64)], [("bcscr",)],
                          queue="qsync")
        else:
            for tt in range(16):
                h_mm(ctx, [(xb[:, xcols(g, 128 * tt, 128)],
                            wb[:, 32 * g:32 * (g + 1)]) for g in range(NDT)],
                     p_c[:, 0:32], ("p_c",),
                     [wbk] + [("xln_bf", (g * T + 128 * tt,
                                          g * T + 128 * (tt + 1)))
                              for g in range(NDT)])
                def f1(e):
                    return e.tensor_tensor(scr[:, 0:32], p_c[:, 0:32],
                                           mc[:, 0:32], ALU.add)
                h_v(ctx, f1, [KEY("scr_a", 0, 32)], [("p_c",), ("m1_sb",)])
                h_act(ctx, scr[:, 32:64], scr[:, 0:32], AF.Abs,
                      [KEY("scr_a", 32, 64)], [KEY("scr_a", 0, 32)])
                h_act(ctx, scr[:, 32:64], scr[:, 32:64], AF.Exp,
                      [KEY("scr_a", 32, 64)], [KEY("scr_a", 32, 64)],
                      scale=-1.0)
                h_act(ctx, scr[:, 32:64], scr[:, 32:64], AF.Ln,
                      [KEY("scr_a", 32, 64)],
                      [KEY("scr_a", 32, 64), ("cpool",)],
                      bias=ctx.c.col(ctx, 1.0))
                def f2(e, tt=tt):
                    return e.tensor_scalar(sm[:, 32 * tt:32 * (tt + 1)],
                                           scr[:, 0:32], 0.0, None, ALU.max)
                h_v(ctx, f2, [("smalltok", (32 * tt, 32 * (tt + 1)))],
                    [KEY("scr_a", 0, 32)])
                def f3(e, tt=tt):
                    return e.tensor_add(sm[:, 32 * tt:32 * (tt + 1)],
                                        sm[:, 32 * tt:32 * (tt + 1)],
                                        scr[:, 32:64])
                h_v(ctx, f3, [("smalltok", (32 * tt, 32 * (tt + 1)))],
                    [("smalltok", (32 * tt, 32 * (tt + 1))),
                     KEY("scr_a", 32, 64)])
    for tt in range(16):
        def ftr(e, tt=tt):
            return e.transpose(p_b[0:32, 0:128], sm[:, 32 * tt:32 * (tt + 1)],
                               idf[:, :])
        ctx.s.op("tensor", ftr,
                 reads=(("smalltok", (32 * tt, 32 * (tt + 1))), ("ident_f32",)),
                 writes=(("p_b",),))
        def fcp(e, tt=tt):
            return e.tensor_copy(wrow[0:32, 128 * tt:128 * (tt + 1)],
                                 p_b[0:32, 0:128])
        h_v(ctx, fcp, [KEY("wbuf_f32", 128 * tt, 128 * (tt + 1), 0, 32)],
            [("p_b",)])
    def fda(e):
        return e.tensor_scalar(wrow[0:32, 0:2048], wrow[0:32, 0:2048],
                               mc[0:32, 33:34], None, ALU.mult)
    h_v(ctx, fda, [KEY("wbuf_f32", 0, 2048, 0, 32)],
        [KEY("wbuf_f32", 0, 2048, 0, 32), ("m1_sb",)])
    for c in range(16):
        def fsc(e, c=c):
            return e.tensor_tensor_scan(
                wrow[0:32, 2048 + 128 * c:2048 + 128 * (c + 1)],
                wrow[0:32, 128 * c:128 * (c + 1)],
                wrow[0:32, 128 * c:128 * (c + 1)], 0.0, ALU.add, ALU.bypass)
        h_v(ctx, fsc, [KEY("wbuf_f32", 2048 + 128 * c, 2048 + 128 * (c + 1),
                           0, 32)],
            [KEY("wbuf_f32", 128 * c, 128 * (c + 1), 0, 32)])
    h_dma(ctx, bcscr[0:64, :], pb_[0:64, 0:2048],
          [("bcscr",)], [KEY("pool_bf_b", 0, 2048, 0, 64)], queue="qgp")
    h_dma(ctx, pb_[64:128, 0:2048], bcscr[0:64, :],
          [KEY("pool_bf_b", 0, 2048, 64, 128)], [("bcscr",)], queue="qsync")
    NH = 32
    for c in range(16):
        t0 = 128 * c
        PR = KEY("wbuf_f32", 2048 + t0, 2048 + t0 + 128, 0, 32)
        h_dma(ctx, bigb[:, 0:2048].rearrange("p (a c2) -> p a c2", a=16),
              xsdram[:, t0:t0 + 128].rearrange("(a p) c2 -> p a c2", p=128),
              [("bigb_bf", (0, 2048))], [("xsdram",)], queue="qscal")
        def ftp(e, c=c):
            return e.transpose(p_b[0:128, 0:32],
                               wrow[0:32, 2048 + t0:2048 + t0 + 128],
                               idf[0:32, 0:32])
        ctx.s.op("tensor", ftp, reads=(PR, ("ident_f32",)), writes=(("p_b",),))
        h_v(ctx, lambda e: e.tensor_copy(sm[:, 512:544], p_b[0:128, 0:32]),
            [("smalltok", (512, 544))], [("p_b",)])
        def fet(e, c=c):
            return e.activation(scr[0:32, 0:1],
                                wrow[0:32, 2048 + t0 + 127:2048 + t0 + 128],
                                AF.Exp)
        ctx.s.op("scalar", fet,
                 reads=(PR,), writes=(KEY("scr_a", 0, 1, 0, 32),))
        def fwret(e, c=c):
            return e.dma_start(out=etot_d[c:c + 1, :], in_=scr[0:32, 0:1])
        ctx.s.op("gpsimd", fwret, reads=(KEY("scr_a", 0, 1, 0, 32),),
                 writes=(("etot_d",),), queue="qgp")
        def fbce(e, c=c):
            b_ = bass.broadcast_tensor_aps(scr[:, 1:33],
                                           etot_d[c:c + 1, :])[1]
            return e.dma_start(out=scr[:, 1:33], in_=b_)
        ctx.s.op("gpsimd", fbce, reads=(("etot_d",),),
                 writes=(KEY("scr_a", 1, 33),), queue="qgp")
        def fsp(e, c=c):
            return e.tensor_scalar(
                wrow[0:32, t0:t0 + 128],
                wrow[0:32, 2048 + t0:2048 + t0 + 128],
                wrow[0:32, 2048 + t0 + 127:2048 + t0 + 128], None,
                ALU.subtract)
        h_v(ctx, fsp, [KEY("wbuf_f32", t0, t0 + 128, 0, 32)], [PR])
        def ftsp(e, c=c):
            return e.transpose(p_b[0:128, 0:32], wrow[0:32, t0:t0 + 128],
                               idf[0:32, 0:32])
        ctx.s.op("tensor", ftsp,
                 reads=(KEY("wbuf_f32", t0, t0 + 128, 0, 32), ("ident_f32",)),
                 writes=(("p_b",),))
        def fde(e, c=c):
            return e.activation(sm[:, 544:576], p_b[0:128, 0:32], AF.Exp,
                                scale=-1.0)
        ctx.s.op("scalar", fde, reads=(("p_b",),),
                 writes=(("smalltok", (544, 576)),))
        def fde2(e, c=c):
            return e.tensor_tensor(sm[:, 544:576], sm[:, 544:576],
                                   sm[:, 32 * c:32 * (c + 1)], ALU.mult)
        h_v(ctx, fde2, [("smalltok", (544, 576))],
            [("smalltok", (544, 576)), ("smalltok", (32 * c, 32 * (c + 1)))])
        def ftb(e, c=c):
            return e.transpose(p_bc.bitcast(BF16)[0:128, 0:64],
                               pa_[0:64, t0:t0 + 128], idb[0:64, 0:64])
        ctx.s.op("tensor", ftb,
                 reads=(KEY("pool_bf_a", t0, t0 + 128, 0, 64), ("ident_bf",)),
                 writes=(("p_bc",),))
        h_v(ctx, lambda e: e.tensor_copy(qt[:, 2112:2176],
                                         p_bc.bitcast(BF16)[0:128, 0:64]),
            [("qtile_bf", (2112, 2176))], [("p_bc",)])
        def fcb(e, c=c):
            return e.matmul(p_a[:, 0:128], pa_[0:64, t0:t0 + 128],
                            pb_[0:64, t0:t0 + 128], start=True, stop=True)
        ctx.s.op("tensor", fcb,
                 reads=(KEY("pool_bf_a", t0, t0 + 128, 0, 64),
                        KEY("pool_bf_b", t0, t0 + 128, 0, 64)),
                 writes=(("p_a",),))
        for h in range(NH):
            lane = 64 * (h % 2)
            hb2 = h // 2
            srow, scol = 64 * (h // 16), 64 * (h % 16)
            def ft1(e, h=h):
                return e.transpose(p_d[0:1, 0:128], sm[:, 512 + h:513 + h],
                                   idf[:, :])
            ctx.s.op("tensor", ft1,
                     reads=(("smalltok", (512, 544)), ("ident_f32",)),
                     writes=(("p_d",),))
            h_v(ctx, lambda e: e.tensor_copy(scr[0:1, 64:192], p_d[0:1, 0:128]),
                [KEY("scr_a", 64, 192, 0, 1)], [("p_d",)])
            def fk1(e):
                return e.matmul(p_st[:, 0:128], onesf[0:1, :],
                                scr[0:1, 64:192], start=True, stop=True)
            ctx.s.op("tensor", fk1,
                     reads=(("ones_f32",), KEY("scr_a", 64, 192, 0, 1)),
                     writes=(("p_stat", (0, 128)),))
            def fe1(e, h=h):
                return e.tensor_scalar(scr[:, 192:320], p_st[:, 0:128],
                                       sm[:, 512 + h:513 + h], 0.0,
                                       ALU.subtract, ALU.min)
            h_v(ctx, fe1, [KEY("scr_a", 192, 320)],
                [("p_stat", (0, 128)), ("smalltok", (512, 544))])
            h_act(ctx, scr[:, 192:320], scr[:, 192:320], AF.Exp,
                  [KEY("scr_a", 192, 320)], [KEY("scr_a", 192, 320)])
            def fw1(e):
                return e.tensor_tensor(scr[:, 320:448], scr[:, 192:320],
                                       p_a[:, 0:128], ALU.mult)
            h_v(ctx, fw1, [KEY("scr_a", 320, 448)],
                [KEY("scr_a", 192, 320), ("p_a",)])
            def fw2(e):
                return e.tensor_tensor(scr[:, 320:448], scr[:, 320:448],
                                       tri_bf[:, :], ALU.mult)
            h_v(ctx, fw2, [KEY("scr_a", 320, 448)],
                [KEY("scr_a", 320, 448), ("tri_bf",)])
            def fw3(e, c=c, h=h):
                return e.tensor_scalar(qt[:, 2368:2496], scr[:, 320:448],
                                       sm[:, 32 * c + h:32 * c + h + 1],
                                       None, ALU.mult)
            h_v(ctx, fw3, [("qtile_bf", (2368, 2496))],
                [KEY("scr_a", 320, 448),
                 ("smalltok", (32 * c, 32 * (c + 1)))])
            def fct(e, srow=srow):
                return e.activation(scr[srow:srow + 64, 448:576],
                                    p_st[srow:srow + 64, 0:128], AF.Exp)
            ctx.s.op("scalar", fct, reads=(("p_stat", (0, 128)),),
                     writes=(KEY("scr_a", 448, 576, srow, srow + 64),))
            def fct2(e, c=c, srow=srow):
                return e.tensor_tensor(qt[srow:srow + 64, 2240:2368],
                                       pb_[srow:srow + 64, t0:t0 + 128],
                                       scr[srow:srow + 64, 448:576], ALU.mult)
            h_v(ctx, fct2, [("qtile_bf", (2240, 2368), (srow, srow + 64))],
                [KEY("pool_bf_b", t0, t0 + 128, srow, srow + 64),
                 KEY("scr_a", 448, 576, srow, srow + 64)])
            def ftx(e, lane=lane, hb2=hb2):
                return e.transpose(
                    p_bc.bitcast(BF16)[0:128, 0:64],
                    bigb[lane:lane + 64, 128 * hb2:128 * hb2 + 128],
                    idb[lane:lane + 64, lane:lane + 64])
            ctx.s.op("tensor", ftx,
                     reads=(KEY("bigb_bf", 128 * hb2, 128 * hb2 + 128,
                                lane, lane + 64), ("ident_bf",)),
                     writes=(("p_bc",),))
            h_v(ctx, lambda e: e.tensor_copy(
                qt[:, 2048:2112], p_bc.bitcast(BF16)[0:128, 0:64]),
                [("qtile_bf", (2048, 2112))], [("p_bc",)])
            def fbh(e, h=h):
                return e.tensor_scalar(qt[:, 2176:2240], qt[:, 2112:2176],
                                       sm[:, 544 + h:545 + h], None, ALU.mult)
            h_v(ctx, fbh, [("qtile_bf", (2176, 2240))],
                [("qtile_bf", (2112, 2176)), ("smalltok", (544, 576))])
            def fy1(e, c=c, lane=lane):
                return e.matmul(p_c[lane:lane + 64, 0:128], qt[:, 2048:2112],
                                qt[:, 2368:2496], start=True, stop=(c == 0))
            ctx.s.op("tensor", fy1,
                     reads=(("qtile_bf", (2048, 2112)),
                            ("qtile_bf", (2368, 2496))),
                     writes=(("p_c", (0, 128), (lane, lane + 64)),))
            if c > 0:
                def fy2(e, srow=srow, scol=scol, lane=lane):
                    return e.matmul(p_c[lane:lane + 64, 0:128],
                                    msb[srow:srow + 64, scol:scol + 64],
                                    qt[srow:srow + 64, 2240:2368],
                                    start=False, stop=True)
                ctx.s.op("tensor", fy2,
                         reads=(KEY("msb_bf", scol, scol + 64, srow, srow + 64),
                                ("qtile_bf", (2240, 2368),
                                 (srow, srow + 64))),
                         writes=(("p_c", (0, 128), (lane, lane + 64)),))
            def fyev(e, lane=lane, hb2=hb2, h=h):
                return e.scalar_tensor_tensor(
                    qt[lane:lane + 64, 128 * hb2:128 * hb2 + 128],
                    bigb[lane:lane + 64, 128 * hb2:128 * hb2 + 128],
                    mc[lane:lane + 64, 64 + h:65 + h],
                    p_c[lane:lane + 64, 0:128], ALU.mult, ALU.add)
            h_v(ctx, fyev, [("qtile_bf", (128 * hb2, 128 * hb2 + 128),
                             (lane, lane + 64))],
                [KEY("bigb_bf", 128 * hb2, 128 * hb2 + 128,
                     lane, lane + 64), ("m1_sb",),
                 ("p_c", (0, 128), (lane, lane + 64))])
            def fsu(e, srow=srow):
                return e.matmul(p_d[srow:srow + 64, 0:64], qt[:, 2176:2240],
                                qt[:, 2048:2112], start=True, stop=True)
            ctx.s.op("tensor", fsu,
                     reads=(("qtile_bf", (2176, 2240)),
                            ("qtile_bf", (2048, 2112))),
                     writes=(("p_d",),))
            if c > 0:
                def fs1(e, srow=srow, scol=scol, h=h):
                    return e.tensor_scalar(scr[srow:srow + 64, 576:640],
                                           msb[srow:srow + 64,
                                               scol:scol + 64],
                                           scr[srow:srow + 64, 1 + h:2 + h],
                                           None, ALU.mult)
                h_v(ctx, fs1, [KEY("scr_a", 576, 640, srow, srow + 64)],
                    [KEY("msb_bf", scol, scol + 64, srow, srow + 64),
                     KEY("scr_a", 1, 33)])
                def fs2(e, srow=srow, scol=scol):
                    return e.tensor_tensor(msb[srow:srow + 64,
                                               scol:scol + 64],
                                           scr[srow:srow + 64, 576:640],
                                           p_d[srow:srow + 64, 0:64], ALU.add)
                h_v(ctx, fs2,
                    [KEY("msb_bf", scol, scol + 64, srow, srow + 64)],
                    [KEY("scr_a", 576, 640, srow, srow + 64), ("p_d",)])
            else:
                def fs3(e, srow=srow, scol=scol):
                    return e.tensor_copy(msb[srow:srow + 64, scol:scol + 64],
                                         p_d[srow:srow + 64, 0:64])
                h_v(ctx, fs3,
                    [KEY("msb_bf", scol, scol + 64, srow, srow + 64)],
                    [("p_d",)])
        h_dma(ctx, ydram[:, t0:t0 + 128].rearrange("(a p) c2 -> p a c2", p=128),
              qt[:, 0:2048].rearrange("p (a c2) -> p a c2", a=16),
              [("ydram",)], [("qtile_bf", (0, 2048))], queue="qgp")
    for cc in range(4):
        t0 = 512 * cc
        h_dma(ctx, bigb[:, 0:8192].rearrange("p (a c2) -> p a c2", a=16),
              ydram[:, t0:t0 + 512].rearrange("(a p) c2 -> p a c2", p=128),
              [("bigb_bf", (0, 8192))], [("ydram",)], queue="qscal")
        h_dma(ctx, bigb[:, 8192:16384].rearrange("p (a c2) -> p a c2", a=16),
              zscr[:, t0:t0 + 512].rearrange("(a p) c2 -> p a c2", p=128),
              [("bigb_bf", (8192, 16384))], [("zscr",)], queue="qsync")
        for ft in range(16):
            def fsl(e, ft=ft):
                return e.activation(scr[:, 0:512], bigb[:, 8192 + 512 * ft:
                                                        8192 + 512 * (ft + 1)],
                                    AF.Silu)
            ctx.s.op("scalar", fsl,
                     reads=(("bigb_bf", (8192 + 512 * ft,
                                         8192 + 512 * (ft + 1))),),
                     writes=(KEY("scr_a", 0, 512),))
            def fg(e, ft=ft):
                return e.tensor_tensor(bigb[:, 512 * ft:512 * (ft + 1)],
                                       bigb[:, 512 * ft:512 * (ft + 1)],
                                       scr[:, 0:512], ALU.mult)
            h_v(ctx, fg, [("bigb_bf", (512 * ft, 512 * (ft + 1)))],
                [("bigb_bf", (512 * ft, 512 * (ft + 1))),
                 KEY("scr_a", 0, 512)])
            def fsq(e, ft=ft):
                return e.activation(scr[:, 512:1024],
                                    bigb[:, 512 * ft:512 * (ft + 1)],
                                    AF.Square)
            ctx.s.op("scalar", fsq,
                     reads=(("bigb_bf", (512 * ft, 512 * (ft + 1))),),
                     writes=(KEY("scr_a", 512, 1024),))
            def fmm(e, ft=ft):
                return e.matmul(p_st[0:1, 0:512], onesf[:, 0:1],
                                scr[:, 512:1024], start=(ft == 0),
                                stop=(ft == 15))
            ctx.s.op("tensor", fmm,
                     reads=(("ones_f32",), KEY("scr_a", 512, 1024)),
                     writes=(("p_stat", (0, 512)),))
        rows = ctx.t["rowpool"]
        h_v(ctx, lambda e: e.tensor_scalar(rows[0:1, 0:512], p_st[0:1, 0:512],
                                           1.0 / 2048.0, 1e-5, ALU.mult,
                                           ALU.add),
            [("rows", (0, 512))], [("p_stat", (0, 512))])
        h_act(ctx, rows[0:1, 512:1024], rows[0:1, 0:512], AF.Sqrt,
              [("rows", (512, 1024))], [("rows", (0, 512))])
        h_v(ctx, lambda e: e.reciprocal(rows[0:1, 512:1024],
                                        rows[0:1, 512:1024]),
            [("rows", (512, 1024))], [("rows", (512, 1024))])
        h_mm(ctx, [(onesf[0:1, :], rows[0:1, 512:1024])], p_bc[:, 0:512],
             ("p_bc",), [("ones_f32",), ("rows", (512, 1024))])
        for ft in range(16):
            def fn1(e, ft=ft):
                return e.tensor_scalar(scr[:, 0:512],
                                       bigb[:, 512 * ft:512 * (ft + 1)],
                                       mc[:, 96 + ft:97 + ft], None, ALU.mult)
            h_v(ctx, fn1, [KEY("scr_a", 0, 512)],
                [("bigb_bf", (512 * ft, 512 * (ft + 1))), ("m1_sb",)])
            def fn2(e, ft=ft):
                return e.tensor_tensor(bigb[:, 512 * ft:512 * (ft + 1)],
                                       scr[:, 0:512], p_bc[:, 0:512], ALU.mult)
            h_v(ctx, fn2, [("bigb_bf", (512 * ft, 512 * (ft + 1)))],
                [KEY("scr_a", 0, 512), ("p_bc",)])
        for dd in range(NDT):
            wb = wb0 if dd % 2 == 0 else wb1
            wbk = ("wband0",) if dd % 2 == 0 else ("wband1",)
            load_band(ctx, wb[:, 0:2048], wbk, ctx.w["L1_attn_out_proj_w"],
                      128 * dd, 128 * (dd + 1), nrows=2048)
            pp = p_c if dd % 2 == 0 else p_d
            pk = ("p_c",) if dd % 2 == 0 else ("p_d",)
            h_mm(ctx, [(wb[:, 128 * g:128 * (g + 1)],
                        bigb[:, 512 * g:512 * (g + 1)]) for g in range(16)],
                 pp[:, 0:512], pk,
                 [wbk] + [("bigb_bf", (512 * g, 512 * (g + 1)))
                          for g in range(16)])
            def fac(e, dd=dd, t0=t0, pp=pp):
                return e.tensor_add(x[:, xcols(dd, t0, 512)], pp[:, 0:512],
                                    x[:, xcols(dd, t0, 512)])
            xk = ("x_res", (dd * T + t0, dd * T + t0 + 512))
            h_v(ctx, fac, [xk], [pk, xk])

    build_norm(ctx, x, "x_res", xb, "xln_bf", kind="rms")
    for of in range(8):
        wb = wb0 if of % 2 == 0 else wb1
        wbk = ("wband0",) if of % 2 == 0 else ("wband1",)
        load_band(ctx, wb[:, 0:1024], wbk, ctx.w["L1_attn_attn_w"],
                  128 * of, 128 * (of + 1))
        for cc in range(4):
            pp = p_a if cc % 2 == 0 else p_b
            pk = ("p_a",) if cc % 2 == 0 else ("p_b",)
            h_mm(ctx, [(wb[:, 128 * g:128 * (g + 1)],
                        xb[:, xcols(g, 512 * cc, 512)]) for g in range(NDT)],
                 pp[:, 0:512], pk,
                 [wbk] + [("xln_bf", (g * T + 512 * cc,
                                      g * T + 512 * (cc + 1)))
                          for g in range(NDT)])
            def fev(e, of=of, cc=cc, pp=pp):
                return e.tensor_copy(bigb[:, of * T + 512 * cc:
                                          of * T + 512 * (cc + 1)],
                                     pp[:, 0:512])
            h_v(ctx, fev, [("bigb_bf", (of * T + 512 * cc,
                                        of * T + 512 * (cc + 1)))], [pk])
    load_band(ctx, wb0[:, 0:512], ("wband0",), ctx.w["L1_attn_attn_w"],
              1024, 1088)
    for cc in range(4):
        h_mm(ctx, [(wb0[:, 64 * g:64 * (g + 1)],
                    xb[:, xcols(g, 512 * cc, 512)]) for g in range(NDT)],
             p_a[0:64, 0:512], ("p_a",),
             [("wband0",)] + [("xln_bf", (g * T + 512 * cc,
                                          g * T + 512 * (cc + 1)))
                              for g in range(NDT)])
        def fev2(e, cc=cc):
            return e.tensor_copy(pa_[0:64, 512 * cc:512 * (cc + 1)],
                                 p_a[0:64, 0:512])
        h_v(ctx, fev2, [KEY("pool_bf_a", 512 * cc, 512 * (cc + 1), 0, 64)],
            [("p_a",)])
    h_dma(ctx, bcscr[0:64, :], pa_[0:64, 0:2048],
          [("bcscr",)], [KEY("pool_bf_a", 0, 2048, 0, 64)], queue="qgp")
    h_dma(ctx, pa_[64:128, 0:2048], bcscr[0:64, :],
          [KEY("pool_bf_a", 0, 2048, 64, 128)], [("bcscr",)], queue="qsync")
    load_band(ctx, wb1[:, 0:512], ("wband1",), ctx.w["L1_attn_attn_w"],
              1088, 1152)
    for tt in range(16):
        h_mm(ctx, [(xb[:, xcols(g, 128 * tt, 128)],
                    wb1[:, 64 * g:64 * (g + 1)]) for g in range(NDT)],
             p_b[:, 0:64], ("p_b",),
             [("wband1",)] + [("xln_bf", (g * T + 128 * tt,
                                          g * T + 128 * (tt + 1)))
                              for g in range(NDT)])
        def fev3(e, tt=tt):
            return e.tensor_copy(pb_[:, 64 * tt:64 * (tt + 1)], p_b[:, 0:64])
        h_v(ctx, fev3, [KEY("pool_bf_b", 64 * tt, 64 * (tt + 1))], [("p_b",)])
    for h in range(16):
        lane = 64 * (h % 2)
        hb2 = h // 2
        for qc in range(4):
            nkt = 4 * qc + 4
            for kt in range(nkt):
                pp = p_a if kt % 2 == 0 else p_b
                pk = ("p_a",) if kt % 2 == 0 else ("p_b",)
                def fs_(e, lane=lane, hb2=hb2, qc=qc, kt=kt, pp=pp):
                    return e.matmul(pp[:, 0:512],
                                    pa_[lane:lane + 64,
                                        128 * kt:128 * (kt + 1)],
                                    bigb[lane:lane + 64,
                                         hb2 * T + 512 * qc:
                                         hb2 * T + 512 * (qc + 1)],
                                    start=True, stop=True)
                ctx.s.op("tensor", fs_,
                         reads=(KEY("pool_bf_a", 128 * kt, 128 * (kt + 1),
                                    lane, lane + 64),
                                KEY("bigb_bf", hb2 * T + 512 * qc,
                                    hb2 * T + 512 * (qc + 1),
                                    lane, lane + 64)),
                         writes=(pk,))
                def fe_(e, pp=pp):
                    return e.activation(ctx.t["ptile_bf"][:, 0:512],
                                        pp[:, 0:512], AF.Exp, scale=0.125)
                ctx.s.op("scalar", fe_, reads=(pk,), writes=(("ptile_bf",),))
                m = kt - 4 * qc
                if m >= 0:
                    def fmk(e, m=m):
                        return e.tensor_tensor(
                            ctx.t["ptile_bf"][:, 0:512],
                            ctx.t["ptile_bf"][:, 0:512],
                            cmask[:, 384 - 128 * m:896 - 128 * m], ALU.mult)
                    h_v(ctx, fmk, [("ptile_bf",)],
                        [("ptile_bf",), ("cmask_bf",)])
                def fnm(e, kt=kt, qc=qc, nkt=nkt):
                    return e.matmul(p_c[0:64, 0:512],
                                    pb_[:, 64 * kt:64 * (kt + 1)],
                                    ctx.t["ptile_bf"][:, 0:512],
                                    start=(kt == 0), stop=(kt == nkt - 1))
                ctx.s.op("tensor", fnm,
                         reads=(KEY("pool_bf_b", 64 * kt, 64 * (kt + 1)),
                                ("ptile_bf",)),
                         writes=(("p_c", (0, 512), (0, 64)),))
                def fdn(e, kt=kt, nkt=nkt):
                    return e.matmul(p_st[0:1, 0:512], onesb[:, 0:1],
                                    ctx.t["ptile_bf"][:, 0:512],
                                    start=(kt == 0), stop=(kt == nkt - 1))
                ctx.s.op("tensor", fdn,
                         reads=(("ones_bf",), ("ptile_bf",)),
                         writes=(("p_stat", (0, 512)),))
            rows = ctx.t["rowpool"]
            h_v(ctx, lambda e: e.reciprocal(rows[0:1, 0:512],
                                            p_st[0:1, 0:512]),
                [("rows", (0, 512))], [("p_stat", (0, 512))])
            h_mm(ctx, [(onesf[0:1, 0:64], rows[0:1, 0:512])],
                 p_bc[0:64, 0:512], ("p_bc",),
                 [("ones_f32",), ("rows", (0, 512))])
            h_v(ctx, lambda e: e.tensor_copy(scr[0:64, 0:512],
                                             p_c[0:64, 0:512]),
                [KEY("scr_a", 0, 512, 0, 64)], [("p_c", (0, 512), (0, 64))])
            def fo_(e, lane=lane, hb2=hb2, qc=qc):
                return e.tensor_tensor(
                    bigb[lane:lane + 64, hb2 * T + 512 * qc:
                         hb2 * T + 512 * (qc + 1)],
                    scr[0:64, 0:512], p_bc[0:64, 0:512], ALU.mult)
            h_v(ctx, fo_,
                [KEY("bigb_bf", hb2 * T + 512 * qc, hb2 * T + 512 * (qc + 1),
                     lane, lane + 64)],
                [KEY("scr_a", 0, 512, 0, 64), ("p_bc",)])
    for cc in range(4):
        t0 = 512 * cc
        for dd in range(NDT):
            wb = wb0 if dd % 2 == 0 else wb1
            wbk = ("wband0",) if dd % 2 == 0 else ("wband1",)
            load_band(ctx, wb[:, 0:1024], wbk, ctx.w["L1_attn_proj_w"],
                      128 * dd, 128 * (dd + 1))
            pp = p_c if dd % 2 == 0 else p_d
            pk = ("p_c",) if dd % 2 == 0 else ("p_d",)
            h_mm(ctx, [(wb[:, 128 * g:128 * (g + 1)],
                        bigb[:, g * T + t0:g * T + t0 + 512])
                       for g in range(NDT)],
                 pp[:, 0:512], pk,
                 [wbk] + [("bigb_bf", (g * T + t0, g * T + t0 + 512))
                          for g in range(NDT)])
            def fac2(e, dd=dd, t0=t0, pp=pp):
                return e.tensor_add(x[:, xcols(dd, t0, 512)], pp[:, 0:512],
                                    x[:, xcols(dd, t0, 512)])
            xk = ("x_res", (dd * T + t0, dd * T + t0 + 512))
            h_v(ctx, fac2, [xk], [pk, xk])

    build_norm(ctx, x, "x_res", xb, "xln_bf", kind="rms")
    def fmk1(e):
        return e.tensor_scalar(scr[:, 0:16], mc[:, 112:128], -1.0, 1.0,
                               ALU.mult, ALU.add)
    h_v(ctx, fmk1, [KEY("scr_a", 0, 16)], [("m1_sb",)])
    MU = float(np.sqrt(0.5))
    DEN = float(np.sqrt(1.0 / (4.0 * np.pi)) * np.sqrt(2.0))
    for cc in range(4):
        t0 = 512 * cc
        for g in range(NDT):
            dstb = pa_ if g < 4 else pb_
            dbk = "pool_bf_a" if g < 4 else "pool_bf_b"
            co = 512 * (g % 4)
            def fxk(e, g=g, dstb=dstb, co=co, t0=t0):
                return e.tensor_scalar(dstb[:, co:co + 512],
                                       xb[:, xcols(g, t0, 512)],
                                       scr[:, g:g + 1], None, ALU.mult)
            h_v(ctx, fxk, [(dbk, (co, co + 512))],
                [("xln_bf", (g * T + t0, g * T + t0 + 512)),
                 KEY("scr_a", 0, 16)])
            off = 0 if t0 > 0 else 1
            def fxk2(e, g=g, dstb=dstb, co=co, t0=t0, off=off):
                return e.scalar_tensor_tensor(
                    dstb[:, co + off:co + 512],
                    xb[:, xcols(g, t0 - 1 + off, 512 - off)],
                    mc[:, 112 + g:113 + g],
                    dstb[:, co + off:co + 512], ALU.mult, ALU.add)
            h_v(ctx, fxk2, [(dbk, (co, co + 512))],
                [("xln_bf", (g * T, (g + 1) * T)), ("m1_sb",),
                 (dbk, (co, co + 512))])
        for fj in range(32):
            wb = wb0 if fj % 2 == 0 else wb1
            wbk = ("wband0",) if fj % 2 == 0 else ("wband1",)
            load_band(ctx, wb[:, 0:1024], wbk, ctx.w["L1_attn_key_w"],
                      128 * fj, 128 * (fj + 1))
            pp = p_a if fj % 2 == 0 else p_b
            pk = ("p_a",) if fj % 2 == 0 else ("p_b",)
            pairs = []
            rks = [wbk]
            for g in range(NDT):
                srcb = pa_ if g < 4 else pb_
                pairs.append((wb[:, 128 * g:128 * (g + 1)],
                              srcb[:, 512 * (g % 4):512 * (g % 4 + 1)]))
                rks.append(("pool_bf_a" if g < 4 else "pool_bf_b",
                            (512 * (g % 4), 512 * (g % 4 + 1))))
            h_mm(ctx, pairs, pp[:, 0:512], pk, rks)
            def fer(e, pp=pp):
                return e.activation(scr[:, 512:1024], pp[:, 0:512], AF.Erf,
                                    scale=1.0 / DEN,
                                    bias=ctx.c.col(ctx, -MU / DEN))
            ctx.s.op("scalar", fer, reads=(pk, ("cpool",)),
                     writes=(KEY("scr_a", 512, 1024),))
            def fk5(e, fj=fj):
                return e.tensor_scalar(bigb[:, 512 * fj:512 * (fj + 1)],
                                       scr[:, 512:1024], 1.0, 0.5,
                                       ALU.add, ALU.mult)
            h_v(ctx, fk5, [("bigb_bf", (512 * fj, 512 * (fj + 1)))],
                [KEY("scr_a", 512, 1024)])
        for g in range(NDT):
            dstb = pa_ if g < 4 else pb_
            dbk = "pool_bf_a" if g < 4 else "pool_bf_b"
            co = 512 * (g % 4)
            def fxr(e, g=g, dstb=dstb, co=co, t0=t0):
                return e.tensor_scalar(dstb[:, co:co + 512],
                                       xb[:, xcols(g, t0, 512)],
                                       scr[:, 8 + g:9 + g], None, ALU.mult)
            h_v(ctx, fxr, [(dbk, (co, co + 512))],
                [("xln_bf", (g * T + t0, g * T + t0 + 512)),
                 KEY("scr_a", 0, 16)])
            off = 0 if t0 > 0 else 1
            def fxr2(e, g=g, dstb=dstb, co=co, t0=t0, off=off):
                return e.scalar_tensor_tensor(
                    dstb[:, co + off:co + 512],
                    xb[:, xcols(g, t0 - 1 + off, 512 - off)],
                    mc[:, 120 + g:121 + g],
                    dstb[:, co + off:co + 512], ALU.mult, ALU.add)
            h_v(ctx, fxr2, [(dbk, (co, co + 512))],
                [("xln_bf", (g * T, (g + 1) * T)), ("m1_sb",),
                 (dbk, (co, co + 512))])
        for dd in range(NDT):
            wbv = wb0 if dd % 2 == 0 else wb1
            wbvk = ("wband0",) if dd % 2 == 0 else ("wband1",)
            load_band(ctx, wbv[:, 0:4096], wbvk, ctx.w["L1_attn_value_w"],
                      128 * dd, 128 * (dd + 1), nrows=4096)
            pp = p_c if dd % 2 == 0 else p_d
            pk = ("p_c",) if dd % 2 == 0 else ("p_d",)
            h_mm(ctx, [(wbv[:, 128 * fj:128 * (fj + 1)],
                        bigb[:, 512 * fj:512 * (fj + 1)]) for fj in range(32)],
                 pp[:, 0:512], pk,
                 [wbvk] + [("bigb_bf", (512 * fj, 512 * (fj + 1)))
                           for fj in range(32)])
            wbr = ctx.t["wbandv"]
            load_band(ctx, wbr[:, 0:1024], ("wbandv",),
                      ctx.w["L1_attn_recept_w"], 128 * dd, 128 * (dd + 1))
            pairs = []
            rks = [("wbandv",)]
            for g in range(NDT):
                srcb = pa_ if g < 4 else pb_
                pairs.append((wbr[:, 128 * g:128 * (g + 1)],
                              srcb[:, 512 * (g % 4):512 * (g % 4 + 1)]))
                rks.append(("pool_bf_a" if g < 4 else "pool_bf_b",
                            (512 * (g % 4), 512 * (g % 4 + 1))))
            h_mm(ctx, pairs, p_st[:, 0:512], ("p_stat", (0, 512)), rks)
            def fsg(e):
                return e.activation(scr[:, 512:1024], p_st[:, 0:512],
                                    AF.Sigmoid)
            ctx.s.op("scalar", fsg, reads=(("p_stat", (0, 512)),),
                     writes=(KEY("scr_a", 512, 1024),))
            def fcv(e, pp=pp):
                return e.tensor_tensor(scr[:, 1024:1536], scr[:, 512:1024],
                                       pp[:, 0:512], ALU.mult)
            h_v(ctx, fcv, [KEY("scr_a", 1024, 1536)],
                [KEY("scr_a", 512, 1024), pk])
            def fac3(e, dd=dd, t0=t0):
                return e.tensor_add(x[:, xcols(dd, t0, 512)],
                                    scr[:, 1024:1536],
                                    x[:, xcols(dd, t0, 512)])
            xk_ = ("x_res", (dd * T + t0, dd * T + t0 + 512))
            h_v(ctx, fac3, [xk_], [KEY("scr_a", 1024, 1536), xk_])

    for k in range(NDT):
        h_dma(ctx, ctx.t["wbuf_f32"][:, 0:2048], xsave[:, k * T:(k + 1) * T],
              [KEY("wbuf_f32", 0, 2048)], [("xsave",)], queue="qsync")
        def ffin(e, k=k):
            return e.tensor_add(x[:, xcols(k)], x[:, xcols(k)],
                                ctx.t["wbuf_f32"][:, 0:2048])
        h_v(ctx, ffin, [("x_res", (k * T, (k + 1) * T))],
            [("x_res", (k * T, (k + 1) * T)), KEY("wbuf_f32", 0, 2048)])




P_CAP = 640
SC_CHUNKS = ((0, 384), (384, 256))


def build_ffn1_route(ctx, emit_counts):
    x = ctx.t["x_res"]
    xb = ctx.t["xln_bf"]
    build_norm(ctx, x, "x_res", xb, "xln_bf", kind="ln")
    wrow = ctx.t["wbuf_f32"]
    scr = ctx.t["scr_a"]
    sm = ctx.t["smalltok"]
    wb0 = ctx.t["wband0"]
    onesf = ctx.t["ones_f32"]
    idf = ctx.t["ident_f32"]
    iot = ctx.t["iota8_sb"]
    p_a, p_b = ctx.psum["p_a"], ctx.psum["p_b"]

    def KEY(name, c0, c1, p0=0, p1=128):
        return (name, (c0, c1), (p0, p1))

    load_band(ctx, wb0[:, 0:64], ("wband0",), ctx.w["L1_ffn_w_router"], 0, 8)
    for tt in range(16):
        c0 = 128 * tt
        h_mm(ctx, [(xb[:, xcols(g, c0, 128)], wb0[:, 8 * g:8 * (g + 1)])
                   for g in range(NDT)],
             p_a[:, 0:8], ("p_a",),
             [("wband0",)] + [("xln_bf", (g * T + c0, g * T + c0 + 128))
                              for g in range(NDT)])
        h_v(ctx, lambda e: e.tensor_reduce(scr[:, 16:17], p_a[:, 0:8],
                                           AX.X, ALU.max),
            [KEY("scr_a", 16, 17)], [("p_a",)])
        h_v(ctx, lambda e: e.tensor_scalar(scr[:, 0:8], p_a[:, 0:8],
                                           scr[:, 16:17], None, ALU.subtract),
            [KEY("scr_a", 0, 8)], [("p_a",), KEY("scr_a", 16, 17)])
        ctx.s.op("scalar",
                 lambda e: e.activation(scr[:, 0:8], scr[:, 0:8], AF.Exp,
                                        accum_out=scr[:, 17:18]),
                 reads=(KEY("scr_a", 0, 8),),
                 writes=(KEY("scr_a", 0, 8), KEY("scr_a", 17, 18)))
        h_v(ctx, lambda e: e.reciprocal(scr[:, 18:19], scr[:, 17:18]),
            [KEY("scr_a", 18, 19)], [KEY("scr_a", 17, 18)])
        def fpr(e, tt=tt):
            return e.tensor_scalar(sm[:, 8 * tt:8 * (tt + 1)], scr[:, 0:8],
                                   scr[:, 18:19], None, ALU.mult)
        h_v(ctx, fpr, [("smalltok", (8 * tt, 8 * (tt + 1)))],
            [KEY("scr_a", 0, 19)])
        W0 = 128 + 4 * tt
        def fm1(e, tt=tt, W0=W0):
            return e.tensor_reduce(sm[:, W0:W0 + 1],
                                   sm[:, 8 * tt:8 * (tt + 1)], AX.X, ALU.max)
        h_v(ctx, fm1, [("smalltok", (W0, W0 + 1))],
            [("smalltok", (8 * tt, 8 * (tt + 1)))])
        def fk1(e, tt=tt, W0=W0):
            return e.tensor_scalar(scr[:, 0:8], sm[:, 8 * tt:8 * (tt + 1)],
                                   sm[:, W0:W0 + 1], None, ALU.is_ge)
        h_v(ctx, fk1, [KEY("scr_a", 0, 8)],
            [("smalltok", (8 * tt, 8 * (tt + 1))), ("smalltok", (W0, W0 + 1))])
        def fs2(e, tt=tt):
            return e.scalar_tensor_tensor(scr[:, 8:16], scr[:, 0:8], -1e30,
                                          sm[:, 8 * tt:8 * (tt + 1)],
                                          ALU.mult, ALU.add)
        h_v(ctx, fs2, [KEY("scr_a", 8, 16)],
            [KEY("scr_a", 0, 8), ("smalltok", (8 * tt, 8 * (tt + 1)))])
        def fm2(e, W0=W0):
            return e.tensor_reduce(sm[:, W0 + 1:W0 + 2], scr[:, 8:16],
                                   AX.X, ALU.max)
        h_v(ctx, fm2, [("smalltok", (W0 + 1, W0 + 2))], [KEY("scr_a", 8, 16)])
        def fk2(e, W0=W0):
            return e.tensor_scalar(scr[:, 8:16], scr[:, 8:16],
                                   sm[:, W0 + 1:W0 + 2], None, ALU.is_ge)
        h_v(ctx, fk2, [KEY("scr_a", 8, 16)],
            [KEY("scr_a", 8, 16), ("smalltok", (W0 + 1, W0 + 2))])
        for j, mi in ((0, 0), (1, 8)):
            def fed(e, mi=mi):
                return e.tensor_tensor(scr[:, 24:32], scr[:, mi:mi + 8],
                                       iot[:, 0:8], ALU.mult)
            h_v(ctx, fed, [KEY("scr_a", 24, 32)],
                [KEY("scr_a", 0, 16), ("iota8_sb",)])
            def fer(e, W0=W0, j=j):
                return e.tensor_reduce(sm[:, W0 + 2 + j:W0 + 3 + j],
                                       scr[:, 24:32], AX.X, ALU.add)
            h_v(ctx, fer, [("smalltok", (W0 + 2 + j, W0 + 3 + j))],
                [KEY("scr_a", 24, 32)])
        for cch in range(2):
            def ftr(e, W0=W0, cch=cch):
                return e.transpose(p_b[0:1, 0:128],
                                   sm[:, W0 + 2 + cch:W0 + 3 + cch], idf[:, :])
            ctx.s.op("tensor", ftr,
                     reads=(("smalltok", (W0 + 2 + cch, W0 + 3 + cch)),
                            ("ident_f32",)),
                     writes=(("p_b",),))
            h_v(ctx, lambda e: e.tensor_copy(scr[0:1, 32:160],
                                             p_b[0:1, 0:128]),
                [KEY("scr_a", 32, 160, 0, 1)], [("p_b",)])
            for pb in (0, 32):
                ctx.s.op("tensor",
                         lambda e, pb=pb: e.matmul(
                             p_b[pb:pb + 8, 0:128], onesf[0:1, 0:8],
                             scr[0:1, 32:160], start=True, stop=True),
                         reads=(("ones_f32",), KEY("scr_a", 32, 160, 0, 1)),
                         writes=(("p_b",),))
                def foh(e, tt=tt, cch=cch, pb=pb):
                    return e.tensor_scalar(
                        wrow[pb:pb + 8, 256 * tt:256 * (tt + 1)].rearrange(
                            "p (a c2) -> p a c2", c2=2)[:, :, cch:cch + 1],
                        p_b[pb:pb + 8, 0:128], iot[pb:pb + 8, 8:9],
                        None, ALU.is_equal)
                h_v(ctx, foh,
                    [KEY("wbuf_f32", 256 * tt, 256 * (tt + 1), pb, pb + 8)],
                    [("p_b",), ("iota8_sb",)])
    h_v(ctx, lambda e: e.tensor_tensor_scan(
        wrow[32:40, 0:4096], wrow[32:40, 0:4096], wrow[32:40, 0:4096],
        0.0, ALU.add, ALU.bypass),
        [KEY("wbuf_f32", 0, 4096, 32, 40)],
        [KEY("wbuf_f32", 0, 4096, 32, 40)])
    if emit_counts:
        cout = ctx.nc.dram_tensor("counts_out", [8, 1], F32,
                                  kind="ExternalOutput").ap()
        h_dma(ctx, cout[:, :], wrow[32:40, 4095:4096], [("counts_out",)],
              [KEY("wbuf_f32", 0, 4096, 32, 40)], queue="qsync")


def build_ffn1_ffn(ctx):
    x = ctx.t["x_res"]
    xb = ctx.t["xln_bf"]
    bigb = ctx.t["bigb_bf"]
    qt = ctx.t["qtile_bf"]
    pa_, pb_ = ctx.t["pool_bf_a"], ctx.t["pool_bf_b"]
    wrow = ctx.t["wbuf_f32"]
    scr = ctx.t["scr_a"]
    sm = ctx.t["smalltok"]
    rows = ctx.t["rowpool"]
    wb0, wb1 = ctx.t["wband0"], ctx.t["wband1"]
    onesf = ctx.t["ones_f32"]
    idf, idb = ctx.t["ident_f32"], ctx.t["ident_bf"]
    iot = ctx.t["iota8_sb"]
    cnt = ctx.t["cnt_sb"]
    p_a, p_b, p_c, p_d = (ctx.psum[n] for n in ("p_a", "p_b", "p_c", "p_d"))
    p_st, p_bc = ctx.psum["p_stat"], ctx.psum["p_bc"]

    def KEY(name, c0, c1, p0=0, p1=128):
        return (name, (c0, c1), (p0, p1))

    if "w1bf" not in ctx.dram:
        w1bf = ctx.dscr("w1bf", (8, D, 4 * D), BF16).ap()
        w2bf = ctx.dscr("w2bf", (8, 4 * D, D), BF16).ap()
        for e_ in range(8):
            h_dma(ctx, w1bf[e_], ctx.w["L1_ffn_w1"][e_],
                  [("w1bf", (e_, e_ + 1))], queue="qgp")
            h_dma(ctx, w2bf[e_], ctx.w["L1_ffn_w2"][e_],
                  [("w2bf", (e_, e_ + 1))], queue="qgp")
        slotd = ctx.dscr("slotd", (1, 4096), F32).ap()
        sbuf_d = ctx.dscr("sbuf_d", (8 * P_CAP + 128, D), BF16).ap()
        yslotd = ctx.dscr("yslotd", (8 * P_CAP + 128, D), F32).ap()
    else:
        w1bf, w2bf = ctx.dram["w1bf"].ap(), ctx.dram["w2bf"].ap()
        slotd = ctx.dram["slotd"].ap()
        sbuf_d = ctx.dram["sbuf_d"].ap()
        yslotd = ctx.dram["yslotd"].ap()

    h_dma(ctx, cnt[0:8, 0:8], ctx.w["counts_in"][:, :], [("cnt_sb",)],
          queue="qsync")
    h_dma(ctx, cnt[0:8, 8:16], ctx.w["coremask_in"][:, :], [("cnt_sb",)],
          queue="qsync")
    h_v(ctx, lambda e: e.tensor_tensor(cnt[0:8, 16:24], cnt[0:8, 0:8],
                                       cnt[0:8, 8:16], ALU.mult),
        [("cnt_sb",)], [("cnt_sb",)])
    h_v(ctx, lambda e: e.tensor_reduce(cnt[0:8, 24:25], cnt[0:8, 16:24],
                                       AX.X, ALU.add),
        [("cnt_sb",)], [("cnt_sb",)])
    for ch in range(8):
        F0 = 512 * ch
        def fmv(e, F0=F0):
            return e.matmul(p_bc[0:8, 512:1024], idf[32:40, 32:40],
                            wrow[32:40, F0:F0 + 512], start=True, stop=True)
        ctx.s.op("tensor", fmv,
                 reads=(("ident_f32",), KEY("wbuf_f32", F0, F0 + 512, 32, 40)),
                 writes=(("p_bc", (512, 1024)),))
        def frr(e, F0=F0):
            return e.scalar_tensor_tensor(scr[0:8, 0:512],
                                          p_bc[0:8, 512:1024], 1.0,
                                          wrow[0:8, F0:F0 + 512],
                                          ALU.subtract, ALU.mult)
        h_v(ctx, frr, [KEY("scr_a", 0, 512, 0, 8)],
            [("p_bc", (512, 1024)), KEY("wbuf_f32", F0, F0 + 512, 0, 8)])
        ctx.s.op("tensor",
                 lambda e: e.matmul(p_st[0:1, 0:512], onesf[0:8, 0:1],
                                    scr[0:8, 0:512], start=True, stop=True),
                 reads=(("ones_f32",), KEY("scr_a", 0, 512, 0, 8)),
                 writes=(("p_stat", (0, 512)),))
        h_v(ctx, lambda e: e.tensor_copy(rows[0:1, 0:512], p_st[0:1, 0:512]),
            [("rows", (0, 512), (0, 1))], [("p_stat", (0, 512))])
        def fmp(e, F0=F0):
            return e.matmul(p_st[0:1, 512:1024], cnt[0:8, 24:25],
                            wrow[0:8, F0:F0 + 512], start=True, stop=True)
        ctx.s.op("tensor", fmp,
                 reads=(("cnt_sb",), KEY("wbuf_f32", F0, F0 + 512, 0, 8)),
                 writes=(("p_stat", (512, 1024)),))
        def fme(e, F0=F0):
            return e.matmul(p_bc[0:1, 0:512], iot[0:8, 8:9],
                            wrow[0:8, F0:F0 + 512], start=True, stop=True)
        ctx.s.op("tensor", fme,
                 reads=(("iota8_sb",), KEY("wbuf_f32", F0, F0 + 512, 0, 8)),
                 writes=(("p_bc", (0, 512)),))
        h_v(ctx, lambda e: e.tensor_tensor(rows[0:1, 512:1024],
                                           rows[0:1, 0:512],
                                           p_st[0:1, 512:1024], ALU.add),
            [("rows", (512, 1024), (0, 1))],
            [("rows", (0, 512), (0, 1)), ("p_stat", (512, 1024))])
        h_v(ctx, lambda e: e.tensor_scalar(rows[0:1, 512:1024],
                                           rows[0:1, 512:1024], 4096.0, None,
                                           ALU.is_lt),
            [("rows", (512, 1024), (0, 1))], [("rows", (512, 1024), (0, 1))])
        h_v(ctx, lambda e: e.tensor_scalar(rows[0:1, 1024:1536],
                                           rows[0:1, 0:512], float(P_CAP),
                                           None, ALU.is_lt),
            [("rows", (1024, 1536), (0, 1))], [("rows", (0, 512), (0, 1))])
        h_v(ctx, lambda e: e.tensor_tensor(rows[0:1, 512:1024],
                                           rows[0:1, 512:1024],
                                           rows[0:1, 1024:1536], ALU.mult),
            [("rows", (512, 1024), (0, 1))],
            [("rows", (512, 1024), (0, 1)), ("rows", (1024, 1536), (0, 1))])
        h_v(ctx, lambda e: e.scalar_tensor_tensor(rows[0:1, 0:512],
                                                  p_bc[0:1, 0:512],
                                                  float(P_CAP),
                                                  rows[0:1, 0:512],
                                                  ALU.mult, ALU.add),
            [("rows", (0, 512), (0, 1))],
            [("p_bc", (0, 512)), ("rows", (0, 512), (0, 1))])
        inv = ctx.t["invrow_sb"]
        h_v(ctx, lambda e: e.tensor_sub(rows[0:1, 0:512], rows[0:1, 0:512],
                                        inv[0:1, 0:512]),
            [("rows", (0, 512), (0, 1))],
            [("rows", (0, 512), (0, 1)), ("invrow_sb",)])
        h_v(ctx, lambda e: e.tensor_tensor(rows[0:1, 0:512],
                                           rows[0:1, 512:1024],
                                           rows[0:1, 0:512], ALU.mult),
            [("rows", (0, 512), (0, 1))],
            [("rows", (512, 1024), (0, 1)), ("rows", (0, 512), (0, 1))])
        h_v(ctx, lambda e: e.tensor_tensor(rows[0:1, 0:512], rows[0:1, 0:512],
                                           inv[0:1, 0:512], ALU.add),
            [("rows", (0, 512), (0, 1))],
            [("rows", (0, 512), (0, 1)), ("invrow_sb",)])
        h_dma(ctx, slotd[0:1, F0:F0 + 512], rows[0:1, 0:512], [("slotd",)],
              [("rows", (0, 512), (0, 1))], queue="qsync")
    h_v(ctx, lambda e: e.memset(rows[:, 0:1024], 0.0),
        [("rows", (0, 1024))], [("rows", (0, 1024))])
    h_dma(ctx, yslotd[8 * P_CAP:8 * P_CAP + 128, :], rows[:, 0:1024],
          [("yslotd",)], [("rows", (0, 1024))], queue="qsync")
    if FFN1_STOP < 1:
        return
    h_dma(ctx, ctx.t["idxf_sb"][:, :].rearrange("p (t2 c2) -> p t2 c2", c2=2),
          slotd[0, :].rearrange("(t2 p c2) -> p t2 c2", p=128, c2=2),
          [("idxf_sb",)], [("slotd",)], queue="qsync")
    h_v(ctx, lambda e: e.tensor_copy(ctx.t["idx_sb"][:, :],
                                     ctx.t["idxf_sb"][:, :]),
        [("idx_sb",)], [("idxf_sb",)])
    if FFN1_STOP < 2:
        return
    for tt in range(16):
        for g in range(NDT):
            def ftp(e, tt=tt, g=g):
                return e.transpose(p_d.bitcast(BF16)[0:128, 0:128],
                                   xb[:, xcols(g, 128 * tt, 128)], idb[:, :])
            ctx.s.op("tensor", ftp,
                     reads=(("xln_bf", (g * T + 128 * tt,
                                        g * T + 128 * (tt + 1))),
                            ("ident_bf",)),
                     writes=(("p_d",),))
            def fev(e, g=g):
                return e.tensor_copy(qt[:, 128 * g:128 * (g + 1)],
                                     p_d.bitcast(BF16)[0:128, 0:128])
            h_v(ctx, fev, [("qtile_bf", (128 * g, 128 * (g + 1)))], [("p_d",)])
        for cch in range(2):
            def fsct(e, tt=tt, cch=cch):
                return e.indirect_dma_start(
                    out=sbuf_d[:, :],
                    out_offset=IndirectOffsetOnAxis(
                        ap=ctx.t["idx_sb"][:, 2 * tt + cch:2 * tt + cch + 1],
                        axis=0),
                    in_=qt[:, 0:1024], in_offset=None)
            ctx.s.op("gpsimd", fsct,
                     reads=(("qtile_bf", (0, 1024)), ("idx_sb",)),
                     writes=(("sbuf_d",),), queue="qgp")

    def xs_ap(g, SC):
        if SC == 384:
            if g < 5:
                return pa_[:, 384 * g:384 * (g + 1)], \
                    ("pool_bf_a", (384 * g, 384 * (g + 1)))
            return pb_[:, 384 * (g - 5):384 * (g - 4)], \
                ("pool_bf_b", (384 * (g - 5), 384 * (g - 4)))
        return pa_[:, SC * g:SC * (g + 1)], ("pool_bf_a", (SC * g, SC * (g + 1)))

    if FFN1_STOP < 3:
        return
    GA = 0.044715
    SQ2PI = float(np.sqrt(2.0 / np.pi))
    for e_ in range(8 if FFN1_STOP >= 6 else 1):
        for off, SC in SC_CHUNKS:
            s0 = P_CAP * e_ + off
            for g in range(NDT):
                dst, dk = xs_ap(g, SC)
                h_dmaT(ctx, dst, sbuf_d[s0:s0 + SC, 128 * g:128 * (g + 1)],
                       [dk], [("sbuf_d",)], queue="qscal")
            if FFN1_STOP < 4:
                continue
            for fs in range(32):
                wb = wb0 if fs % 2 == 0 else wb1
                wbk = ("wband0",) if fs % 2 == 0 else ("wband1",)
                h_dma(ctx, wb[:, 0:1024].rearrange("p (g c) -> p g c", g=8),
                      w1bf[e_][:, 128 * fs:128 * (fs + 1)].rearrange(
                          "(g p) c -> p g c", p=128),
                      [wbk], [("w1bf", (e_, e_ + 1))],
                      queue="qsync" if fs % 2 == 0 else "qgp")
                pp = p_a if fs % 2 == 0 else p_b
                pk = ("p_a",) if fs % 2 == 0 else ("p_b",)
                h_mm(ctx, [(wb[:, 128 * g:128 * (g + 1)], xs_ap(g, SC)[0])
                           for g in range(NDT)],
                     pp[:, 0:SC], pk,
                     [wbk] + [xs_ap(g, SC)[1] for g in range(NDT)])
                ctx.s.op("scalar",
                         lambda e, pp=pp, SC=SC: e.activation(
                             scr[:, 0:SC], pp[:, 0:SC], AF.Square),
                         reads=(pk,), writes=(KEY("scr_a", 0, SC),))
                h_v(ctx, lambda e, SC=SC: e.tensor_scalar(
                    scr[:, 0:SC], scr[:, 0:SC], GA, 1.0, ALU.mult, ALU.add),
                    [KEY("scr_a", 0, SC)], [KEY("scr_a", 0, SC)])
                h_v(ctx, lambda e, pp=pp, SC=SC: e.tensor_tensor(
                    scr[:, 768:768 + SC], scr[:, 0:SC], pp[:, 0:SC],
                    ALU.mult),
                    [KEY("scr_a", 768, 768 + SC)],
                    [KEY("scr_a", 0, SC), pk])
                ctx.s.op("scalar",
                         lambda e, SC=SC: e.activation(
                             scr[:, 768:768 + SC], scr[:, 768:768 + SC],
                             AF.Tanh, scale=SQ2PI),
                         reads=(KEY("scr_a", 768, 768 + SC),),
                         writes=(KEY("scr_a", 768, 768 + SC),))
                def g5(e, pp=pp, fs=fs, SC=SC):
                    return e.scalar_tensor_tensor(
                        bigb[:, SC * fs:SC * (fs + 1)], scr[:, 768:768 + SC],
                        1.0, pp[:, 0:SC], ALU.add, ALU.mult)
                h_v(ctx, g5, [("bigb_bf", (SC * fs, SC * (fs + 1)))],
                    [KEY("scr_a", 768, 768 + SC), pk])
            if FFN1_STOP < 5:
                continue
            ngrp = SC // 128
            PTS = [(p_c, ("p_c",), 0), (p_d, ("p_d",), 0),
                   (p_st, ("p_stat",), 0), (p_bc, ("p_bc",), 0),
                   (p_st, ("p_stat",), 512), (p_bc, ("p_bc",), 512)]
            for fs in range(32):
                wb = wb0 if fs % 2 == 0 else wb1
                wbk = ("wband0",) if fs % 2 == 0 else ("wband1",)
                h_dma(ctx, wb[:, 0:1024], w2bf[e_][128 * fs:128 * (fs + 1), :],
                      [wbk], [("w2bf", (e_, e_ + 1))],
                      queue="qsync" if fs % 2 == 0 else "qgp")
                for gi in range(ngrp):
                    for dh in range(2):
                        pt, ptk, pc0 = PTS[2 * gi + dh]
                        def fy(e, fs=fs, gi=gi, pt=pt, pc0=pc0, wb=wb, SC=SC):
                            return e.matmul(
                                pt[0:128, pc0:pc0 + 512],
                                bigb[:, SC * fs + 128 * gi:
                                     SC * fs + 128 * (gi + 1)],
                                wb[:, 512 * dh:512 * (dh + 1)],
                                start=(fs == 0), stop=(fs == 31),
                                skip_group_check=True)
                        ctx.s.op("tensor", fy,
                                 reads=(wbk,
                                        ("bigb_bf", (SC * fs, SC * (fs + 1)))),
                                 writes=(ptk + ((pc0, pc0 + 512), (0, 128)),))
            for gi in range(ngrp):
                for dh in range(2):
                    pt, ptk, pc0 = PTS[2 * gi + dh]
                    h_v(ctx, lambda e, pt=pt, pc0=pc0, dh=dh: e.tensor_copy(
                        scr[:, 512 * dh:512 * (dh + 1)],
                        pt[0:128, pc0:pc0 + 512]),
                        [KEY("scr_a", 512 * dh, 512 * (dh + 1))],
                        [ptk + ((pc0, pc0 + 512), (0, 128))])
                def fst(e, gi=gi, s0=s0):
                    return e.dma_start(
                        out=yslotd[s0 + 128 * gi:s0 + 128 * (gi + 1), :],
                        in_=scr[:, 0:1024])
                ctx.s.op("sync", fst, reads=(KEY("scr_a", 0, 1024),),
                         writes=(("yslotd",),), queue="qsync")
    if FFN1_STOP < 7:
        return
    for tt in range(16):
        for cch in range(2):
            tgt = wrow[:, 1024 * cch:1024 * (cch + 1)]
            tk = KEY("wbuf_f32", 1024 * cch, 1024 * (cch + 1))
            h_v(ctx, lambda e, tgt=tgt: e.memset(tgt, 0.0), [tk], [tk])
            def fgb(e, tt=tt, cch=cch, tgt=tgt):
                return e.indirect_dma_start(
                    out=tgt, out_offset=None,
                    in_=yslotd[:, :],
                    in_offset=IndirectOffsetOnAxis(
                        ap=ctx.t["idx_sb"][:, 2 * tt + cch:2 * tt + cch + 1],
                        axis=0))
            ctx.s.op("gpsimd", fgb, reads=(("yslotd",), ("idx_sb",), tk),
                     writes=(tk,), queue="qgp")
        W0 = 128 + 4 * tt
        def fc1(e, W0=W0):
            return e.tensor_scalar(wrow[:, 0:1024], wrow[:, 0:1024],
                                   sm[:, W0:W0 + 1], None, ALU.mult)
        h_v(ctx, fc1, [KEY("wbuf_f32", 0, 1024)],
            [KEY("wbuf_f32", 0, 1024), ("smalltok", (W0, W0 + 2))])
        def fc2(e, W0=W0):
            return e.scalar_tensor_tensor(wrow[:, 0:1024], wrow[:, 1024:2048],
                                          sm[:, W0 + 1:W0 + 2],
                                          wrow[:, 0:1024], ALU.mult, ALU.add)
        h_v(ctx, fc2, [KEY("wbuf_f32", 0, 1024)],
            [KEY("wbuf_f32", 0, 2048), ("smalltok", (W0, W0 + 2))])
        for g in range(NDT):
            ctx.s.op("tensor",
                     lambda e, g=g: e.transpose(p_d[0:128, 0:128],
                                                wrow[:, 128 * g:128 * (g + 1)],
                                                idf[:, :]),
                     reads=(KEY("wbuf_f32", 0, 1024), ("ident_f32",)),
                     writes=(("p_d",),))
            xk = ("x_res", (g * T + 128 * tt, g * T + 128 * (tt + 1)))
            h_v(ctx, lambda e, g=g, tt=tt: e.scalar_tensor_tensor(
                x[:, xcols(g, 128 * tt, 128)], p_d[0:128, 0:128], 0.25,
                x[:, xcols(g, 128 * tt, 128)], ALU.mult, ALU.add),
                [xk], [("p_d",), xk])
    for g in range(NDT):
        h_v(ctx, lambda e, g=g: e.tensor_scalar(
            x[:, xcols(g)], x[:, xcols(g)],
            ctx.t["b1h_sb"][:, 8 + g:9 + g], None, ALU.add),
            [("x_res", (g * T, (g + 1) * T))],
            [("x_res", (g * T, (g + 1) * T)), ("b1h_sb",)])



STEP_SEQ = ["ffn0a", "attn0", "ffn0b",
            "ffn1a", "attn1", "ffn1b",
            "ffn2a", "attn2", "ffn2b",
            "ffn3a", "attn3", "ffn3b"]


def build_program(steps):
    nc = bass.Bass("TRN2", target_bir_lowering=False, debug=False)
    es = ExitStack()
    consts = ConstMgr()
    sched = Sched(nc)

    w = {}
    for name, shape, dtype in _input_specs(steps):
        w[name] = nc.dram_tensor(name, list(shape), dtype,
                                 kind="ExternalInput").ap()
    xout = nc.dram_tensor("out", [128, NDT * T], F32,
                          kind="ExternalOutput").ap()

    ctx = Ctx(nc, es, sched, consts, w)
    ctx.sb("x_res", (128, NDT * T), F32)
    ctx.sb("xln_bf", (128, NDT * T), BF16)
    ctx.sb("bigb_bf", (128, NDT * T), BF16)
    ctx.sb("scr_a", (128, 1536), F32)
    ctx.sb("wband0", (128, 4096), BF16)
    ctx.sb("wband1", (128, 4096), BF16)
    ctx.sb("wbuf_f32", (128, 4096), F32)
    ctx.sb("cols_sb", (128, 8), F32)
    ctx.sb("qtile_bf", (128, 2560), BF16)
    ctx.sb("rowpool", (128, 2048), F32)
    ctx.sb("ones_f32", (128, 128), F32)
    ctx.sb("ident_bf", (128, 128), BF16)
    ctx.sb("b1_sb", (128, 32), F32)
    ctx.sb("b2h_sb", (128, 8), F32)
    ctx.sb("cpool", (128, CPOOL_N), F32)
    if "attn1" in steps:
        ctx.sb("m1_sb", (128, 128), F32)
        ctx.sb("cw1_sb", (128, 96), F32)
        ctx.sb("tri_bf", (128, 128), BF16)
        ctx.sb("wbandv", (128, 1024), BF16)
        for nm, sh, dt_ in (("pool_bf_a", (128, 2048), BF16),
                            ("pool_bf_b", (128, 2048), BF16),
                            ("cmask_bf", (128, 896), BF16),
                            ("ptile_bf", (128, 512), BF16),
                            ("ones_bf", (128, 2), BF16),
                            ("msb_bf", (128, 1024), BF16),
                            ("smalltok", (128, 1024), F32),
                            ("ident_f32", (128, 128), F32)):
            if nm not in ctx.t:
                ctx.sb(nm, sh, dt_)
    if "attn3" in steps:
        ctx.sb("a3_sb", (128, 128), F32)
        ctx.sb("bias4_sb", (128, 32), F32)
        ctx.sb("convw_sb", (128, 32), F32)
        ctx.sb("bbc_sb", (128, 32), F32)
        ctx.sb("carry_sb", (128, 16), F32)
        for nm in ("pool_bf_a", "pool_bf_b"):
            if nm not in ctx.t:
                ctx.sb(nm, (128, 2048), BF16)
        if "smalltok" not in ctx.t:
            ctx.sb("smalltok", (128, 1024), F32)
        if "ident_f32" not in ctx.t:
            ctx.sb("ident_f32", (128, 128), F32)
    FFN3 = any(s.startswith("ffn3") for s in steps)
    if FFN3:
        for nm in ("pool_bf_a", "pool_bf_b"):
            if nm not in ctx.t:
                ctx.sb(nm, (128, 2048), BF16)
    FFN2 = any(s.startswith("ffn2") for s in steps)
    if FFN2:
        ctx.sb("sel16b", (16, 2048), BF16)
        ctx.sb("rsh_t", (128, 1), F32)
        if "smalltok" not in ctx.t:
            ctx.sb("smalltok", (128, 1024), F32)
        if "ident_f32" not in ctx.t:
            ctx.sb("ident_f32", (128, 128), F32)
        if "pool_bf_a" not in ctx.t:
            ctx.sb("pool_bf_a", (128, 2048), BF16)
    if "attn2" in steps:
        for nm in ("pool_bf_a", "pool_bf_b"):
            if nm not in ctx.t:
                ctx.sb(nm, (128, 2048), BF16)
        ctx.sb("cmask_bf", (128, 896), BF16)
        ctx.sb("ptile_bf", (128, 512), BF16)
        ctx.sb("ones_bf", (128, 2), BF16)
        if "msb_bf" not in ctx.t:
            ctx.sb("msb_bf", (128, 1024), BF16)
    if "attn0" in steps:
        if "smalltok" not in ctx.t:
            ctx.sb("smalltok", (128, 1024), F32)
        if "msb_bf" not in ctx.t:
            ctx.sb("msb_bf", (128, 1024), BF16)
        ctx.sb("sel16", (16, 2048), F32)
        if "ident_f32" not in ctx.t:
            ctx.sb("ident_f32", (128, 128), F32)
        ctx.sb("wtf1_bf", (64, 32), BF16)
        ctx.sb("bfd_t", (128, 16), F32)
        ctx.sb("bsd_t", (128, 16), F32)
        ctx.sb("btf2_t", (128, 1), F32)

    FFN1 = any(s.startswith("ffn1") for s in steps)
    if FFN1:
        ctx.sb("iota8_sb", (128, 16), F32)
        if "smalltok" not in ctx.t:
            ctx.sb("smalltok", (128, 1024), F32)
        if "ident_f32" not in ctx.t:
            ctx.sb("ident_f32", (128, 128), F32)
    if any(s in ("ffn1a", "ffn1b") for s in steps):
        ctx.sb("cnt_sb", (8, 32), F32)
        ctx.sb("idxf_sb", (128, 32), F32)
        ctx.sb("idx_sb", (128, 32), I32)
        ctx.sb("b1h_sb", (128, 16), F32)
        ctx.sb("invrow_sb", (1, 512), F32)
        for nm in ("pool_bf_a", "pool_bf_b"):
            if nm not in ctx.t:
                ctx.sb(nm, (128, 2048), BF16)

    ctx.ps("p_stat", 1024)
    ctx.ps("p_bc", 1024)
    ctx.ps("p_a", 512)
    ctx.ps("p_b", 512)
    ctx.ps("p_c", 512)
    ctx.ps("p_d", 512)

    with nc.Block() as block:
        sched.setup(es, block)
        h_dma(ctx, ctx.t["ones_f32"][:, :], w["ones_f32_in"][:, :],
              [("ones_f32",)])
        h_dma(ctx, ctx.t["cpool"][:, :], w["cpool_in"][:, :], [("cpool",)])
        h_dma(ctx, ctx.t["ident_bf"][:, :], w["ident_bf_in"][:, :],
              [("ident_bf",)], queue="qgp")
        h_dma(ctx, ctx.t["x_res"][:, :], w["xfm"][:, :], [("x_res",)])
        if any(s.startswith("ffn0") for s in steps):
            h_dma(ctx, ctx.t["b1_sb"][:, :], w["L0_ffn_b1r"][:, :],
                  [("b1_sb",)])
            h_dma(ctx, ctx.t["b2h_sb"][:, :], w["L0_ffn_b2r"][:, :],
                  [("b2h_sb",)])
        if any(s.startswith("ffn2") for s in steps):
            h_dma(ctx, ctx.t["sel16b"][:, :], w["sel16b_in"][:, :],
                  [("sel16b",)], queue="qgp")
            h_dma(ctx, ctx.t["rsh_t"][:, :], w["rsh_in"][:, :], [("rsh_t",)])
            h_v(ctx, lambda e: e.tensor_scalar_mul(
                ctx.t["rsh_t"][:, :], ctx.t["rsh_t"][:, :], 0.5),
                [("rsh_t",)], [("rsh_t",)])
            if "ident_f32_in" in w and "attn0" not in steps and "attn2" in steps:
                pass
        if "attn2" in steps:
            h_dma(ctx, ctx.t["cmask_bf"][:, :], w["cmask_in"][:, :],
                  [("cmask_bf",)], queue="qgp")
            h_dma(ctx, ctx.t["ones_bf"][:, :], w["ones_bf_in"][:, :],
                  [("ones_bf",)], queue="qgp")
        if "attn1" in steps:
            for nm in ("m1_sb", "cw1_sb"):
                h_dma(ctx, ctx.t[nm][:, :], w[nm + "_in"][:, :], [(nm,)])
            h_dma(ctx, ctx.t["tri_bf"][:, :], w["tri_in"][:, :], [("tri_bf",)],
                  queue="qgp")
            if "cmask_in" in w:
                h_dma(ctx, ctx.t["cmask_bf"][:, :], w["cmask_in"][:, :],
                      [("cmask_bf",)], queue="qgp")
            if "ones_bf_in" in w:
                h_dma(ctx, ctx.t["ones_bf"][:, :], w["ones_bf_in"][:, :],
                      [("ones_bf",)], queue="qgp")
        if "attn3" in steps:
            for nm in ("a3_sb", "bias4_sb", "convw_sb", "bbc_sb"):
                h_dma(ctx, ctx.t[nm][:, :], w[nm + "_in"][:, :], [(nm,)])
        if "ident_f32" in ctx.t and "attn0" not in steps:
            h_dma(ctx, ctx.t["ident_f32"][:, :], w["ident_f32_in"][:, :],
                  [("ident_f32",)])
        if any(s.startswith("ffn1") for s in steps):
            h_dma(ctx, ctx.t["iota8_sb"][:, :], w["iota8_in"][:, :],
                  [("iota8_sb",)])
        if any(s in ("ffn1a", "ffn1b") for s in steps):
            h_dma(ctx, ctx.t["b1h_sb"][:, 0:8], w["b1f_in"][:, :],
                  [("b1h_sb",)])
            h_dma(ctx, ctx.t["invrow_sb"][:, :], w["invrow_in"][:, :],
                  [("invrow_sb",)])
            h_v(ctx, lambda e: e.tensor_scalar_mul(
                ctx.t["b1h_sb"][:, 8:16], ctx.t["b1h_sb"][:, 0:8], 0.5),
                [("b1h_sb",)], [("b1h_sb",)])
        if "attn0" in steps and ATTN0_STOP > -1:
            for nm in ("sel16", "ident_f32", "bfd_t", "bsd_t", "btf2_t"):
                if nm == "sel16" and ATTN0_STOP == -0.5:
                    continue
                h_dma(ctx, ctx.t[nm][:, :], w[nm + "_in"][:, :], [(nm,)])
        sched.barrier_all()
        if any(s.startswith("ffn0") for s in steps):
            h_v(ctx, lambda e: e.tensor_scalar_mul(
                ctx.t["b2h_sb"][:, :], ctx.t["b2h_sb"][:, :], 0.5),
                [("b2h_sb",)], [("b2h_sb",)])
            build_l0ffn_prep(ctx)
        for st in steps:
            if st in ("ffn0a", "ffn0b"):
                build_ffn0(ctx)
            elif st == "attn0":
                build_attn0(ctx)
            elif st == "attn2":
                build_attn2(ctx)
            elif st in ("ffn2a", "ffn2b"):
                build_ffn2(ctx)
            elif st == "attn1":
                build_attn1(ctx)
            elif st == "attn3":
                build_attn3(ctx)
            elif st in ("ffn3a", "ffn3b"):
                build_ffn3(ctx, st == "ffn3a" or "ffn3a" not in steps)
            elif st in ("ffn1ar", "ffn1br"):
                build_ffn1_route(ctx, emit_counts=True)
            elif st in ("ffn1a", "ffn1b"):
                build_ffn1_route(ctx, emit_counts=False)
                build_ffn1_ffn(ctx)
            else:
                raise ValueError("step not implemented: " + st)
        h_dma(ctx, xout[:, :], ctx.t["x_res"][:, :],
              [("out",)], [("x_res",)])
        if DBG_DUMP == "wrows":
            dbg = nc.dram_tensor("dbg_wrows", [16, 2048], BF16,
                                 kind="ExternalOutput").ap()
            h_dma(ctx, dbg[:, :], ctx.t["pool_bf_a"][0:16, 0:2048],
                  [("dbgw",)], [("pool_bf_a",)])
            dbg2 = nc.dram_tensor("dbg_probs", [128, 256], F32,
                                  kind="ExternalOutput").ap()
            h_dma(ctx, dbg2[:, :], ctx.t["smalltok"][:, 0:256],
                  [("dbgp",)], [("smalltok",)])
        if DBG_DUMP == "bigb":
            dbg = nc.dram_tensor("dbg_bigb", [128, NDT * T], BF16,
                                 kind="ExternalOutput").ap()
            h_dma(ctx, dbg[:, :], ctx.t["bigb_bf"][:, :],
                  [("dbgb",)], [("bigb_bf",)])
        if DBG_DUMP and "w1qt" in ctx.dram:
            dq1 = nc.dram_tensor("dbg_q1", [D, 4 * D], BF16,
                                 kind="ExternalOutput").ap()
            dq2 = nc.dram_tensor("dbg_q2", [4 * D, D], BF16,
                                 kind="ExternalOutput").ap()
            h_dma(ctx, dq1[:, :], ctx.dram["w1qt"].ap()[:, :],
                  [("dbgq1",)], [("q1",)])
            h_dma(ctx, dq2[:, :], ctx.dram["w2qt"].ap()[:, :],
                  [("dbgq2",)], [("q2",)])
        sched.final_wait()
    es.close()

    assert ctx.sbuf_bytes < 208 * 1024, ctx.sbuf_bytes
    return nc, consts.pool(CPOOL_N)


def _input_specs(steps):
    specs = [
        ("xfm", (128, NDT * T), F32),
        ("ones_f32_in", (128, 128), F32),
        ("ident_bf_in", (128, 128), F32),
        ("cpool_in", (128, CPOOL_N), F32),
    ]
    if any(s.startswith("ffn0") for s in steps):
        specs += [
            ("L0_ffn_W1T", (D, 4 * D), F32),
            ("L0_ffn_W2T", (4 * D, D), F32),
            ("L0_ffn_b1r", (128, 32), F32),
            ("L0_ffn_b2r", (128, 8), F32),
        ]
    if "attn1" in steps:
        specs += [
            ("L1_attn_in_proj_w", (D, 4256), F32),
            ("L1_attn_out_proj_w", (2048, D), F32),
            ("L1_attn_attn_w", (D, 1152), F32),
            ("L1_attn_proj_w", (D, D), F32),
            ("L1_attn_key_w", (D, 4 * D), F32),
            ("L1_attn_recept_w", (D, D), F32),
            ("L1_attn_value_w", (4 * D, D), F32),
            ("m1_sb_in", (128, 128), F32), ("cw1_sb_in", (128, 96), F32),
            ("tri_in", (128, 128), F32),
        ]
        if "attn2" not in steps:
            specs += [("cmask_in", (128, 896), F32),
                      ("ones_bf_in", (128, 2), F32)]
        if not any(s in ("attn0", "attn2", "attn3") or s.startswith("ffn2")
                   for s in steps):
            specs += [("ident_f32_in", (128, 128), F32)]
    if "attn3" in steps:
        specs += [
            ("L3_attn_Win", (D, D), F32), ("L3_attn_Wg", (D, D), F32),
            ("L3_attn_Wout", (D, D), F32), ("L3_attn_Wdt", (D, D), F32),
            ("L3_attn_Wbc", (D, 32), F32),
            ("a3_sb_in", (128, 128), F32), ("bias4_sb_in", (128, 32), F32),
            ("convw_sb_in", (128, 32), F32), ("bbc_sb_in", (128, 32), F32),
        ]
        if not any(s in ("attn0", "attn2") or s.startswith("ffn2")
                   for s in steps):
            specs += [("ident_f32_in", (128, 128), F32)]
    if any(s.startswith("ffn3") for s in steps):
        specs += [
            ("L3_ffn_cprojT", (16 * D, D), F32),
            ("L3_ffn_bd1", (4, 8, 128, 128), F32),
            ("L3_ffn_bd2", (4, 8, 128, 512), F32),
        ]
    if any(s.startswith("ffn2") for s in steps):
        specs += [
            ("L2_ffn_selT", (D, 16), F32),
            ("L2_ffn_keys", (16, D, 256), F32),
            ("L2_ffn_values", (16, 256, D), F32),
            ("sel16b_in", (16, 2048), F32),
            ("rsh_in", (128, 1), F32),
        ]
        if not any("attn0" in s or "attn2" in s for s in steps):
            specs += [("ident_f32_in", (128, 128), F32)]
    if "attn2" in steps:
        specs += [
            ("L2_attn_wqT", (D, D), F32), ("L2_attn_wkT", (D, D), F32),
            ("L2_attn_wvT", (D, D), F32), ("L2_attn_woT", (D, D), F32),
            ("L2_attn_aqT", (D, 16), F32), ("L2_attn_akT", (D, 16), F32),
            ("L2_attn_avT", (D, 16), F32), ("L2_attn_aoT", (D, 16), F32),
            ("L2_attn_b3", (16, 3, D), F32), ("L2_attn_boT", (16, D), F32),
            ("cmask_in", (128, 896), F32), ("ones_bf_in", (128, 2), F32),
        ]
    if any(s.startswith("ffn1") for s in steps):
        specs += [("L1_ffn_w_router", (D, 8), F32), ("iota8_in", (128, 16), F32)]
        if not any(n == "ident_f32_in" for n, _, _ in specs) \
                and "attn0" not in steps:
            specs += [("ident_f32_in", (128, 128), F32)]
    if any(s in ("ffn1a", "ffn1b") for s in steps):
        specs += [
            ("L1_ffn_w1", (8, D, 4 * D), F32),
            ("L1_ffn_w2", (8, 4 * D, D), F32),
            ("b1f_in", (128, 8), F32),
            ("invrow_in", (1, 512), F32),
            ("counts_in", (8, 8), F32),
            ("coremask_in", (8, 8), F32),
        ]
    if "attn0" in steps:
        specs += [
            ("L0_attn_Wq", (D, D), F32), ("L0_attn_Wk", (D, D), F32),
            ("L0_attn_Wv", (D, D), F32), ("L0_attn_Wo", (D, D), F32),
            ("L0_attn_Wbfs", (D, 48), F32), ("L0_attn_Wtf1", (64, 32), F32),
            ("sel16_in", (16, 2048), F32), ("ident_f32_in", (128, 128), F32),
            ("bfd_t_in", (128, 16), F32), ("bsd_t_in", (128, 16), F32),
            ("btf1_t_in", (128, 512), F32), ("wtf2_t_in", (128, 512), F32),
            ("btf2_t_in", (128, 1), F32),
        ]
    return specs


def to_fm(x2d):
    return np.ascontiguousarray(
        x2d.reshape(T, NDT, 128).transpose(2, 1, 0).reshape(128, NDT * T))


def from_fm(xfm):
    return np.ascontiguousarray(
        xfm.reshape(128, NDT, T).transpose(2, 1, 0).reshape(T, D))


def retile_vec(v):
    n = v.shape[0] // 128
    return np.ascontiguousarray(v.reshape(n, 128).T)


def stage_inputs(steps, inputs, consts_pool, x_percore=None, counts=None):
    base = {
        "ones_f32_in": np.ones((128, 128), np.float32),
        "ident_bf_in": np.eye(128, dtype=np.float32),
        "cpool_in": consts_pool,
    }
    if any(s.startswith("ffn1") for s in steps):
        base["L1_ffn_w_router"] = np.ascontiguousarray(
            inputs["L1_ffn_w_router"])
        io = np.zeros((128, 16), np.float32)
        io[:, 0:8] = np.arange(8, dtype=np.float32)[None, :]
        io[:, 8] = (np.arange(128) % 8).astype(np.float32)
        base["iota8_in"] = io
        base["ident_f32_in"] = np.eye(128, dtype=np.float32)
    if any(s in ("ffn1a", "ffn1b") for s in steps):
        base["L1_ffn_w1"] = np.ascontiguousarray(inputs["L1_ffn_w1"])
        base["L1_ffn_w2"] = np.ascontiguousarray(inputs["L1_ffn_w2"])
        base["b1f_in"] = retile_vec(
            np.asarray(inputs["L1_ffn_bias"]).reshape(-1))
        base["invrow_in"] = (8 * 640 + (np.arange(512) // 2) % 128).astype(
            np.float32).reshape(1, 512)
        base["counts_in"] = np.ascontiguousarray(counts.T)
    if "attn1" in steps:
        for nm in ("in_proj_w", "out_proj_w", "attn_w", "proj_w", "key_w",
                   "recept_w", "value_w"):
            base["L1_attn_" + nm] = np.ascontiguousarray(
                inputs["L1_attn_" + nm])
        m1 = np.zeros((128, 128), np.float32)
        m1[:, 0:32] = np.tile(inputs["L1_attn_dt_bias"][None, :], (128, 1))
        m1[0:32, 32] = inputs["L1_attn_A_log"]
        m1[:, 64:96] = np.tile(inputs["L1_attn_D"][None, :], (128, 1))
        m1[:, 96:112] = retile_vec(inputs["L1_attn_mnorm_w"])
        m1[:, 112:120] = retile_vec(inputs["L1_attn_time_maa_k"])
        m1[:, 120:128] = retile_vec(inputs["L1_attn_time_maa_r"])
        base["m1_sb_in"] = m1
        cw1 = np.zeros((128, 96), np.float32)
        cwr = inputs["L1_attn_conv_w"].reshape(17, 128, 4).transpose(1, 0, 2)
        cw1[:, 0:68] = cwr.reshape(128, 68)
        cw1[:, 68:85] = inputs["L1_attn_conv_b"].reshape(17, 128).T
        base["cw1_sb_in"] = cw1
        jj, ii = np.meshgrid(np.arange(128), np.arange(128), indexing="ij")
        base["tri_in"] = (jj <= ii).astype(np.float32)
        if "cmask_in" not in base:
            cm = (np.arange(896)[None, :] >= (np.arange(128)[:, None] + 384))
            base["cmask_in"] = cm.astype(np.float32)
            base["ones_bf_in"] = np.ones((128, 2), np.float32)
        base["ident_f32_in"] = np.eye(128, dtype=np.float32)
    if "attn3" in steps:
        for nm in ("Win", "Wg", "Wout", "Wdt"):
            base["L3_attn_" + nm] = np.ascontiguousarray(inputs["L3_attn_" + nm])
        base["L3_attn_Wbc"] = np.ascontiguousarray(np.concatenate(
            [inputs["L3_attn_Wb"], inputs["L3_attn_Wc"]], axis=1))
        base["a3_sb_in"] = np.ascontiguousarray(
            inputs["L3_attn_A"].reshape(8, 128, 16).transpose(1, 0, 2)
            .reshape(128, 128))
        b4 = np.zeros((128, 32), np.float32)
        b4[:, 0:8] = retile_vec(inputs["L3_attn_b_in"])
        b4[:, 8:16] = retile_vec(inputs["L3_attn_bg"])
        b4[:, 16:24] = retile_vec(inputs["L3_attn_bdt"])
        b4[:, 24:32] = retile_vec(inputs["L3_attn_bout"])
        base["bias4_sb_in"] = b4
        cw = np.zeros((128, 32), np.float32)
        cwr = inputs["L3_attn_convw"].reshape(8, 128, 3).transpose(1, 0, 2)
        cw[:, 0:24] = cwr.reshape(128, 24)
        cw[:, 24:32] = retile_vec(inputs["L3_attn_convb"])
        base["convw_sb_in"] = cw
        bbc = np.zeros((128, 32), np.float32)
        bbc[:, 0:16] = np.tile(inputs["L3_attn_bb"][None, :], (128, 1))
        bbc[:, 16:32] = np.tile(inputs["L3_attn_bc"][None, :], (128, 1))
        base["bbc_sb_in"] = bbc
        base["ident_f32_in"] = np.eye(128, dtype=np.float32)
    if any(s.startswith("ffn3") for s in steps):
        base["L3_ffn_cprojT"] = np.ascontiguousarray(inputs["L3_ffn_w_cproj"].T)
        mk = inputs["L3_ffn_rail_maker"]
        up = inputs["L3_ffn_rail_up_proj"]
        bd1 = np.zeros((4, 8, 128, 128), np.float32)
        bd2 = np.zeros((4, 8, 128, 512), np.float32)
        for h in range(4):
            for rg in range(8):
                for j in range(8):
                    r = 8 * rg + j
                    bd1[h, rg, 16 * j:16 * (j + 1), 16 * j:16 * (j + 1)] = mk[h, r]
                    bd2[h, rg, 16 * j:16 * (j + 1), 64 * j:64 * (j + 1)] = \
                        up[h, r].T
        base["L3_ffn_bd1"] = bd1
        base["L3_ffn_bd2"] = bd2
    if any(s.startswith("ffn2") for s in steps):
        base["L2_ffn_selT"] = np.ascontiguousarray(inputs["L2_ffn_expert_sel"].T)
        base["L2_ffn_keys"] = np.ascontiguousarray(inputs["L2_ffn_keys"])
        base["L2_ffn_values"] = np.ascontiguousarray(inputs["L2_ffn_values"])
        sel = np.zeros((16, 2048), np.float32)
        for hh in range(16):
            sel[hh, 128 * hh:128 * (hh + 1)] = 1.0
        base["sel16b_in"] = sel
        base["rsh_in"] = np.tile(
            inputs["L2_ffn_route_scale"].astype(np.float32).reshape(1, 1),
            (128, 1))
        base["ident_f32_in"] = np.eye(128, dtype=np.float32)
    if "attn2" in steps:
        for s_, d_ in (("wq", "wqT"), ("wk", "wkT"), ("wv", "wvT"),
                       ("wo", "woT")):
            base["L2_attn_" + d_] = np.ascontiguousarray(
                inputs["L2_attn_" + s_].T)
        for s_, d_ in (("aq", "aqT"), ("ak", "akT"), ("av", "avT"),
                       ("ao", "aoT")):
            base["L2_attn_" + d_] = np.ascontiguousarray(
                inputs["L2_attn_" + s_].T)
        base["L2_attn_b3"] = np.ascontiguousarray(np.stack(
            [inputs["L2_attn_bq"].T, inputs["L2_attn_bk"].T,
             inputs["L2_attn_bv"].T], axis=1)).astype(np.float32)
        base["L2_attn_boT"] = np.ascontiguousarray(
            inputs["L2_attn_bo"].T).astype(np.float32)
        cm = (np.arange(896)[None, :] >= (np.arange(128)[:, None] + 384))
        base["cmask_in"] = cm.astype(np.float32)
        base["ones_bf_in"] = np.ones((128, 2), np.float32)
    if "attn0" in steps:
        for nm in ("L0_attn_Wq", "L0_attn_Wk", "L0_attn_Wv", "L0_attn_Wo",
                   "L0_attn_Wtf1"):
            base[nm] = np.ascontiguousarray(inputs[nm])
        base["L0_attn_Wbfs"] = np.ascontiguousarray(np.concatenate(
            [inputs["L0_attn_Wb"], inputs["L0_attn_Wfd"], inputs["L0_attn_Wsd"]],
            axis=1))
        sel = np.zeros((16, 2048), np.float32)
        for hh in range(16):
            sel[hh, 128 * hh:128 * (hh + 1)] = 1.0
        base["sel16_in"] = sel
        base["ident_f32_in"] = np.eye(128, dtype=np.float32)
        base["bfd_t_in"] = np.tile(inputs["L0_attn_bfd"][None, :], (128, 1)).astype(np.float32)
        base["bsd_t_in"] = np.tile(inputs["L0_attn_bsd"][None, :], (128, 1)).astype(np.float32)
        base["btf1_t_in"] = np.tile(inputs["L0_attn_btf1"][None, :], (128, 16)).astype(np.float32)
        base["wtf2_t_in"] = np.tile(inputs["L0_attn_Wtf2"][:, 0][None, :], (128, 16)).astype(np.float32)
        base["btf2_t_in"] = np.tile(inputs["L0_attn_btf2"][None, :], (128, 1)).astype(np.float32)
    if any(s.startswith("ffn0") for s in steps):
        base["L0_ffn_W1T"] = np.ascontiguousarray(inputs["L0_ffn_W1"].T)
        base["L0_ffn_W2T"] = np.ascontiguousarray(inputs["L0_ffn_W2"].T)
        base["L0_ffn_b1r"] = retile_vec(inputs["L0_ffn_b1"])
        base["L0_ffn_b2r"] = retile_vec(inputs["L0_ffn_b2"])
    in_maps = []
    for c in range(8):
        m = dict(base)
        if x_percore is not None:
            m["xfm"] = np.ascontiguousarray(x_percore[c])
        else:
            m["xfm"] = to_fm(np.asarray(inputs["L0_attn_x"][c]))
        if any(s in ("ffn1a", "ffn1b") for s in steps):
            cm = np.zeros((8, 8), np.float32)
            cm[:, 0:c] = 1.0
            m["coremask_in"] = cm
        in_maps.append(m)
    return in_maps


def run_steps(steps, inputs, x_percore=None, counts=None):
    nc, pool = build_program(steps)
    in_maps = stage_inputs(steps, inputs, pool, x_percore, counts)
    res = run_bass_kernel_spmd(nc, in_maps, core_ids=list(range(8)))
    return res.results


def kernel(**inputs):
    r1 = run_steps(["ffn0a", "attn0", "ffn0b", "ffn1ar"], inputs)
    x1 = [r["out"] for r in r1]
    c1 = np.stack([r["counts_out"][:, 0] for r in r1])
    r2 = run_steps(["ffn1a", "attn1", "ffn1br"], inputs,
                   x_percore=x1, counts=c1)
    x2 = [r["out"] for r in r2]
    c2 = np.stack([r["counts_out"][:, 0] for r in r2])
    r3 = run_steps(["ffn1b", "ffn2a", "attn2", "ffn2b",
                    "ffn3a", "attn3", "ffn3b"], inputs,
                   x_percore=x2, counts=c2)
    return np.stack([from_fm(r["out"]) for r in r3], axis=0)


# version 04 (final)
# speedup vs baseline: 1.0043x; 1.0043x over previous
import math
import numpy as np
from contextlib import ExitStack

import concourse.bass as bass
import concourse.mybir as mybir
from concourse.bass_utils import run_bass_kernel_spmd
from concourse.bass_core_rust import IndirectOffsetOnAxis

F32 = mybir.dt.float32
BF16 = mybir.dt.bfloat16
I32 = mybir.dt.int32
AF = mybir.ActivationFunctionType
ALU = mybir.AluOpType
AX = mybir.AxisListType

B, T, D = 8, 2048, 1024
NDT = D // 128

ENGINES = ("tensor", "vector", "scalar", "gpsimd", "sync")
QUEUES = {"qsync": "sync", "qgp": "gpsimd", "qscal": "scalar"}
NSLOT = 3
QSLOTS = [q + "#" + str(i) for q in QUEUES for i in range(NSLOT)]


class Sched:

    def __init__(self, nc):
        self.nc = nc
        self.block = None
        self.count = {e: 0 for e in ENGINES}
        self.dma_count = {q: 0 for q in QUEUES}
        self.slot_count = {s: 0 for s in QSLOTS}
        self.sem = {}
        self.last_writer = {}
        self.readers = {}
        self.vc = {e: {} for e in ENGINES}
        self.snaps = {c: [] for c in list(ENGINES) + QSLOTS}
        self.n_waits = 0
        self.n_ops = 0

    def setup(self, es, block):
        for name in list(ENGINES) + QSLOTS:
            self.sem[name] = es.enter_context(self.nc.semaphore("sem_" + name))
        self.block = block

    def _snap_at(self, clk, val):
        lst = self.snaps[clk]
        lo, hi = 0, len(lst)
        while lo < hi:
            mid = (lo + hi) // 2
            if lst[mid][0] <= val:
                lo = mid + 1
            else:
                hi = mid
        return lst[lo - 1][1] if lo > 0 else {}

    def _collect_wait(self, eng, clk, val, waits):
        if self.vc[eng].get(clk, 0) >= val:
            return
        waits[clk] = max(waits.get(clk, 0), val)

    def _apply_wait(self, eng, clk, val):
        snap = self._snap_at(clk, val)
        vc = self.vc[eng]
        for k, v in snap.items():
            if vc.get(k, 0) < v:
                vc[k] = v
        if vc.get(clk, 0) < val:
            vc[clk] = val

    @staticmethod
    def _kconf(a, b):
        if a[0] != b[0]:
            return False
        for i in (1, 2):
            ra = a[i] if len(a) > i else None
            rb = b[i] if len(b) > i else None
            if ra is None or rb is None:
                continue
            if not (isinstance(ra, tuple) and isinstance(rb, tuple)):
                return True
            if ra[1] <= rb[0] or rb[1] <= ra[0]:
                return False
        return True

    def _conflicts(self, a):
        out = []
        for k in self.last_writer:
            if self._kconf(k, a):
                out.append(k)
        for k in self.readers:
            if k not in self.last_writer and self._kconf(k, a):
                out.append(k)
        return out

    def op(self, eng, fn, reads=(), writes=(), queue=None):
        assert eng in ENGINES
        if PARANOID:
            self.barrier_all()
        clk = queue if queue is not None else eng
        waits = {}
        def edge(src_clk, src_val):
            if src_clk == eng == "tensor":
                return
            self._collect_wait(eng, src_clk, src_val, waits)
        for r in reads:
            for k in self._conflicts(r):
                w = self.last_writer.get(k)
                if w is not None:
                    edge(w[0], w[1])
        for wkey in writes:
            for k in self._conflicts(wkey):
                w = self.last_writer.get(k)
                if w is not None:
                    edge(w[0], w[1])
                for r in self.readers.get(k, ()):
                    edge(r[0], r[1])
        if queue is not None:
            slot = self.dma_count[queue] % NSLOT
            clk = queue + "#" + str(slot)
            if self.slot_count[clk]:
                self._collect_wait(eng, clk, 16 * self.slot_count[clk], waits)
            self.dma_count[queue] += 1
            self.slot_count[clk] += 1
            myval = 16 * self.slot_count[clk]
            inc = 16
        else:
            self.count[eng] += 1
            myval = self.count[eng]
            inc = 1
        for ck, cv in waits.items():
            self._apply_wait(eng, ck, cv)
        self.n_waits += len(waits)
        self.n_ops += 1
        sem = self.sem

        def body(engobj):
            for ck, cv in waits.items():
                engobj.wait_ge(sem[ck], cv)
            last = fn(engobj)
            last.then_inc(sem[clk], inc)

        getattr(self.block, eng)(body)

        vc_now = dict(self.vc[eng])
        vc_now[clk] = myval
        if queue is not None:
            vc_now[eng] = self.count[eng]
        self.snaps[clk].append((myval, vc_now))
        for r in reads:
            self.readers.setdefault(r, []).append((clk, myval))
        for wkey in writes:
            self.last_writer[wkey] = (clk, myval)
            self.readers[wkey] = []
        return myval

    def raw_wait(self, eng, clk, val):
        if self.vc[eng].get(clk, 0) >= val:
            return
        sem = self.sem

        def body(engobj):
            engobj.wait_ge(sem[clk], val)

        getattr(self.block, eng)(body)
        self._apply_wait(eng, clk, val)

    def barrier_all(self):
        for e in ENGINES:
            for s in QSLOTS:
                if self.slot_count[s]:
                    self.raw_wait(e, s, 16 * self.slot_count[s])
            for e2 in ENGINES:
                if e2 != e and self.count[e2]:
                    self.raw_wait(e, e2, self.count[e2])

    def final_wait(self):
        for s in QSLOTS:
            tot = 16 * self.slot_count[s]
            if tot:
                self.raw_wait("gpsimd" if not s.startswith("qgp") else "sync",
                              s, tot)


class ConstMgr:
    def __init__(self):
        self.scalars = {}

    def scalar(self, v):
        v = float(np.float32(v))
        if v not in self.scalars:
            self.scalars[v] = len(self.scalars)
        return self.scalars[v]

    def col(self, ctx, v):
        i = self.scalar(v)
        assert i < CPOOL_N
        return ctx.t["cpool"][:, i:i + 1]

    def pool(self, n):
        vals = np.zeros((128, n), np.float32)
        for v, i in self.scalars.items():
            vals[:, i] = v
        return vals


CPOOL_N = 32
DBG_DUMP = False
ATTN0_STOP = 99
ATTN2_BARRIER = False
PARANOID = False
FFN3_STOP = 99
FFN1_STOP = 99


class Ctx:
    def __init__(self, nc, es, sched, consts, weights):
        self.nc = nc
        self.es = es
        self.s = sched
        self.c = consts
        self.w = weights
        self.sbuf_bytes = 0
        self.t = {}
        self.dram = {}
        self.psum = {}

    def sb(self, name, shape, dtype=F32):
        t = self.es.enter_context(self.nc.sbuf_tensor(name, list(shape), dtype))
        self.sbuf_bytes += int(np.prod(shape[1:])) * (2 if dtype == BF16 else 4)
        self.t[name] = t
        return t

    def ps(self, name, cols=512, dtype=F32):
        t = self.es.enter_context(self.nc.psum_tensor([128, cols], dtype))
        self.psum[name] = t
        return t

    def dscr(self, name, shape, dtype=F32):
        t = self.nc.dram_tensor(name, list(shape), dtype, kind="Internal")
        self.dram[name] = t
        return t



def h_dma(ctx, dst, src, dkeys, skeys=(), queue="qsync"):
    eng = QUEUES[queue]
    ctx.s.op(eng, lambda e: e.dma_start(out=dst, in_=src),
             reads=tuple(skeys), writes=tuple(dkeys), queue=queue)


def h_dmaT(ctx, dst, src, dkeys, skeys=(), queue="qscal"):
    eng = QUEUES[queue]
    ctx.s.op(eng, lambda e: e.dma_start_transpose(dst, src),
             reads=tuple(skeys), writes=tuple(dkeys), queue=queue)


def h_mm(ctx, pairs, psum_ap, pkey, rkeys, start=True, stop=True):
    n = len(pairs)

    def fn(e):
        last = None
        for i, (l, r) in enumerate(pairs):
            last = e.matmul(psum_ap, l, r,
                            start=(start and i == 0), stop=(stop and i == n - 1))
        return last

    ctx.s.op("tensor", fn, reads=tuple(rkeys), writes=(pkey,))


def h_tr(ctx, out_psum, in_ap, pkey, rkeys, ident=None):
    ident = ident if ident is not None else ctx.t["ident_bf"][:, :]
    ctx.s.op("tensor", lambda e: e.transpose(out_psum, in_ap, ident),
             reads=tuple(rkeys), writes=(pkey,))


def h_act(ctx, out, in_, func, okeys, rkeys, bias=0.0, scale=1.0,
          accum_out=None):
    def fn(e):
        return e.activation(out, in_, func, bias=bias, scale=scale,
                            accum_out=accum_out)

    ctx.s.op("scalar", fn, reads=tuple(rkeys), writes=tuple(okeys))


def h_v(ctx, fn, okeys, rkeys, eng="vector"):
    ctx.s.op(eng, fn, reads=tuple(rkeys), writes=tuple(okeys))


def xcols(k, c0=0, n=T):
    return slice(k * T + c0, k * T + c0 + n)



def build_norm(ctx, xsrc, xkey, out_bf, outkey, kind="ln", eps=1e-5, CH=512,
               ndt=NDT):
    ones = ctx.t["ones_f32"]
    sq = ctx.t["wbuf_f32"]
    rows = ctx.t["rowpool"]
    p_st = ctx.psum["p_stat"]
    p_bc = ctx.psum["p_bc"]
    inv_n = 1.0 / (128 * ndt)
    for c0 in range(0, T, CH):
        for k in range(ndt):
            h_act(ctx, sq[:, k * CH:(k + 1) * CH], xsrc[:, xcols(k, c0, CH)],
                  AF.Square, [("wbuf_f32", (k * CH, (k + 1) * CH))], [(xkey, (k * T + c0, k * T + c0 + CH))])
        if kind == "ln":
            h_mm(ctx, [(ones[:, 0:1], xsrc[:, xcols(k, c0, CH)])
                       for k in range(ndt)],
                 p_st[0:1, 0:CH], ("p_stat", (0, 512)), [(xkey,)])
        h_mm(ctx, [(ones[:, 0:1], sq[:, k * CH:(k + 1) * CH]) for k in range(ndt)],
             p_st[0:1, CH:2 * CH], ("p_stat", (CH, 2 * CH)),
             [("wbuf_f32", (k * CH, (k + 1) * CH)) for k in range(ndt)])
        mrow = rows[0:1, 0:CH]
        vrow = rows[0:1, CH:2 * CH]
        rrow = rows[0:1, 2 * CH:3 * CH]
        nrow = rows[0:1, 3 * CH:4 * CH]
        if kind == "ln":
            h_v(ctx, lambda e: e.tensor_scalar_mul(mrow, p_st[0:1, 0:CH], inv_n),
                [("rows", (0, 512))], [("p_stat", (0, 512))])
            h_act(ctx, vrow, mrow, AF.Square, [("rows", (512, 1024))], [("rows", (0, 512))])
            h_v(ctx, lambda e: e.scalar_tensor_tensor(
                vrow, p_st[0:1, CH:2 * CH], inv_n, vrow, ALU.mult, ALU.subtract),
                [("rows", (512, 1024))], [("p_stat", (CH, 2 * CH)), ("rows", (512, 1024))])
        else:
            h_v(ctx, lambda e: e.tensor_scalar_mul(vrow, p_st[0:1, CH:2 * CH],
                                                   inv_n),
                [("rows", (512, 1024))], [("p_stat", (CH, 2 * CH))])
        h_act(ctx, rrow, vrow, AF.Sqrt, [("rows", (1024, 1536))], [("rows", (512, 1024))],
              bias=ctx.c.col(ctx, eps)[0:1, 0:1])
        h_v(ctx, lambda e: e.reciprocal(rrow, rrow), [("rows", (1024, 1536))], [("rows", (1024, 1536))])
        h_mm(ctx, [(ones[0:1, :], rrow)], p_bc[:, 0:CH], ("p_bc", (0, 256)),
             [("rows", (1024, 1536))])
        if kind == "ln":
            h_v(ctx, lambda e: e.tensor_mul(nrow, mrow, rrow), [("rows", (1536, 2048))],
                [("rows", (0, 512)), ("rows", (1024, 1536))])
            h_mm(ctx, [(ones[0:1, :], nrow)], p_bc[:, CH:2 * CH], ("p_bc", (256, 512)),
                 [("rows", (1536, 2048))])
            for k in range(ndt):
                def f1(e, k=k, c0=c0):
                    return e.tensor_tensor(sq[:, k * CH:(k + 1) * CH],
                                           xsrc[:, xcols(k, c0, CH)],
                                           p_bc[:, 0:CH], ALU.mult)

                h_v(ctx, f1, [("wbuf_f32", (k * CH, (k + 1) * CH))],
                    [(xkey, (k * T + c0, k * T + c0 + CH)), ("p_bc", (0, 256))])

                def f2(e, k=k, c0=c0):
                    return e.tensor_tensor(out_bf[:, xcols(k, c0, CH)],
                                           sq[:, k * CH:(k + 1) * CH],
                                           p_bc[:, CH:2 * CH], ALU.subtract)

                h_v(ctx, f2, [(outkey, (k * T + c0, k * T + c0 + CH))],
                    [("wbuf_f32", (k * CH, (k + 1) * CH)), ("p_bc", (256, 512))])
        else:
            for k in range(ndt):
                def f1(e, k=k, c0=c0):
                    return e.tensor_tensor(out_bf[:, xcols(k, c0, CH)],
                                           xsrc[:, xcols(k, c0, CH)],
                                           p_bc[:, 0:CH], ALU.mult)

                h_v(ctx, f1, [(outkey, (k * T + c0, k * T + c0 + CH))],
                    [(xkey, (k * T + c0, k * T + c0 + CH)), ("p_bc", (0, 256))])



def build_l0ffn_prep(ctx):
    w1t = ctx.w["L0_ffn_W1T"]
    w2t = ctx.w["L0_ffn_W2T"]
    w1qt = ctx.dscr("w1qt", (D, 4 * D), BF16).ap()
    w2qt = ctx.dscr("w2qt", (4 * D, D), BF16).ap()
    wl = ctx.t["wbuf_f32"]
    acc = ctx.t["scr_a"]
    rows = ctx.t["rowpool"]
    p_a = ctx.psum["p_stat"]
    p_t = ctx.psum["p_bc"]
    qt = ctx.t["bigb_bf"]
    for (wsrc, nrow_, ncol, qdst, tag) in (
            (w1t, D, 4 * D, w1qt, "q1"), (w2t, 4 * D, D, w2qt, "q2")):
        ntile = nrow_ // 128
        for i in range(ntile):
            h_dma(ctx, wl[:, 0:ncol], wsrc[128 * i:128 * (i + 1), :],
                  [("wbuf_f32",)], queue="qsync")

            def fred(e, i=i, ncol=ncol):
                return e.tensor_reduce(acc[:, i:i + 1], wl[:, 0:ncol], AX.X,
                                       ALU.add, apply_absolute_value=True)

            h_v(ctx, fred, [("scr_a", (i, i + 1))], [("wbuf_f32",)])

        def ftot(e, ntile=ntile):
            return e.tensor_reduce(acc[:, 64:65], acc[:, 0:ntile], AX.X, ALU.add)

        h_v(ctx, ftot, [("scr_a", (64, 65))],
            [("scr_a", (i, i + 1)) for i in range(ntile)])
        h_mm(ctx, [(ctx.t["ones_f32"][:, 0:1], acc[:, 64:65])], p_a[0:1, 0:1],
             ("p_stat", (0, 512)), [("scr_a", (64, 65))])

        def fthr(e, n=nrow_ * ncol):
            return e.tensor_scalar(rows[0:1, 0:1], p_a[0:1, 0:1], 0.5 / n,
                                   0.5e-5, ALU.mult, ALU.add)

        h_v(ctx, fthr, [("rows", (0, 1))], [("p_stat", (0, 512))])
        h_mm(ctx, [(ctx.t["ones_f32"][0:1, :], rows[0:1, 0:1])], p_t[:, 0:1],
             ("p_bc", (0, 512)), [("rows", (0, 1))])
        cols = ctx.t["cols_sb"]
        h_v(ctx, lambda e: e.tensor_copy(cols[:, 0:1], p_t[:, 0:1]),
            [("cols_sb", (0, 1))], [("p_bc", (0, 256))])
        h_v(ctx, lambda e: e.tensor_scalar_mul(cols[:, 1:2], cols[:, 0:1],
                                               -1.0),
            [("cols_sb", (1, 2))], [("cols_sb", (0, 1))])
        for i in range(ntile):
            h_dma(ctx, wl[:, 0:ncol], wsrc[128 * i:128 * (i + 1), :],
                  [("wbuf_f32",)], queue="qsync")

            def fle(e, ncol=ncol):
                return e.tensor_scalar(qt[:, 4096:4096 + ncol], wl[:, 0:ncol],
                                       cols[:, 1:2], None, ALU.is_le)

            h_v(ctx, fle, [("bigb_bf", (4096, 8192))],
                [("wbuf_f32",), ("cols_sb", (1, 2))])

            def fge(e, ncol=ncol):
                return e.tensor_scalar(wl[:, 0:ncol], wl[:, 0:ncol],
                                       cols[:, 0:1], None, ALU.is_ge)

            h_v(ctx, fge, [("wbuf_f32",)], [("wbuf_f32",), ("cols_sb", (0, 1))])

            def fsub(e, ncol=ncol):
                return e.tensor_tensor(qt[:, 0:ncol], wl[:, 0:ncol],
                                       qt[:, 4096:4096 + ncol], ALU.subtract)

            h_v(ctx, fsub, [("bigb_bf", (0, 4096))],
                [("wbuf_f32",), ("bigb_bf", (4096, 8192))])
            h_dma(ctx, qdst[128 * i:128 * (i + 1), :], qt[:, 0:ncol],
                  [(tag,)], [("bigb_bf", (0, 4096))], queue="qsync")


def build_ffn0(ctx):
    x = ctx.t["x_res"]
    xb = ctx.t["xln_bf"]
    build_norm(ctx, x, "x_res", xb, "xln_bf", kind="ln")
    w1qt = ctx.dram["w1qt"].ap()
    w2qt = ctx.dram["w2qt"].ap()
    b1 = ctx.t["b1_sb"]
    hbuf = ctx.t["bigb_bf"]
    wband = [ctx.t["wband0"], ctx.t["wband1"]]
    p_h = [ctx.psum["p_a"], ctx.psum["p_b"]]
    p_o = [ctx.psum["p_c"], ctx.psum["p_d"]]
    CH = 512
    for c0 in range(0, T, CH):
        for fc in range(8):
            wb = wband[fc % 2]
            wbk = ("wband0",) if fc % 2 == 0 else ("wband1",)
            h_dma(ctx, wb[:, :].rearrange("p (g f) -> p g f", g=NDT),
                  w1qt[:, 512 * fc:512 * (fc + 1)].rearrange(
                      "(g p) f -> p g f", p=128),
                  [wbk], [("q1",)], queue="qsync")
            for fs in range(4):
                fi = 4 * fc + fs
                pp = p_h[fs % 2]
                ppk = ("p_a",) if fs % 2 == 0 else ("p_b",)
                pairs = [(wb[:, 512 * g + 128 * fs:512 * g + 128 * (fs + 1)],
                          xb[:, xcols(g, c0, CH)]) for g in range(NDT)]
                h_mm(ctx, pairs, pp[:, 0:CH], ppk,
                     [wbk] + [("xln_bf", (g * T + c0, g * T + c0 + CH))
                              for g in range(NDT)])
                h_act(ctx, hbuf[:, CH * fi:CH * (fi + 1)], pp[:, 0:CH], AF.Relu,
                      [("bigb_bf", (CH * fi, CH * (fi + 1)))], [ppk, ("b1_sb",)],
                      bias=b1[:, fi:fi + 1])
        for dd in range(NDT):
            wb = wband[dd % 2]
            wbk = ("wband0",) if dd % 2 == 0 else ("wband1",)
            h_dma(ctx, wb[:, :].rearrange("p (g c) -> p g c", g=32),
                  w2qt[:, 128 * dd:128 * (dd + 1)].rearrange(
                      "(g p) c -> p g c", p=128),
                  [wbk], [("q2",)], queue="qsync")
            po = p_o[dd % 2]
            pok = ("p_c",) if dd % 2 == 0 else ("p_d",)
            pairs = [(wb[:, 128 * fi:128 * (fi + 1)],
                      hbuf[:, CH * fi:CH * (fi + 1)]) for fi in range(32)]
            h_mm(ctx, pairs, po[:, 0:CH], pok,
                 [wbk] + [("bigb_bf", (CH * fi, CH * (fi + 1))) for fi in range(32)])

            def facc(e, dd=dd, c0=c0, po=po):
                return e.scalar_tensor_tensor(
                    x[:, xcols(dd, c0, CH)], po[:, 0:CH], 0.5,
                    x[:, xcols(dd, c0, CH)], ALU.mult, ALU.add)

            xk = ("x_res", (dd * T + c0, dd * T + c0 + CH))
            h_v(ctx, facc, [xk], [pok, xk])
    b2h = ctx.t["b2h_sb"]
    for dd in range(NDT):
        def fb2(e, dd=dd):
            return e.tensor_scalar(x[:, xcols(dd)], x[:, xcols(dd)],
                                   b2h[:, dd:dd + 1], None, ALU.add)

        h_v(ctx, fb2, [("x_res", (dd * T, (dd + 1) * T))],
            [("x_res", (dd * T, (dd + 1) * T)), ("b2h_sb",)])




def load_band(ctx, dst_sb, dkey, w_ap, c0, c1, nrows=D, queue="qgp", skeys=()):
    g = nrows // 128
    h_dma(ctx, dst_sb.rearrange("p (g c) -> p g c", g=g),
          w_ap[:, c0:c1].rearrange("(g p) c -> p g c", p=128),
          [dkey], skeys, queue=queue)


def build_attn0(ctx):
    x = ctx.t["x_res"]
    xb = ctx.t["xln_bf"]
    build_norm(ctx, x, "x_res", xb, "xln_bf", kind="ln")
    H, DK = 16, 64
    bigb = ctx.t["bigb_bf"]
    qt = ctx.t["qtile_bf"]
    sm = ctx.t["smalltok"]
    wrow = ctx.t["wbuf_f32"]
    msb = ctx.t["msb_bf"]
    sel = ctx.t["sel16"]
    wb0, wb1 = ctx.t["wband0"], ctx.t["wband1"]
    p_a, p_b, p_c, p_d = (ctx.psum[n] for n in ("p_a", "p_b", "p_c", "p_d"))
    p_trb = ctx.psum["p_d"].bitcast(BF16)
    scr = ctx.t["scr_a"]
    cols = ctx.t["cols_sb"]
    onesf = ctx.t["ones_f32"]
    idf = ctx.t["ident_f32"]
    idb = ctx.t["ident_bf"]

    def KEY(name, c0, c1, p0=0, p1=128):
        return (name, (c0, c1), (p0, p1))

    if ATTN0_STOP <= 0:
        return
    h_dma(ctx, ctx.t["rowpool"][:, 0:512], ctx.w["btf1_t_in"][:, :],
          [("rows", (0, 512))])
    h_dma(ctx, ctx.t["rowpool"][:, 512:1024], ctx.w["wtf2_t_in"][:, :],
          [("rows", (512, 1024))])
    load_band(ctx, wb0[:, 0:384], ("wband0",), ctx.w["L0_attn_Wbfs"], 0, 48)
    if ATTN0_STOP <= 0.5:
        return
    for tt in range(16):
        c0 = tt * 128
        pairs = [(xb[:, xcols(g, c0, 128)], wb0[:, 48 * g:48 * (g + 1)])
                 for g in range(NDT)]
        h_mm(ctx, pairs, p_a[:, 0:48], ("p_a",),
             [("wband0",)] + [("xln_bf", (g * T + c0, g * T + c0 + 128))
                              for g in range(NDT)])
        h_act(ctx, sm[:, 16 * tt:16 * (tt + 1)], p_a[:, 0:16], AF.Sigmoid,
              [KEY("smalltok", 16 * tt, 16 * tt + 16)], [("p_a",)])
        for j, (bt, prow) in enumerate(
                ((ctx.t["bfd_t"], 0), (ctx.t["bsd_t"], 32))):
            def fadd(e, j=j, bt=bt):
                return e.tensor_tensor(scr[:, 16 * j:16 * (j + 1)],
                                       p_a[:, 16 * (j + 1):16 * (j + 2)],
                                       bt[:, :], ALU.add)
            h_v(ctx, fadd, [KEY("scr_a", 16 * j, 16 * j + 16)],
                [("p_a",), (("bfd_t",) if j == 0 else ("bsd_t",))])
            h_act(ctx, scr[:, 32 + 16 * j:48 + 16 * j],
                  scr[:, 16 * j:16 * (j + 1)], AF.Sigmoid,
                  [KEY("scr_a", 32 + 16 * j, 48 + 16 * j)],
                  [KEY("scr_a", 16 * j, 16 * j + 16)])
            def ftr(e, j=j):
                return e.transpose(p_b[0:16, 0:128],
                                   scr[:, 32 + 16 * j:48 + 16 * j], idf[:, :])
            ctx.s.op("tensor", ftr,
                     reads=(KEY("scr_a", 32 + 16 * j, 48 + 16 * j),
                            ("ident_f32",)),
                     writes=(("p_b",),))
            def fcp(e, tt=tt, prow=prow):
                return e.tensor_copy(wrow[prow:prow + 16, 128 * tt:128 * (tt + 1)],
                                     p_b[0:16, 0:128])
            h_v(ctx, fcp, [KEY("wbuf_f32", 128 * tt, 128 * (tt + 1), prow, prow + 16)],
                [("p_b",)])

    if ATTN0_STOP <= 1:
        return
    for j, prow in ((0, 0), (1, 32)):
        h_act(ctx, wrow[64:80, 0:2048], wrow[prow:prow + 16, 0:2048], AF.Ln,
              [KEY("wbuf_f32", 0, 2048, 64, 80)],
              [KEY("wbuf_f32", 0, 2048, prow, prow + 16), ("cpool",)],
              bias=ctx.c.col(ctx, 1e-6)[0:16, 0:1])
        def fscan(e):
            return e.tensor_tensor_scan(wrow[64:80, 0:2048], wrow[64:80, 0:2048],
                                        wrow[64:80, 0:2048], 0.0,
                                        ALU.add, ALU.bypass)
        h_v(ctx, fscan, [KEY("wbuf_f32", 0, 2048, 64, 80)],
            [KEY("wbuf_f32", 0, 2048, 64, 80)])
        def fsub(e):
            return e.tensor_scalar(wrow[64:80, 0:2048], wrow[64:80, 0:2048],
                                   wrow[64:80, 2047:2048], None, ALU.subtract)
        h_v(ctx, fsub, [KEY("wbuf_f32", 0, 2048, 64, 80)],
            [KEY("wbuf_f32", 0, 2048, 64, 80)])
        h_act(ctx, wrow[prow:prow + 16, 2048:4096], wrow[64:80, 0:2048], AF.Exp,
              [KEY("wbuf_f32", 2048, 4096, prow, prow + 16)],
              [KEY("wbuf_f32", 0, 2048, 64, 80)], scale=-1.0)
        def fred(e, prow=prow):
            return e.tensor_reduce(cols[0:16, 4:5],
                                   wrow[prow:prow + 16, 2048:4096], AX.X, ALU.add)
        h_v(ctx, fred, [KEY("cols_sb", 4, 5, 0, 16)],
            [KEY("wbuf_f32", 2048, 4096, prow, prow + 16)])
        def faddeps(e, j=j):
            return e.tensor_scalar(cols[0:16, 4:5], cols[0:16, 4:5],
                                   1e-6, None, ALU.add)
        h_v(ctx, faddeps, [KEY("cols_sb", 4, 5, 0, 16)],
            [KEY("cols_sb", 4, 5, 0, 16)])
        def frec(e, j=j):
            return e.reciprocal(cols[0:16, 2 + j:3 + j], cols[0:16, 4:5])
        h_v(ctx, frec, [KEY("cols_sb", 2 + j, 3 + j, 0, 16)],
            [KEY("cols_sb", 4, 5, 0, 16)])

    if ATTN0_STOP <= 2:
        return
    for j, prow, dst0 in ((0, 0, 256), (1, 32, 512)):
        for tt in range(16):
            def ftr(e, prow=prow, tt=tt):
                return e.transpose(p_b[0:128, 0:16],
                                   wrow[prow:prow + 16,
                                        2048 + 128 * tt:2048 + 128 * (tt + 1)],
                                   idf[prow:prow + 16, prow:prow + 16])
            ctx.s.op("tensor", ftr,
                     reads=(KEY("wbuf_f32", 2048 + 128 * tt, 2048 + 128 * (tt + 1),
                                prow, prow + 16), ("ident_f32",)),
                     writes=(("p_b",),))
            def fcp(e, dst0=dst0, tt=tt):
                return e.tensor_copy(sm[:, dst0 + 16 * tt:dst0 + 16 * (tt + 1)],
                                     p_b[0:128, 0:16])
            h_v(ctx, fcp, [KEY("smalltok", dst0 + 16 * tt, dst0 + 16 * tt + 16)],
                [("p_b",)])

    if ATTN0_STOP <= 3:
        return
    def proj_chunk_loop(wname, post):
        for c in range(2):
            wb = wb0 if c == 0 else wb1
            wbk = ("wband0",) if c == 0 else ("wband1",)
            load_band(ctx, wb[:, :], wbk, ctx.w[wname], 512 * c, 512 * (c + 1))
            for tt in range(16):
                c0 = tt * 128
                pp = p_a if tt % 2 == 0 else p_b
                pk = ("p_a",) if tt % 2 == 0 else ("p_b",)
                pairs = [(xb[:, xcols(g, c0, 128)],
                          wb[:, 512 * g:512 * (g + 1)]) for g in range(NDT)]
                h_mm(ctx, pairs, pp[:, 0:512], pk,
                     [wbk] + [("xln_bf", (g * T + c0, g * T + c0 + 128))
                              for g in range(NDT)])
                post(tt, c, pp, pk)

    def norm_scale(tt, c, pp, pk, extra_mul_slice, dst_ap, dstkey,
                   extra_key=None):
        h_act(ctx, scr[:, 0:512], pp[:, 0:512], AF.Square,
              [KEY("scr_a", 0, 512)], [pk])
        def fr(e):
            return e.tensor_reduce(scr[:, 512:520],
                                   scr[:, 0:512].rearrange("p (h d) -> p h d", d=64),
                                   AX.X, ALU.add)
        h_v(ctx, fr, [KEY("scr_a", 512, 520)], [KEY("scr_a", 0, 512)])
        h_v(ctx, lambda e: e.tensor_scalar_max(scr[:, 512:520], scr[:, 512:520],
                                               1e-24),
            [KEY("scr_a", 512, 520)], [KEY("scr_a", 512, 520)])
        h_act(ctx, scr[:, 520:528], scr[:, 512:520], AF.Sqrt,
              [KEY("scr_a", 520, 528)], [KEY("scr_a", 512, 520)])
        h_v(ctx, lambda e: e.reciprocal(scr[:, 512:520], scr[:, 520:528]),
            [KEY("scr_a", 512, 520)], [KEY("scr_a", 520, 528)])
        if extra_mul_slice is not None:
            def fm(e):
                return e.tensor_tensor(scr[:, 512:520], scr[:, 512:520],
                                       extra_mul_slice, ALU.mult)
            h_v(ctx, fm, [KEY("scr_a", 512, 520)],
                [KEY("scr_a", 512, 520), extra_key])
        def fs(e):
            a, b = bass.broadcast_tensor_aps(
                pp[:, 0:512].rearrange("p (h d) -> p h d", d=64),
                scr[:, 512:520].rearrange("p (h o) -> p h o", o=1))
            return e.tensor_tensor(dst_ap.rearrange("p (h d) -> p h d", d=64),
                                   a, b, ALU.mult)
        h_v(ctx, fs, [dstkey], [pk, KEY("scr_a", 512, 520)])

    def k_post(tt, c, pp, pk):
        norm_scale(tt, c, pp, pk, sm[:, 16 * tt + 8 * c:16 * tt + 8 * (c + 1)],
                   bigb[:, 1024 * tt + 512 * c:1024 * tt + 512 * (c + 1)],
                   KEY("bigb_bf", 1024 * tt + 512 * c, 1024 * tt + 512 * (c + 1)),
                   extra_key=KEY("smalltok", 16 * tt + 8 * c,
                                 16 * tt + 8 * (c + 1)))

    proj_chunk_loop("L0_attn_Wk", k_post)

    if ATTN0_STOP <= 4:
        return
    h_dma(ctx, ctx.t["wtf1_bf"][0:64, 0:32], ctx.w["L0_attn_Wtf1"][:, :],
          [("wtf1_bf",)], queue="qgp")
    for tt in range(16):
        for h in range(H):
            def ftr(e, tt=tt, h=h):
                return e.transpose(p_trb[0:64, 0:128],
                                   bigb[:, 1024 * tt + 64 * h:1024 * tt + 64 * (h + 1)],
                                   idb[:, :])
            ctx.s.op("tensor", ftr,
                     reads=(KEY("bigb_bf", 1024 * tt + 64 * h,
                                1024 * tt + 64 * (h + 1)), ("ident_bf",)),
                     writes=(("p_d",),))
            def fcp(e):
                return e.tensor_copy(qt[0:64, 2048:2176], p_trb[0:64, 0:128])
            h_v(ctx, fcp, [KEY("qtile_bf", 2048, 2176, 0, 64)], [("p_d",)])
            h_mm(ctx, [(qt[0:64, 2048:2176], ctx.t["wtf1_bf"][0:64, 0:32])],
                 p_c[:, 32 * h:32 * (h + 1)], ("p_c", (32 * h, 32 * h + 32)),
                 [KEY("qtile_bf", 2048, 2176, 0, 64), ("wtf1_bf",)])
        def fa(e):
            return e.tensor_tensor(scr[:, 0:512], p_c[:, 0:512],
                                   ctx.t["rowpool"][:, 0:512], ALU.add)
        h_v(ctx, fa, [KEY("scr_a", 0, 512)],
            [("p_c",), ("rows", (0, 512))])
        h_act(ctx, scr[:, 512:1024], scr[:, 0:512], AF.Silu,
              [KEY("scr_a", 512, 1024)], [KEY("scr_a", 0, 512)])
        def fm2(e):
            return e.tensor_tensor(scr[:, 512:1024], scr[:, 512:1024],
                                   ctx.t["rowpool"][:, 512:1024], ALU.mult)
        h_v(ctx, fm2, [KEY("scr_a", 512, 1024)],
            [KEY("scr_a", 512, 1024), ("rows", (512, 1024))])
        def fr2(e):
            return e.tensor_reduce(scr[:, 1024:1040],
                                   scr[:, 512:1024].rearrange(
                                       "p (h j) -> p h j", j=32), AX.X, ALU.add)
        h_v(ctx, fr2, [KEY("scr_a", 1024, 1040)], [KEY("scr_a", 512, 1024)])
        h_act(ctx, scr[:, 1040:1056], scr[:, 1024:1040], AF.Sigmoid,
              [KEY("scr_a", 1040, 1056)], [KEY("scr_a", 1024, 1040), ("btf2_t",)],
              bias=ctx.t["btf2_t"][:, 0:1])
        h_v(ctx, lambda e: e.tensor_scalar_min(scr[:, 1040:1056],
                                               scr[:, 1040:1056], 0.99),
            [KEY("scr_a", 1040, 1056)], [KEY("scr_a", 1040, 1056)])
        h_v(ctx, lambda e: e.tensor_scalar_max(scr[:, 1040:1056],
                                               scr[:, 1040:1056], 0.01),
            [KEY("scr_a", 1040, 1056)], [KEY("scr_a", 1040, 1056)])
        for j, coff, scl in ((0, 2048, 0.3), (1, 0, -0.3)):
            def fal(e, scl=scl, j=j):
                return e.tensor_scalar(scr[:, 1056 + 16 * j:1072 + 16 * j],
                                       scr[:, 1040:1056], scl, 0.5,
                                       ALU.mult, ALU.add)
            h_v(ctx, fal, [KEY("scr_a", 1056 + 16 * j, 1072 + 16 * j)],
                [KEY("scr_a", 1040, 1056)])
            def ftr2(e, j=j):
                return e.transpose(p_b[0:16, 0:128],
                                   scr[:, 1056 + 16 * j:1072 + 16 * j], idf[:, :])
            ctx.s.op("tensor", ftr2,
                     reads=(KEY("scr_a", 1056 + 16 * j, 1072 + 16 * j),
                            ("ident_f32",)),
                     writes=(("p_b",),))
            def fcp2(e, coff=coff, tt=tt):
                return e.tensor_copy(wrow[0:16,
                                          coff + 128 * tt:coff + 128 * (tt + 1)],
                                     p_b[0:16, 0:128])
            h_v(ctx, fcp2,
                [KEY("wbuf_f32", coff + 128 * tt, coff + 128 * (tt + 1), 0, 16)],
                [("p_b",)])

    if ATTN0_STOP <= 5:
        return
    for c in range(2):
        wb = wb0 if c == 0 else wb1
        wbk = ("wband0",) if c == 0 else ("wband1",)
        load_band(ctx, wb[:, :], wbk, ctx.w["L0_attn_Wv"], 512 * c, 512 * (c + 1))
        for tt in range(16):
            c0 = tt * 128
            pp = p_a if tt % 2 == 0 else p_b
            pk = ("p_a",) if tt % 2 == 0 else ("p_b",)
            pairs = [(xb[:, xcols(g, c0, 128)],
                      wb[:, 512 * g:512 * (g + 1)]) for g in range(NDT)]
            h_mm(ctx, pairs, pp[:, 0:512], pk,
                 [wbk] + [("xln_bf", (g * T + c0, g * T + c0 + 128))
                          for g in range(NDT)])
            def fvs(e, tt=tt, c=c, pp=pp):
                a, b = bass.broadcast_tensor_aps(
                    pp[:, 0:512].rearrange("p (h d) -> p h d", d=64),
                    sm[:, 16 * tt + 8 * c:16 * tt + 8 * (c + 1)].rearrange(
                        "p (h o) -> p h o", o=1))
                return e.tensor_tensor(
                    qt[:, 512:1024].rearrange("p (h d) -> p h d", d=64),
                    a, b, ALU.mult)
            h_v(ctx, fvs, [KEY("qtile_bf", 512, 1024)],
                [pk, KEY("smalltok", 16 * tt + 8 * c, 16 * tt + 8 * (c + 1))])
            for j, src0, dst0 in ((0, 256, 1024), (1, 512, 1536)):
                def fcv(e, tt=tt, c=c, j=j, src0=src0):
                    return e.tensor_copy(
                        qt[:, 2176 + 8 * j:2184 + 8 * j],
                        sm[:, src0 + 16 * tt + 8 * c:src0 + 16 * tt + 8 * (c + 1)])
                h_v(ctx, fcv, [KEY("qtile_bf", 2176 + 8 * j, 2184 + 8 * j)],
                    [KEY("smalltok", src0 + 16 * tt + 8 * c,
                         src0 + 16 * tt + 8 * (c + 1))])
                def fkf(e, tt=tt, c=c, j=j, dst0=dst0):
                    a, b = bass.broadcast_tensor_aps(
                        bigb[:, 1024 * tt + 512 * c:1024 * tt + 512 * (c + 1)]
                        .rearrange("p (h d) -> p h d", d=64),
                        qt[:, 2176 + 8 * j:2184 + 8 * j].rearrange(
                            "p (h o) -> p h o", o=1))
                    return e.tensor_tensor(
                        qt[:, dst0:dst0 + 512].rearrange("p (h d) -> p h d", d=64),
                        a, b, ALU.mult)
                h_v(ctx, fkf, [KEY("qtile_bf", dst0, dst0 + 512)],
                    [KEY("bigb_bf", 1024 * tt + 512 * c,
                         1024 * tt + 512 * (c + 1)),
                     KEY("qtile_bf", 2176 + 8 * j, 2184 + 8 * j)])
            for j in range(4):
                for typ, src0, pm, pmk in ((0, 1024, p_c, ("p_c",)),
                                           (1, 1536, p_d, ("p_d",))):
                    def fmm(e, j=j, src0=src0, pm=pm, tt=tt):
                        return e.matmul(pm[:, 128 * j:128 * (j + 1)],
                                        qt[:, src0 + 128 * j:src0 + 128 * (j + 1)],
                                        qt[:, 512 + 128 * j:512 + 128 * (j + 1)],
                                        start=(tt == 0), stop=(tt == 15),
                                        skip_group_check=True)
                    ctx.s.op("tensor", fmm,
                             reads=(KEY("qtile_bf", src0 + 128 * j,
                                        src0 + 128 * (j + 1)),
                                    KEY("qtile_bf", 512 + 128 * j,
                                        512 + 128 * (j + 1))),
                             writes=(pmk + ((128 * j, 128 * (j + 1)),),))
        for hh in range(8):
            h = 8 * c + hh
            j, lane = hh // 2, hh % 2
            for typ, pm, pmk, d0 in ((0, p_c, "p_c", 0), (1, p_d, "p_d", 512)):
                def fsel(e, h=h, typ=typ):
                    return e.matmul(ctx.psum["p_bc"][0:64, 0:1],
                                    sel[:, 128 * h:128 * h + 64],
                                    cols[0:16, 2 + typ:3 + typ],
                                    start=True, stop=True)
                ctx.s.op("tensor", fsel,
                         reads=(("sel16",), KEY("cols_sb", 2 + typ, 3 + typ, 0, 16)),
                         writes=(("p_bc", (0, 1)),))
                h_v(ctx, lambda e: e.tensor_copy(cols[0:64, 5:6],
                                                 ctx.psum["p_bc"][0:64, 0:1]),
                    [KEY("cols_sb", 5, 6, 0, 64)], [("p_bc", (0, 1))])
                def fev(e, j=j, lane=lane, h=h, pm=pm, d0=d0):
                    return e.tensor_scalar(
                        msb[64 * lane:64 * (lane + 1),
                            d0 + 64 * (h // 2):d0 + 64 * (h // 2) + 64],
                        pm[64 * lane:64 * (lane + 1), 128 * j + 64 * lane:
                           128 * j + 64 * lane + 64],
                        cols[0:64, 5:6], None, ALU.mult)
                h_v(ctx, fev,
                    [KEY("msb_bf", d0 + 64 * (h // 2), d0 + 64 * (h // 2) + 64,
                         64 * lane, 64 * (lane + 1))],
                    [(pmk, (128 * j, 128 * (j + 1))),
                     KEY("cols_sb", 5, 6, 0, 64)])

    if ATTN0_STOP <= 6:
        return
    def q_post(tt, c, pp, pk):
        norm_scale(tt, c, pp, pk, None,
                   qt[:, 0:512], KEY("qtile_bf", 0, 512))
        for hh in range(8):
            h = 8 * c + hh
            pr, lane = h // 2, h % 2
            def ftr(e, hh=hh):
                return e.transpose(p_trb[0:64, 0:128],
                                   qt[:, 64 * hh:64 * (hh + 1)], idb[:, :])
            ctx.s.op("tensor", ftr,
                     reads=(KEY("qtile_bf", 64 * hh, 64 * (hh + 1)),
                            ("ident_bf",)),
                     writes=(("p_d",),))
            def fcp(e, pr=pr, lane=lane, tt=tt):
                return e.tensor_copy(
                    bigb[64 * lane:64 * (lane + 1),
                         2048 * pr + 128 * tt:2048 * pr + 128 * (tt + 1)],
                    p_trb[0:64, 0:128])
            h_v(ctx, fcp,
                [KEY("bigb_bf", 2048 * pr + 128 * tt, 2048 * pr + 128 * (tt + 1),
                     64 * lane, 64 * (lane + 1))],
                [("p_d",)])

    proj_chunk_loop("L0_attn_Wq", q_post)

    if ATTN0_STOP <= 7:
        return
    for h in range(H):
        pr, lane = h // 2, h % 2
        for cc in range(4):
            qslice = bigb[64 * lane:64 * (lane + 1),
                          2048 * pr + 512 * cc:2048 * pr + 512 * (cc + 1)]
            qkey = KEY("bigb_bf", 2048 * pr + 512 * cc, 2048 * pr + 512 * (cc + 1),
                       64 * lane, 64 * (lane + 1))
            for typ, d0, pp, pk in ((0, 0, p_a, "p_a"), (1, 512, p_b, "p_b")):
                def fo(e, h=h, lane=lane, d0=d0, pp=pp, qslice=qslice):
                    return e.matmul(pp[0:64, 0:512],
                                    msb[64 * lane:64 * (lane + 1),
                                        d0 + 64 * (h // 2):d0 + 64 * (h // 2) + 64],
                                    qslice, start=True, stop=True)
                ctx.s.op("tensor", fo,
                         reads=(KEY("msb_bf", d0 + 64 * (h // 2),
                                    d0 + 64 * (h // 2) + 64,
                                    64 * lane, 64 * (lane + 1)), qkey),
                         writes=((pk, (0, 512)),))
            for typ, coff, pp, pk in ((0, 2048, p_c, "p_c"), (1, 0, p_d, "p_d")):
                def fab(e, h=h, coff=coff, pp=pp, cc=cc):
                    return e.matmul(pp[0:64, 0:512], sel[:, 128 * h:128 * h + 64],
                                    wrow[0:16, coff + 512 * cc:coff + 512 * (cc + 1)],
                                    start=True, stop=True)
                ctx.s.op("tensor", fab,
                         reads=(("sel16",),
                                KEY("wbuf_f32", coff + 512 * cc, coff + 512 * (cc + 1),
                                    0, 16)),
                         writes=((pk, (0, 512)),))
            h_v(ctx, lambda e: e.tensor_copy(scr[0:64, 0:512], p_a[0:64, 0:512]),
                [KEY("scr_a", 0, 512, 0, 64)], [("p_a", (0, 512))])
            h_v(ctx, lambda e: e.tensor_tensor(scr[0:64, 512:1024],
                                               scr[0:64, 0:512],
                                               p_c[0:64, 0:512], ALU.mult),
                [KEY("scr_a", 512, 1024, 0, 64)],
                [KEY("scr_a", 0, 512, 0, 64), ("p_c", (0, 512))])
            h_v(ctx, lambda e: e.tensor_copy(scr[0:64, 0:512], p_b[0:64, 0:512]),
                [KEY("scr_a", 0, 512, 0, 64)], [("p_b", (0, 512))])
            h_v(ctx, lambda e: e.tensor_tensor(scr[0:64, 1024:1536],
                                               scr[0:64, 0:512],
                                               p_d[0:64, 0:512], ALU.mult),
                [KEY("scr_a", 1024, 1536, 0, 64)],
                [KEY("scr_a", 0, 512, 0, 64), ("p_d", (0, 512))])
            def fsum(e, h=h, cc=cc, pr=pr, lane=lane):
                return e.tensor_add(
                    xb[64 * lane:64 * (lane + 1),
                       pr * T + 512 * cc:pr * T + 512 * (cc + 1)],
                    scr[0:64, 512:1024], scr[0:64, 1024:1536])
            h_v(ctx, fsum,
                [("xln_bf", (pr * T + 512 * cc, pr * T + 512 * (cc + 1)),
                  (64 * lane, 64 * (lane + 1)))],
                [KEY("scr_a", 512, 1024, 0, 64), KEY("scr_a", 1024, 1536, 0, 64)])

    if ATTN0_STOP <= 8:
        return
    for dd in range(NDT):
        wb = wb0 if dd % 2 == 0 else wb1
        wbk = ("wband0",) if dd % 2 == 0 else ("wband1",)
        load_band(ctx, wb[:, 0:1024], wbk, ctx.w["L0_attn_Wo"],
                  128 * dd, 128 * (dd + 1))
        for cc in range(4):
            pp = p_c if cc % 2 == 0 else p_d
            pk = ("p_c",) if cc % 2 == 0 else ("p_d",)
            pairs = [(wb[:, 128 * g:128 * (g + 1)],
                      xb[:, xcols(g, 512 * cc, 512)]) for g in range(NDT)]
            h_mm(ctx, pairs, pp[:, 0:512], pk,
                 [wbk] + [("xln_bf", (g * T + 512 * cc, g * T + 512 * (cc + 1)))
                          for g in range(NDT)])
            def facc(e, dd=dd, cc=cc, pp=pp):
                return e.tensor_add(x[:, xcols(dd, 512 * cc, 512)],
                                    pp[:, 0:512], x[:, xcols(dd, 512 * cc, 512)])
            xk = ("x_res", (dd * T + 512 * cc, dd * T + 512 * (cc + 1)))
            h_v(ctx, facc, [xk], [pk, xk])




def build_attn2(ctx):
    x = ctx.t["x_res"]
    xb = ctx.t["xln_bf"]
    build_norm(ctx, x, "x_res", xb, "xln_bf", kind="ln")
    bigb = ctx.t["bigb_bf"]
    pa_, pb_ = ctx.t["pool_bf_a"], ctx.t["pool_bf_b"]
    qt = ctx.t["qtile_bf"]
    msb = ctx.t["msb_bf"]
    cmask = ctx.t["cmask_bf"]
    rows = ctx.t["rowpool"]
    scr = ctx.t["scr_a"]
    onesf = ctx.t["ones_f32"]
    onesb = ctx.t["ones_bf"]
    wb0, wb1 = ctx.t["wband0"], ctx.t["wband1"]
    p_a, p_b, p_c, p_d = (ctx.psum[n] for n in ("p_a", "p_b", "p_c", "p_d"))
    p_st = ctx.psum["p_stat"]
    p_bc = ctx.psum["p_bc"]

    def KEY(name, c0, c1, p0=0, p1=128):
        return (name, (c0, c1), (p0, p1))

    LT0 = 14336
    for gi, anm in enumerate(("L2_attn_aqT", "L2_attn_akT", "L2_attn_avT")):
        rb = 32 * gi
        load_band(ctx, wb0[:, 0:128], ("wband0",), ctx.w[anm], 0, 16)
        for cc in range(4):
            h_mm(ctx,
                 [(wb0[:, 16 * g:16 * (g + 1)], xb[:, xcols(g, 512 * cc, 512)])
                  for g in range(NDT)],
                 p_a[rb:rb + 16, 0:512], ("p_a",),
                 [("wband0",)] + [("xln_bf", (g * T + 512 * cc,
                                              g * T + 512 * (cc + 1)))
                                  for g in range(NDT)])
            def fev(e, rb=rb, cc=cc):
                return e.mul(bigb[rb:rb + 16, LT0 + 512 * cc:
                                  LT0 + 512 * (cc + 1)],
                             p_a[rb:rb + 16, 0:512], 1.0 / 16.0)
            ctx.s.op("scalar", fev,
                     reads=(("p_a",),),
                     writes=(KEY("bigb_bf", LT0 + 512 * cc,
                                 LT0 + 512 * (cc + 1), rb, rb + 16),))

    for pr in range(8):
        wq = wb0 if pr % 2 == 0 else wb1
        wqk = ("wband0",) if pr % 2 == 0 else ("wband1",)
        load_band(ctx, wq[:, 0:1024], wqk, ctx.w["L2_attn_wqT"],
                  128 * pr, 128 * (pr + 1))
        load_band(ctx, wq[:, 1024:2048], wqk, ctx.w["L2_attn_wkT"],
                  128 * pr, 128 * (pr + 1))
        load_band(ctx, wq[:, 2048:3072], wqk, ctx.w["L2_attn_wvT"],
                  128 * pr, 128 * (pr + 1))
        for gi in range(3):
            def mkdma(gi=gi, pr=pr):
                h_dma(ctx, msb[32 * gi:32 * gi + 16, 0:128],
                      ctx.w["L2_attn_b3"][:, gi, 128 * pr:128 * (pr + 1)],
                      [KEY("msb_bf", 0, 128, 32 * gi, 32 * gi + 16)],
                      queue="qgp")
            mkdma()
        for dst, dkey, woff, rb in (
                (pa_, "pool_bf_a", 0, 0),
                (pb_, "pool_bf_b", 1024, 32)):
            for cc in range(4):
                pairs = [(wq[:, woff + 128 * g:woff + 128 * (g + 1)],
                          xb[:, xcols(g, 512 * cc, 512)]) for g in range(NDT)]

                def fl(e, rb=rb, cc=cc):
                    return e.matmul(p_c[:, 0:512],
                                    msb[rb:rb + 16, 0:128],
                                    bigb[rb:rb + 16,
                                         LT0 + 512 * cc:LT0 + 512 * (cc + 1)],
                                    start=False, stop=True)
                h_mm(ctx, pairs, p_c[:, 0:512], ("p_c",),
                     [wqk] + [("xln_bf", (g * T + 512 * cc,
                                          g * T + 512 * (cc + 1)))
                              for g in range(NDT)], stop=False)
                ctx.s.op("tensor", fl,
                         reads=(KEY("msb_bf", 0, 128, rb, rb + 16),
                                KEY("bigb_bf", LT0 + 512 * cc,
                                    LT0 + 512 * (cc + 1), rb, rb + 16)),
                         writes=(("p_c",),))
                def fev(e, dst=dst, cc=cc):
                    return e.tensor_copy(dst[:, 512 * cc:512 * (cc + 1)],
                                         p_c[:, 0:512])
                h_v(ctx, fev, [(dkey, (512 * cc, 512 * (cc + 1)))], [("p_c",)])
        for tt in range(16):
            pairs = [(xb[:, xcols(g, 128 * tt, 128)],
                      wq[:, 2048 + 128 * g:2048 + 128 * (g + 1)])
                     for g in range(NDT)]
            def flv(e, tt=tt):
                return e.matmul(p_c[:, 0:128],
                                bigb[64:80, LT0 + 128 * tt:
                                     LT0 + 128 * (tt + 1)],
                                msb[64:80, 0:128], start=False, stop=True)
            h_mm(ctx, pairs, p_c[:, 0:128], ("p_c",),
                 [wqk] + [("xln_bf", (g * T + 128 * tt, g * T + 128 * (tt + 1)))
                          for g in range(NDT)], stop=False)
            ctx.s.op("tensor", flv,
                     reads=(KEY("msb_bf", 0, 128, 64, 80),
                            KEY("bigb_bf", LT0 + 128 * tt,
                                LT0 + 128 * (tt + 1), 64, 80)),
                     writes=(("p_c",),))
            def fevv(e, tt=tt):
                return e.tensor_copy(qt[:, 128 * tt:128 * (tt + 1)],
                                     p_c[:, 0:128])
            h_v(ctx, fevv, [("qtile_bf", (128 * tt, 128 * (tt + 1)))],
                [("p_c",)])
        for lane in (0, 64):
            hb = lane
            for qc in range(4):
                nkt = 4 * qc + 4
                for kt in range(nkt):
                    pp = p_a if kt % 2 == 0 else p_b
                    pk = ("p_a",) if kt % 2 == 0 else ("p_b",)
                    def fs(e, lane=lane, qc=qc, kt=kt, pp=pp):
                        return e.matmul(
                            pp[:, 0:512],
                            pb_[lane:lane + 64, 128 * kt:128 * (kt + 1)],
                            pa_[lane:lane + 64, 512 * qc:512 * (qc + 1)],
                            start=True, stop=True)
                    ctx.s.op("tensor", fs,
                             reads=(("pool_bf_b", (128 * kt, 128 * (kt + 1)),
                                     (lane, lane + 64)),
                                    ("pool_bf_a", (512 * qc, 512 * (qc + 1)),
                                     (lane, lane + 64))),
                             writes=(pk,))
                    def fe(e, pp=pp):
                        return e.activation(ctx.t["ptile_bf"][:, 0:512],
                                            pp[:, 0:512], AF.Exp, scale=0.125)
                    ctx.s.op("scalar", fe, reads=(pk,),
                             writes=(("ptile_bf",),))
                    m = kt - 4 * qc
                    if m >= 0:
                        def fmask(e, m=m):
                            return e.tensor_tensor(
                                ctx.t["ptile_bf"][:, 0:512],
                                ctx.t["ptile_bf"][:, 0:512],
                                cmask[:, 384 - 128 * m:896 - 128 * m], ALU.mult)
                        h_v(ctx, fmask, [("ptile_bf",)],
                            [("ptile_bf",), ("cmask_bf",)])
                    def fnum(e, lane=lane, kt=kt, qc=qc):
                        return e.matmul(p_c[lane:lane + 64, 0:512],
                                        qt[:, 128 * kt + lane:128 * kt + lane + 64],
                                        ctx.t["ptile_bf"][:, 0:512],
                                        start=(kt == 0), stop=(kt == nkt - 1))
                    ctx.s.op("tensor", fnum,
                             reads=(("qtile_bf", (128 * kt + lane,
                                                  128 * kt + lane + 64)),
                                    ("ptile_bf",)),
                             writes=(("p_c", (0, 512), (lane, lane + 64)),))
                    dpo = 0 if lane == 0 else 512
                    def fden(e, kt=kt, dpo=dpo):
                        return e.matmul(p_st[0:1, dpo:dpo + 512],
                                        onesb[:, 0:1],
                                        ctx.t["ptile_bf"][:, 0:512],
                                        start=(kt == 0), stop=(kt == nkt - 1))
                    ctx.s.op("tensor", fden,
                             reads=(("ones_bf",), ("ptile_bf",)),
                             writes=(("p_stat", (dpo, dpo + 512)),))
                dpo = 0 if lane == 0 else 512
                def frc(e, dpo=dpo):
                    return e.reciprocal(rows[0:1, 0:512], p_st[0:1, dpo:dpo + 512])
                h_v(ctx, frc, [("rows", (0, 512), (0, 1))],
                    [("p_stat", (dpo, dpo + 512))])
                h_mm(ctx, [(onesf[0:1, 0:64], rows[0:1, 0:512])],
                     p_bc[0:64, 0:512], ("p_bc",), [("rows", (0, 512), (0, 1))])
                def fcp(e, lane=lane):
                    return e.tensor_copy(scr[0:64, 0:512], p_c[lane:lane + 64, 0:512])
                h_v(ctx, fcp, [KEY("scr_a", 0, 512, 0, 64)],
                    [("p_c", (0, 512), (lane, lane + 64))])
                def fo(e, pr=pr, lane=lane, qc=qc):
                    return e.tensor_tensor(
                        bigb[lane:lane + 64,
                             2048 * pr + 512 * qc:2048 * pr + 512 * (qc + 1)],
                        scr[0:64, 0:512], p_bc[0:64, 0:512], ALU.mult)
                h_v(ctx, fo,
                    [KEY("bigb_bf", 2048 * pr + 512 * qc,
                         2048 * pr + 512 * (qc + 1), lane, lane + 64)],
                    [KEY("scr_a", 0, 512, 0, 64), ("p_bc",)])

    if ATTN2_BARRIER:
        ctx.s.barrier_all()
    for cc in range(4):
        load_band(ctx, wb0[:, 0:128], ("wband0",), ctx.w["L2_attn_aoT"], 0, 16)
        pairs = [(wb0[:, 16 * g:16 * (g + 1)],
                  bigb[:, g * T + 512 * cc:g * T + 512 * (cc + 1)])
                 for g in range(NDT)]
        h_mm(ctx, pairs, p_a[0:16, 0:512], ("p_a",),
             [("wband0",)] + [("bigb_bf", (g * T + 512 * cc,
                                           g * T + 512 * (cc + 1)))
                              for g in range(NDT)])
        def fevo(e, cc=cc):
            return e.mul(msb[0:16, 512:1024], p_a[0:16, 0:512], 1.0 / 16.0)
        ctx.s.op("scalar", fevo, reads=(("p_a",),),
                 writes=(KEY("msb_bf", 512, 1024, 0, 16),))
        for dd in range(NDT):
            wb = wb1
            load_band(ctx, wb[:, 0:1024], ("wband1",), ctx.w["L2_attn_woT"],
                      128 * dd, 128 * (dd + 1))
            h_dma(ctx, msb[0:16, 384:512],
                  ctx.w["L2_attn_boT"][:, 128 * dd:128 * (dd + 1)],
                  [KEY("msb_bf", 384, 512, 0, 16)], queue="qgp")
            pairs = [(wb[:, 128 * g:128 * (g + 1)],
                      bigb[:, g * T + 512 * cc:g * T + 512 * (cc + 1)])
                     for g in range(NDT)]
            h_mm(ctx, pairs, p_d[:, 0:512], ("p_d",),
                 [("wband1",)] + [("bigb_bf", (g * T + 512 * cc,
                                               g * T + 512 * (cc + 1)))
                                  for g in range(NDT)], stop=False)
            def flo(e, cc=cc):
                return e.matmul(p_d[:, 0:512], msb[0:16, 384:512],
                                msb[0:16, 512:1024], start=False, stop=True)
            ctx.s.op("tensor", flo,
                     reads=(KEY("msb_bf", 384, 512, 0, 16),
                            KEY("msb_bf", 512, 1024, 0, 16)),
                     writes=(("p_d",),))
            def facc(e, dd=dd, cc=cc):
                return e.tensor_add(x[:, xcols(dd, 512 * cc, 512)],
                                    p_d[:, 0:512], x[:, xcols(dd, 512 * cc, 512)])
            xk = ("x_res", (dd * T + 512 * cc, dd * T + 512 * (cc + 1)))
            h_v(ctx, facc, [xk], [("p_d",), xk])




def build_ffn2(ctx):
    x = ctx.t["x_res"]
    xb = ctx.t["xln_bf"]
    build_norm(ctx, x, "x_res", xb, "xln_bf", kind="ln")
    bigb = ctx.t["bigb_bf"]
    sm = ctx.t["smalltok"]
    scr = ctx.t["scr_a"]
    selb = ctx.t["sel16b"]
    wrows = ctx.t["pool_bf_a"]
    wb0, wb1 = ctx.t["wband0"], ctx.t["wband1"]
    idf = ctx.t["ident_f32"]
    p_a, p_b, p_c, p_d = (ctx.psum[n] for n in ("p_a", "p_b", "p_c", "p_d"))
    p_bc = ctx.psum["p_bc"]

    def KEY(name, c0, c1, p0=0, p1=128):
        return (name, (c0, c1), (p0, p1))

    load_band(ctx, wb0[:, 0:128], ("wband0",), ctx.w["L2_ffn_selT"], 0, 16)
    for tt in range(16):
        c0 = 128 * tt
        h_mm(ctx, [(xb[:, xcols(g, c0, 128)], wb0[:, 16 * g:16 * (g + 1)])
                   for g in range(NDT)],
             p_a[:, 0:16], ("p_a",),
             [("wband0",)] + [("xln_bf", (g * T + c0, g * T + c0 + 128))
                              for g in range(NDT)])
        h_act(ctx, sm[:, 16 * tt:16 * (tt + 1)], p_a[:, 0:16], AF.Sigmoid,
              [KEY("smalltok", 16 * tt, 16 * tt + 16)], [("p_a",)])
        P0, R0, M0 = 256, 272, 288
        def fcpy(e, tt=tt):
            return e.tensor_copy(scr[:, P0:P0 + 16], sm[:, 16 * tt:16 * (tt + 1)])
        h_v(ctx, fcpy, [KEY("scr_a", P0, P0 + 16)],
            [KEY("smalltok", 16 * tt, 16 * tt + 16)])
        for rnd in range(4):
            def fmx(e):
                return e.tensor_reduce(scr[:, R0:R0 + 1], scr[:, P0:P0 + 16],
                                       AX.X, ALU.max)
            h_v(ctx, fmx, [KEY("scr_a", R0, R0 + 1)], [KEY("scr_a", P0, P0 + 16)])
            if rnd < 3:
                def fmsk(e):
                    return e.tensor_scalar(scr[:, M0:M0 + 16], scr[:, P0:P0 + 16],
                                           scr[:, R0:R0 + 1], None, ALU.is_ge)
                h_v(ctx, fmsk, [KEY("scr_a", M0, M0 + 16)],
                    [KEY("scr_a", P0, P0 + 16), KEY("scr_a", R0, R0 + 1)])
                def fsub(e):
                    return e.scalar_tensor_tensor(scr[:, P0:P0 + 16],
                                                  scr[:, M0:M0 + 16], -1e30,
                                                  scr[:, P0:P0 + 16],
                                                  ALU.mult, ALU.add)
                h_v(ctx, fsub, [KEY("scr_a", P0, P0 + 16)],
                    [KEY("scr_a", M0, M0 + 16), KEY("scr_a", P0, P0 + 16)])
        def fm4(e, tt=tt):
            return e.tensor_scalar(scr[:, M0:M0 + 16], sm[:, 16 * tt:16 * (tt + 1)],
                                   scr[:, R0:R0 + 1], None, ALU.is_ge)
        h_v(ctx, fm4, [KEY("scr_a", M0, M0 + 16)],
            [KEY("smalltok", 16 * tt, 16 * tt + 16), KEY("scr_a", R0, R0 + 1)])
        def fwr(e, tt=tt):
            return e.tensor_tensor(scr[:, P0:P0 + 16], sm[:, 16 * tt:16 * (tt + 1)],
                                   scr[:, M0:M0 + 16], ALU.mult)
        h_v(ctx, fwr, [KEY("scr_a", P0, P0 + 16)],
            [KEY("smalltok", 16 * tt, 16 * tt + 16), KEY("scr_a", M0, M0 + 16)])
        def fsm(e):
            return e.tensor_reduce(scr[:, R0:R0 + 1], scr[:, P0:P0 + 16],
                                   AX.X, ALU.add)
        h_v(ctx, fsm, [KEY("scr_a", R0, R0 + 1)], [KEY("scr_a", P0, P0 + 16)])
        h_v(ctx, lambda e: e.tensor_scalar_max(scr[:, R0:R0 + 1],
                                               scr[:, R0:R0 + 1], 1e-9),
            [KEY("scr_a", R0, R0 + 1)], [KEY("scr_a", R0, R0 + 1)])
        h_v(ctx, lambda e: e.reciprocal(scr[:, R0 + 1:R0 + 2], scr[:, R0:R0 + 1]),
            [KEY("scr_a", R0 + 1, R0 + 2)], [KEY("scr_a", R0, R0 + 1)])
        def fwg(e):
            return e.tensor_tensor(scr[:, P0:P0 + 16], scr[:, P0:P0 + 16],
                                   scr[:, R0 + 1:R0 + 2].broadcast(1, 16)
                                   if False else scr[:, P0:P0 + 16], ALU.mult)
        def fwg2(e):
            return e.tensor_scalar(scr[:, P0:P0 + 16], scr[:, P0:P0 + 16],
                                   scr[:, R0 + 1:R0 + 2], None, ALU.mult)
        h_v(ctx, fwg2, [KEY("scr_a", P0, P0 + 16)],
            [KEY("scr_a", P0, P0 + 16), KEY("scr_a", R0 + 1, R0 + 2)])
        def fwg3(e):
            return e.tensor_scalar(scr[:, P0:P0 + 16], scr[:, P0:P0 + 16],
                                   ctx.t["rsh_t"][:, 0:1], None, ALU.mult)
        h_v(ctx, fwg3, [KEY("scr_a", P0, P0 + 16)],
            [KEY("scr_a", P0, P0 + 16), ("rsh_t",)])
        def ftr(e):
            return e.transpose(p_b[0:16, 0:128], scr[:, P0:P0 + 16], idf[:, :])
        ctx.s.op("tensor", ftr, reads=(KEY("scr_a", P0, P0 + 16), ("ident_f32",)),
                 writes=(("p_b",),))
        def fcpr(e, tt=tt):
            return e.tensor_copy(wrows[0:16, 128 * tt:128 * (tt + 1)],
                                 p_b[0:16, 0:128])
        h_v(ctx, fcpr, [KEY("pool_bf_a", 128 * tt, 128 * (tt + 1), 0, 16)],
            [("p_b",)])

    for cc in range(4):
        t0 = 512 * cc
        for e_ in range(16):
            wb = wb0 if e_ % 2 == 0 else wb1
            wbk = ("wband0",) if e_ % 2 == 0 else ("wband1",)
            h_dma(ctx, wb[:, 0:2048].rearrange("p (g c) -> p g c", g=NDT),
                  ctx.w["L2_ffn_keys"][e_].rearrange("(g p) c -> p g c", p=128),
                  [wbk], queue="qgp")
            h_mm(ctx, [(selb[:, 128 * e_:128 * (e_ + 1)],
                        wrows[0:16, t0:t0 + 512])],
                 p_bc[:, 0:512], ("p_bc",),
                 [("sel16b",), KEY("pool_bf_a", t0, t0 + 512, 0, 16)])
            for fj in range(2):
                pp = p_a if fj == 0 else p_b
                pk = ("p_a",) if fj == 0 else ("p_b",)
                h_mm(ctx, [(wb[:, 256 * g + 128 * fj:256 * g + 128 * (fj + 1)],
                            xb[:, xcols(g, t0, 512)]) for g in range(NDT)],
                     pp[:, 0:512], pk,
                     [wbk] + [("xln_bf", (g * T + t0, g * T + t0 + 512))
                              for g in range(NDT)])
                def fre(e, pp=pp, fj=fj):
                    return e.activation(scr[:, 512 * fj:512 * (fj + 1)],
                                        pp[:, 0:512], AF.Relu)
                ctx.s.op("scalar", fre, reads=(pk,),
                         writes=(KEY("scr_a", 512 * fj, 512 * (fj + 1)),))
                def fwm(e, e_=e_, fj=fj):
                    return e.tensor_tensor(
                        bigb[:, 1024 * e_ + 512 * fj:1024 * e_ + 512 * (fj + 1)],
                        scr[:, 512 * fj:512 * (fj + 1)], p_bc[:, 0:512],
                        ALU.mult)
                h_v(ctx, fwm,
                    [KEY("bigb_bf", 1024 * e_ + 512 * fj,
                         1024 * e_ + 512 * (fj + 1))],
                    [KEY("scr_a", 512 * fj, 512 * (fj + 1)), ("p_bc",)])
        for dd in range(NDT):
            wb = wb0 if dd % 2 == 0 else wb1
            wbk = ("wband0",) if dd % 2 == 0 else ("wband1",)
            h_dma(ctx, wb[:, 0:4096].rearrange("p (a g c) -> p a g c", a=16, g=2),
                  ctx.w["L2_ffn_values"][:, :, 128 * dd:128 * (dd + 1)]
                  .rearrange("a (g p) c -> p a g c", p=128),
                  [wbk], queue="qgp")
            pp = p_c if dd % 2 == 0 else p_d
            pk = ("p_c",) if dd % 2 == 0 else ("p_d",)
            pairs = []
            rk = [wbk]
            for e_ in range(16):
                for fj in range(2):
                    pairs.append((wb[:, 256 * e_ + 128 * fj:
                                     256 * e_ + 128 * (fj + 1)],
                                  bigb[:, 1024 * e_ + 512 * fj:
                                       1024 * e_ + 512 * (fj + 1)]))
                    rk.append(KEY("bigb_bf", 1024 * e_ + 512 * fj,
                                  1024 * e_ + 512 * (fj + 1)))
            h_mm(ctx, pairs, pp[:, 0:512], pk, rk)
            def facc(e, dd=dd, t0=t0, pp=pp):
                return e.tensor_add(x[:, xcols(dd, t0, 512)], pp[:, 0:512],
                                    x[:, xcols(dd, t0, 512)])
            xk = ("x_res", (dd * T + t0, dd * T + t0 + 512))
            h_v(ctx, facc, [xk], [pk, xk])




def build_ffn3(ctx, first):
    x = ctx.t["x_res"]
    xb = ctx.t["xln_bf"]
    build_norm(ctx, x, "x_res", xb, "xln_bf", kind="ln")
    bigb = ctx.t["bigb_bf"]
    qt = ctx.t["qtile_bf"]
    pba, pbb = ctx.t["pool_bf_a"], ctx.t["pool_bf_b"]
    acc = ctx.t["rowpool"]
    wb0, wb1 = ctx.t["wband0"], ctx.t["wband1"]
    p_a, p_b, p_c, p_d = (ctx.psum[n] for n in ("p_a", "p_b", "p_c", "p_d"))
    p_st, p_bc = ctx.psum["p_stat"], ctx.psum["p_bc"]

    def KEY(name, c0, c1, p0=0, p1=128):
        return (name, (c0, c1), (p0, p1))

    if first:
        wcbf = ctx.dscr("wcbf", (16 * D, D), BF16).ap()
        for i in range(8):
            h_dma(ctx, wcbf[2048 * i:2048 * (i + 1), :],
                  ctx.w["L3_ffn_cprojT"][2048 * i:2048 * (i + 1), :],
                  [("wcbf", (2048 * i, 2048 * (i + 1)))], queue="qgp")
    else:
        wcbf = ctx.dram["wcbf"].ap()

    CP = [(p_st, 0, "p_stat"), (p_st, 512, "p_stat"), (p_bc, 0, "p_bc"),
          (p_bc, 512, "p_bc"), (p_a, 0, "p_a"), (p_b, 0, "p_b"),
          (p_c, 0, "p_c"), (p_d, 0, "p_d")]
    ZS = [(p_st, 0, "p_stat"), (p_st, 512, "p_stat"), (p_bc, 0, "p_bc"),
          (p_bc, 512, "p_bc")]

    CHT = 256
    for tc in range(T // CHT):
        t0 = tc * CHT
        for h in range(4):
            half = h % 2
            wbd2 = wb0 if half == 0 else wb1
            wbd2k = ("wband0",) if half == 0 else ("wband1",)
            h_dma(ctx, wbd2[:, 0:4096].rearrange("p (g c) -> p g c", g=8),
                  ctx.w["L3_ffn_bd2"][h].rearrange("g p c -> p g c"),
                  [wbd2k], queue="qgp")
            h_dma(ctx, qt[:, 0:1024].rearrange("p (g c) -> p g c", g=8),
                  ctx.w["L3_ffn_bd1"][h].rearrange("g p c -> p g c"),
                  [KEY("qtile_bf", 0, 1024)], queue="qgp")
            for rg in range(8):
                h_mm(ctx, [(qt[:, 128 * rg:128 * (rg + 1)],
                            xb[:, xcols(rg, t0, CHT)])],
                     p_a[:, 0:CHT], ("p_a", (0, 256)),
                     [KEY("qtile_bf", 0, 1024),
                      ("xln_bf", (rg * T + t0, rg * T + t0 + CHT))])
                h_v(ctx, lambda e: e.tensor_copy(qt[:, 1024:1024 + CHT],
                                                 p_a[:, 0:CHT]),
                    [KEY("qtile_bf", 1024, 1024 + CHT)], [("p_a", (0, 256))])
                for m in range(4):
                    pz, zo, zk = ZS[m]
                    def fz(e, m=m, rg=rg, pz=pz, zo=zo, wbd2=wbd2):
                        return e.matmul(pz[:, zo:zo + CHT],
                                        wbd2[:, 512 * rg + 128 * m:
                                             512 * rg + 128 * (m + 1)],
                                        qt[:, 1024:1024 + CHT],
                                        start=True, stop=True)
                    ctx.s.op("tensor", fz,
                             reads=(wbd2k, KEY("qtile_bf", 1024, 1024 + CHT)),
                             writes=((zk, (zo, zo + CHT)),))
                for mb, pz, zk in ((0, p_st, "p_stat"), (1, p_bc, "p_bc")):
                    def fg(e, rg=rg, pz=pz, half=half, mb=mb):
                        return e.activation(
                            bigb[:, 8192 * half + 1024 * rg + 512 * mb:
                                 8192 * half + 1024 * rg + 512 * (mb + 1)]
                            .rearrange("p (a c) -> p a c", a=2),
                            pz[:, 0:1024].rearrange("p (a c) -> p a c", a=2)
                            [:, :, 0:CHT],
                            AF.Gelu)
                    ctx.s.op("scalar", fg,
                             reads=((zk, (0, 768)),),
                             writes=(KEY("bigb_bf",
                                         8192 * half + 1024 * rg + 512 * mb,
                                         8192 * half + 1024 * rg + 512 * (mb + 1)),))
            for b2 in range(16):
                fi0 = h * 32 + 2 * b2
                wc = pba if b2 % 2 == 0 else pbb
                wck = ("pool_bf_a",) if b2 % 2 == 0 else ("pool_bf_b",)
                h_dma(ctx, wc[:, 0:2048].rearrange("p (a c) -> p a c", a=2),
                      wcbf[128 * fi0:128 * (fi0 + 2), :]
                      .rearrange("(a p) c -> p a c", p=128),
                      [wck], [("wcbf",)],
                      queue="qsync" if b2 % 2 == 0 else "qscal")
                for a in range(2):
                    li = 2 * b2 + a
                    hsl = 8192 * half + 256 * li
                    for dd in range(8):
                        pt, off, ptk = CP[dd]
                        def fc(e, pt=pt, off=off, dd=dd, wc=wc, a=a, hsl=hsl,
                               li=li):
                            return e.matmul(
                                pt[:, off:off + CHT],
                                wc[:, 1024 * a + 128 * dd:
                                   1024 * a + 128 * (dd + 1)],
                                bigb[:, hsl:hsl + 256],
                                start=(li == 0), stop=(li == 31),
                                skip_group_check=True)
                        ctx.s.op("tensor", fc,
                                 reads=(wck, KEY("bigb_bf", hsl, hsl + 256)),
                                 writes=((ptk, (off, off + CHT)),))
            for dd in range(8):
                pt, off, ptk = CP[dd]
                if h == 0:
                    def fe0(e, pt=pt, off=off, dd=dd):
                        return e.tensor_copy(acc[:, 256 * dd:256 * (dd + 1)],
                                             pt[:, off:off + CHT])
                    h_v(ctx, fe0, [("rows", (256 * dd, 256 * (dd + 1)))],
                        [(ptk, (off, off + CHT))])
                else:
                    def fe1(e, pt=pt, off=off, dd=dd):
                        return e.tensor_add(acc[:, 256 * dd:256 * (dd + 1)],
                                            pt[:, off:off + CHT],
                                            acc[:, 256 * dd:256 * (dd + 1)])
                    h_v(ctx, fe1, [("rows", (256 * dd, 256 * (dd + 1)))],
                        [(ptk, (off, off + CHT)),
                         ("rows", (256 * dd, 256 * (dd + 1)))])
        for dd in range(8):
            def fac(e, dd=dd, t0=t0):
                return e.scalar_tensor_tensor(
                    x[:, xcols(dd, t0, CHT)],
                    acc[:, 256 * dd:256 * (dd + 1)], 0.5,
                    x[:, xcols(dd, t0, CHT)], ALU.mult, ALU.add)
            xk = ("x_res", (dd * T + t0, dd * T + t0 + CHT))
            h_v(ctx, fac, [xk], [("rows", (256 * dd, 256 * (dd + 1))), xk])




def build_attn3(ctx):
    x = ctx.t["x_res"]
    xb = ctx.t["xln_bf"]
    build_norm(ctx, x, "x_res", xb, "xln_bf", kind="ln")
    bigb = ctx.t["bigb_bf"]
    qt = ctx.t["qtile_bf"]
    qtf = ctx.t["qtile_bf"].bitcast(F32)
    pba, pbb = ctx.t["pool_bf_a"], ctx.t["pool_bf_b"]
    wrow = ctx.t["wbuf_f32"]
    scr = ctx.t["scr_a"]
    sm = ctx.t["smalltok"]
    wb0, wb1 = ctx.t["wband0"], ctx.t["wband1"]
    At = ctx.t["a3_sb"]
    b4 = ctx.t["bias4_sb"]
    cwt = ctx.t["convw_sb"]
    carry = ctx.t["carry_sb"]
    p_a, p_b, p_c, p_d = (ctx.psum[n] for n in ("p_a", "p_b", "p_c", "p_d"))
    p_st, p_bc = ctx.psum["p_stat"], ctx.psum["p_bc"]
    idf = ctx.t["ident_f32"]

    def KEY(name, c0, c1, p0=0, p1=128):
        return (name, (c0, c1), (p0, p1))

    xc32 = ctx.dscr("xc32", (D, T), F32).ap()
    ybf = ctx.dscr("ybf", (D, T), BF16).ap()
    bcr = ctx.dscr("bcrows", (32, T), F32).ap()

    for k in range(NDT):
        wb = wb0 if k % 2 == 0 else wb1
        wbk = ("wband0",) if k % 2 == 0 else ("wband1",)
        load_band(ctx, wb[:, 0:1024], wbk, ctx.w["L3_attn_Win"],
                  128 * k, 128 * (k + 1))
        for cc in range(4):
            pp = p_a if cc % 2 == 0 else p_b
            pk = ("p_a",) if cc % 2 == 0 else ("p_b",)
            h_mm(ctx, [(wb[:, 128 * g:128 * (g + 1)],
                        xb[:, xcols(g, 512 * cc, 512)]) for g in range(NDT)],
                 pp[:, 0:512], pk,
                 [wbk] + [("xln_bf", (g * T + 512 * cc, g * T + 512 * (cc + 1)))
                          for g in range(NDT)])
            def fev(e, cc=cc, pp=pp, k=k):
                return e.tensor_scalar(wrow[:, 512 * cc:512 * (cc + 1)],
                                       pp[:, 0:512], b4[:, k:k + 1], None,
                                       ALU.add)
            h_v(ctx, fev, [KEY("wbuf_f32", 512 * cc, 512 * (cc + 1))],
                [pk, ("bias4_sb",)])
        XI, XO = 0, 2048
        def c1(e, k=k):
            return e.tensor_scalar(wrow[:, XO:XO + 2048], wrow[:, XI:XI + 2048],
                                   cwt[:, 3 * k + 1:3 * k + 2], None, ALU.mult)
        h_v(ctx, c1, [KEY("wbuf_f32", XO, XO + 2048)],
            [KEY("wbuf_f32", XI, XI + 2048), ("convw_sb",)])
        def c0(e, k=k):
            return e.scalar_tensor_tensor(wrow[:, XO + 1:XO + 2048],
                                          wrow[:, XI:XI + 2047],
                                          cwt[:, 3 * k:3 * k + 1],
                                          wrow[:, XO + 1:XO + 2048],
                                          ALU.mult, ALU.add)
        h_v(ctx, c0, [KEY("wbuf_f32", XO, XO + 2048)],
            [KEY("wbuf_f32", XI, XI + 2048), ("convw_sb",),
             KEY("wbuf_f32", XO, XO + 2048)])
        def c2(e, k=k):
            return e.scalar_tensor_tensor(wrow[:, XO:XO + 2047],
                                          wrow[:, XI + 1:XI + 2048],
                                          cwt[:, 3 * k + 2:3 * k + 3],
                                          wrow[:, XO:XO + 2047],
                                          ALU.mult, ALU.add)
        h_v(ctx, c2, [KEY("wbuf_f32", XO, XO + 2048)],
            [KEY("wbuf_f32", XI, XI + 2048), ("convw_sb",),
             KEY("wbuf_f32", XO, XO + 2048)])
        def cb(e, k=k):
            return e.tensor_scalar(wrow[:, XO:XO + 2048], wrow[:, XO:XO + 2048],
                                   cwt[:, 24 + k:25 + k], None, ALU.add)
        h_v(ctx, cb, [KEY("wbuf_f32", XO, XO + 2048)],
            [KEY("wbuf_f32", XO, XO + 2048), ("convw_sb",)])
        h_act(ctx, wrow[:, XI:XI + 2048], wrow[:, XO:XO + 2048], AF.Silu,
              [KEY("wbuf_f32", XI, XI + 2048)], [KEY("wbuf_f32", XO, XO + 2048)])
        def fbf(e, k=k):
            return e.tensor_copy(bigb[:, k * T:(k + 1) * T],
                                 wrow[:, XI:XI + 2048])
        h_v(ctx, fbf, [("bigb_bf", (k * T, (k + 1) * T))],
            [KEY("wbuf_f32", XI, XI + 2048)])
        h_dma(ctx, xc32[128 * k:128 * (k + 1), :], wrow[:, XI:XI + 2048],
              [("xc32", (128 * k, 128 * (k + 1)))],
              [KEY("wbuf_f32", XI, XI + 2048)], queue="qsync")

    load_band(ctx, wb0[:, 0:256], ("wband0",), ctx.w["L3_attn_Wbc"], 0, 32)
    for tt in range(16):
        c0_ = 128 * tt
        h_mm(ctx, [(bigb[:, g * T + c0_:g * T + c0_ + 128],
                    wb0[:, 32 * g:32 * (g + 1)]) for g in range(NDT)],
             p_a[:, 0:32], ("p_a",),
             [("wband0",)] + [("bigb_bf", (g * T + c0_, g * T + c0_ + 128))
                              for g in range(NDT)])
        def fb(e):
            return e.tensor_tensor(scr[:, 0:32], p_a[:, 0:32],
                                   ctx.t["bbc_sb"][:, :], ALU.add)
        h_v(ctx, fb, [KEY("scr_a", 0, 32)], [("p_a",), ("bbc_sb",)])
        def ftr(e):
            return e.transpose(p_b[0:32, 0:128], scr[:, 0:32], idf[:, :])
        ctx.s.op("tensor", ftr, reads=(KEY("scr_a", 0, 32), ("ident_f32",)),
                 writes=(("p_b",),))
        def fcp(e, tt=tt):
            return e.tensor_copy(wrow[0:32, 128 * tt:128 * (tt + 1)],
                                 p_b[0:32, 0:128])
        h_v(ctx, fcp, [KEY("wbuf_f32", 128 * tt, 128 * (tt + 1), 0, 32)],
            [("p_b",)])
    h_dma(ctx, bcr[:, :], wrow[0:32, 0:2048], [("bcrows",)],
          [KEY("wbuf_f32", 0, 2048, 0, 32)], queue="qsync")

    HB = 1024
    for k in range(NDT):
        for hf in range(2):
            t0 = HB * hf
            wb = wb0 if k % 2 == 0 else wb1
            wbk = ("wband0",) if k % 2 == 0 else ("wband1",)
            if hf == 0:
                load_band(ctx, wb[:, 0:1024], wbk, ctx.w["L3_attn_Wdt"],
                          128 * k, 128 * (k + 1))
            for cc in range(2):
                pp = p_a if cc % 2 == 0 else p_b
                pk = ("p_a",) if cc % 2 == 0 else ("p_b",)
                h_mm(ctx, [(wb[:, 128 * g:128 * (g + 1)],
                            bigb[:, g * T + t0 + 512 * cc:
                                 g * T + t0 + 512 * (cc + 1)])
                           for g in range(NDT)],
                     pp[:, 0:512], pk,
                     [wbk] + [("bigb_bf", (g * T + t0 + 512 * cc,
                                           g * T + t0 + 512 * (cc + 1)))
                              for g in range(NDT)])
                def fr(e, cc=cc, pp=pp, k=k):
                    return e.tensor_scalar(wrow[:, 512 * cc:512 * (cc + 1)],
                                           pp[:, 0:512], b4[:, 16 + k:17 + k],
                                           0.0, ALU.add, ALU.max)
                h_v(ctx, fr, [KEY("wbuf_f32", 512 * cc, 512 * (cc + 1))],
                    [pk, ("bias4_sb",)])
                def fa(e, cc=cc, pp=pp, k=k):
                    return e.activation(scr[:, 512 * cc:512 * (cc + 1)],
                                        pp[:, 0:512], AF.Abs,
                                        bias=b4[:, 16 + k:17 + k])
                ctx.s.op("scalar", fa, reads=(pk, ("bias4_sb",)),
                         writes=(KEY("scr_a", 512 * cc, 512 * (cc + 1)),))
                h_act(ctx, scr[:, 512 * cc:512 * (cc + 1)],
                      scr[:, 512 * cc:512 * (cc + 1)], AF.Exp,
                      [KEY("scr_a", 512 * cc, 512 * (cc + 1))],
                      [KEY("scr_a", 512 * cc, 512 * (cc + 1))], scale=-1.0)
                h_act(ctx, scr[:, 512 * cc:512 * (cc + 1)],
                      scr[:, 512 * cc:512 * (cc + 1)], AF.Ln,
                      [KEY("scr_a", 512 * cc, 512 * (cc + 1))],
                      [KEY("scr_a", 512 * cc, 512 * (cc + 1)), ("cpool",)],
                      bias=ctx.c.col(ctx, 1.0))
                def fadd(e, cc=cc):
                    return e.tensor_add(wrow[:, 512 * cc:512 * (cc + 1)],
                                        wrow[:, 512 * cc:512 * (cc + 1)],
                                        scr[:, 512 * cc:512 * (cc + 1)])
                h_v(ctx, fadd, [KEY("wbuf_f32", 512 * cc, 512 * (cc + 1))],
                    [KEY("wbuf_f32", 512 * cc, 512 * (cc + 1)),
                     KEY("scr_a", 512 * cc, 512 * (cc + 1))])
            h_dma(ctx, scr[:, 0:HB], xc32[128 * k:128 * (k + 1), t0:t0 + HB],
                  [KEY("scr_a", 0, HB)], [("xc32", (128 * k, 128 * (k + 1)))],
                  queue="qscal")
            def fdbx(e):
                return e.tensor_mul(wrow[:, 1024:2048], wrow[:, 0:1024],
                                    scr[:, 0:HB])
            h_v(ctx, fdbx, [KEY("wbuf_f32", 1024, 2048)],
                [KEY("wbuf_f32", 0, 1024), KEY("scr_a", 0, HB)])
            h_v(ctx, lambda e: e.memset(wrow[:, 2048:3072], 0.0),
                [KEY("wbuf_f32", 2048, 3072)], [])
            for n in range(16):
                h_dma(ctx, wrow[:, 3072:4096],
                      bass.broadcast_tensor_aps(
                          wrow[:, 3072:4096], bcr[n:n + 1, t0:t0 + HB])[1],
                      [KEY("wbuf_f32", 3072, 4096)], [("bcrows",)],
                      queue="qgp")
                h_dma(ctx, qtf[:, 0:HB],
                      bass.broadcast_tensor_aps(
                          qtf[:, 0:HB], bcr[16 + n:17 + n, t0:t0 + HB])[1],
                      [("qtile_bf", (0, 2048))], [("bcrows",)],
                      queue="qsync")
                def fda(e, k=k, n=n):
                    return e.activation(scr[:, 0:HB], wrow[:, 0:1024], AF.Exp,
                                        scale=At[:, 16 * k + n:16 * k + n + 1])
                ctx.s.op("scalar", fda,
                         reads=(KEY("wbuf_f32", 0, 1024), ("a3_sb",)),
                         writes=(KEY("scr_a", 0, HB),))
                def fu(e):
                    return e.tensor_mul(sm[:, 0:HB], wrow[:, 1024:2048],
                                        wrow[:, 3072:4096])
                h_v(ctx, fu, [("smalltok",)],
                    [KEY("wbuf_f32", 1024, 2048), KEY("wbuf_f32", 3072, 4096)],
                    eng="gpsimd")
                def fs(e, k=k, n=n, hf=hf):
                    init = carry[:, n:n + 1] if hf == 1 else 0.0
                    return e.tensor_tensor_scan(sm[:, 0:HB], scr[:, 0:HB],
                                                sm[:, 0:HB], init,
                                                ALU.mult, ALU.add)
                h_v(ctx, fs, [("smalltok",)],
                    [KEY("scr_a", 0, HB), ("smalltok",), ("carry_sb",)])
                if hf == 0:
                    def fcar(e, n=n):
                        return e.tensor_copy(carry[:, n:n + 1],
                                             sm[:, HB - 1:HB])
                    h_v(ctx, fcar, [("carry_sb",)], [("smalltok",)])
                def ft1(e):
                    return e.tensor_mul(scr[:, 0:HB], sm[:, 0:HB], qtf[:, 0:HB])
                h_v(ctx, ft1, [KEY("scr_a", 0, HB)],
                    [("smalltok",), ("qtile_bf", (0, 2048))])
                def fy(e):
                    return e.tensor_add(wrow[:, 2048:3072], wrow[:, 2048:3072],
                                        scr[:, 0:HB])
                h_v(ctx, fy, [KEY("wbuf_f32", 2048, 3072)],
                    [KEY("wbuf_f32", 2048, 3072), KEY("scr_a", 0, HB)])
            h_dma(ctx, ybf[128 * k:128 * (k + 1), t0:t0 + HB],
                  wrow[:, 2048:3072],
                  [("ybf", (128 * k, 128 * (k + 1)))],
                  [KEY("wbuf_f32", 2048, 3072)], queue="qgp")

    for cc in range(4):
        t0 = 512 * cc
        for k in range(NDT):
            wb = wb0 if k % 2 == 0 else wb1
            wbk = ("wband0",) if k % 2 == 0 else ("wband1",)
            load_band(ctx, wb[:, 0:1024], wbk, ctx.w["L3_attn_Wg"],
                      128 * k, 128 * (k + 1))
            pp = p_a if k % 2 == 0 else p_b
            pk = ("p_a",) if k % 2 == 0 else ("p_b",)
            h_mm(ctx, [(wb[:, 128 * g:128 * (g + 1)],
                        xb[:, xcols(g, t0, 512)]) for g in range(NDT)],
                 pp[:, 0:512], pk,
                 [wbk] + [("xln_bf", (g * T + t0, g * T + t0 + 512))
                          for g in range(NDT)])
            def fz1(e, pp=pp, k=k):
                return e.activation(scr[:, 0:512], pp[:, 0:512], AF.Silu,
                                    bias=b4[:, 8 + k:9 + k])
            ctx.s.op("scalar", fz1, reads=(pk, ("bias4_sb",)),
                     writes=(KEY("scr_a", 0, 512),))
            h_dma(ctx, qt[:, 0:512], ybf[128 * k:128 * (k + 1), t0:t0 + 512],
                  [("qtile_bf", (0, 2048))], [("ybf", (128 * k, 128 * (k + 1)))],
                  queue="qscal")
            dst = pba if k < 4 else pbb
            dk = "pool_bf_a" if k < 4 else "pool_bf_b"
            def fg(e, k=k, dst=dst):
                return e.tensor_mul(dst[:, 512 * (k % 4):512 * (k % 4 + 1)],
                                    scr[:, 0:512], qt[:, 0:512])
            h_v(ctx, fg, [(dk, (512 * (k % 4), 512 * (k % 4 + 1)))],
                [KEY("scr_a", 0, 512), ("qtile_bf", (0, 2048))])
        for dd in range(NDT):
            wb = wb0 if dd % 2 == 0 else wb1
            wbk = ("wband0",) if dd % 2 == 0 else ("wband1",)
            load_band(ctx, wb[:, 0:1024], wbk, ctx.w["L3_attn_Wout"],
                      128 * dd, 128 * (dd + 1))
            pp = p_c if dd % 2 == 0 else p_d
            pk = ("p_c",) if dd % 2 == 0 else ("p_d",)
            pairs = []
            rks = [wbk]
            for k in range(NDT):
                srcb = pba if k < 4 else pbb
                pairs.append((wb[:, 128 * k:128 * (k + 1)],
                              srcb[:, 512 * (k % 4):512 * (k % 4 + 1)]))
                rks.append(("pool_bf_a" if k < 4 else "pool_bf_b",
                            (512 * (k % 4), 512 * (k % 4 + 1))))
            h_mm(ctx, pairs, pp[:, 0:512], pk, rks)
            def fo(e, dd=dd, t0=t0, pp=pp):
                return e.scalar_tensor_tensor(
                    x[:, xcols(dd, t0, 512)], pp[:, 0:512],
                    b4[:, 24 + dd:25 + dd], x[:, xcols(dd, t0, 512)],
                    ALU.add, ALU.add)
            xk = ("x_res", (dd * T + t0, dd * T + t0 + 512))
            h_v(ctx, fo, [xk], [pk, ("bias4_sb",), xk])




def build_attn1(ctx):
    x = ctx.t["x_res"]
    xb = ctx.t["xln_bf"]
    bigb = ctx.t["bigb_bf"]
    qt = ctx.t["qtile_bf"]
    pa_, pb_ = ctx.t["pool_bf_a"], ctx.t["pool_bf_b"]
    wrow = ctx.t["wbuf_f32"]
    scr = ctx.t["scr_a"]
    sm = ctx.t["smalltok"]
    msb = ctx.t["msb_bf"]
    cmask = ctx.t["cmask_bf"]
    wb0, wb1 = ctx.t["wband0"], ctx.t["wband1"]
    onesf, onesb = ctx.t["ones_f32"], ctx.t["ones_bf"]
    idf, idb = ctx.t["ident_f32"], ctx.t["ident_bf"]
    tri_bf = ctx.t["tri_bf"]
    mc = ctx.t["m1_sb"]
    p_a, p_b, p_c, p_d = (ctx.psum[n] for n in ("p_a", "p_b", "p_c", "p_d"))
    p_st, p_bc = ctx.psum["p_stat"], ctx.psum["p_bc"]

    def KEY(name, c0, c1, p0=0, p1=128):
        return (name, (c0, c1), (p0, p1))

    def fA1(e):
        return e.activation(scr[0:32, 0:1], mc[0:32, 32:33], AF.Exp)
    ctx.s.op("scalar", fA1, reads=(("m1_sb",),),
             writes=(KEY("scr_a", 0, 1, 0, 32),))
    def fA2(e):
        return e.tensor_scalar_mul(mc[0:32, 33:34], scr[0:32, 0:1], -1.0)
    h_v(ctx, fA2, [("m1_sb",)], [KEY("scr_a", 0, 1, 0, 32), ("m1_sb",)])

    xsave = ctx.dscr("xsave", (128, NDT * T), F32).ap()
    zscr = ctx.dscr("zscr", (2048, T), BF16).ap()
    bcscr = ctx.dscr("bcscr", (128, T), BF16).ap()
    ydram = ctx.dscr("ydram", (2048, T), BF16).ap()
    xsdram = ctx.dscr("xsdram", (2048, T), BF16).ap()
    etot_d = ctx.dscr("etot_d", (16, 32), F32).ap()

    h_dma(ctx, xsave[:, :], x[:, :], [("xsave",)], [("x_res",)], queue="qsync")
    build_norm(ctx, x, "x_res", xb, "xln_bf", kind="ln")
    for k in range(NDT):
        def fcp(e, k=k):
            return e.tensor_copy(x[:, xcols(k)], xb[:, xcols(k)])
        h_v(ctx, fcp, [("x_res", (k * T, (k + 1) * T))],
            [("xln_bf", (k * T, (k + 1) * T))])

    build_norm(ctx, x, "x_res", xb, "xln_bf", kind="rms")
    cw = ctx.t["cw1_sb"]
    for of in range(34):
        wb = wb0 if of % 2 == 0 else wb1
        wbk = ("wband0",) if of % 2 == 0 else ("wband1",)
        c0w, c1w = (128 * of, 128 * (of + 1)) if of < 33 else (4224, 4256)
        load_band(ctx, wb[:, 0:8 * (c1w - c0w)], wbk,
                  ctx.w["L1_attn_in_proj_w"], c0w, c1w)
        if of < 33:
            nw = c1w - c0w
            for cc in range(4):
                pp = p_a if cc % 2 == 0 else p_b
                pk = ("p_a",) if cc % 2 == 0 else ("p_b",)
                h_mm(ctx, [(wb[:, nw * g:nw * (g + 1)],
                            xb[:, xcols(g, 512 * cc, 512)]) for g in range(NDT)],
                     pp[:, 0:512], pk,
                     [wbk] + [("xln_bf", (g * T + 512 * cc,
                                          g * T + 512 * (cc + 1)))
                              for g in range(NDT)])
                if of < 16:
                    def fz(e, pp=pp):
                        return e.tensor_copy(qt[:, 0:512], pp[:, 0:512])
                    h_v(ctx, fz, [("qtile_bf", (0, 512))], [pk])
                    h_dma(ctx, zscr[128 * of:128 * (of + 1),
                                    512 * cc:512 * (cc + 1)], qt[:, 0:512],
                          [("zscr", (128 * of, 128 * (of + 1)))],
                          [("qtile_bf", (0, 512))], queue="qgp")
                else:
                    def fxc(e, cc=cc, pp=pp):
                        return e.tensor_copy(wrow[:, 512 * cc:512 * (cc + 1)],
                                             pp[:, 0:512])
                    h_v(ctx, fxc, [KEY("wbuf_f32", 512 * cc, 512 * (cc + 1))],
                        [pk])
            if of >= 16:
                ci = of - 16
                def c3(e, ci=ci):
                    return e.tensor_scalar(wrow[:, 2048:4096], wrow[:, 0:2048],
                                           cw[:, 4 * ci + 3:4 * ci + 4], None,
                                           ALU.mult)
                h_v(ctx, c3, [KEY("wbuf_f32", 2048, 4096)],
                    [KEY("wbuf_f32", 0, 2048), ("cw1_sb",)])
                for j, sh in ((2, 1), (1, 2), (0, 3)):
                    def cj(e, ci=ci, j=j, sh=sh):
                        return e.scalar_tensor_tensor(
                            wrow[:, 2048 + sh:4096], wrow[:, 0:2048 - sh],
                            cw[:, 4 * ci + j:4 * ci + j + 1],
                            wrow[:, 2048 + sh:4096], ALU.mult, ALU.add)
                    h_v(ctx, cj, [KEY("wbuf_f32", 2048, 4096)],
                        [KEY("wbuf_f32", 0, 2048), ("cw1_sb",),
                         KEY("wbuf_f32", 2048, 4096)])
                def cb(e, ci=ci):
                    return e.tensor_scalar(wrow[:, 2048:4096],
                                           wrow[:, 2048:4096],
                                           cw[:, 68 + ci:69 + ci], None, ALU.add)
                h_v(ctx, cb, [KEY("wbuf_f32", 2048, 4096)],
                    [KEY("wbuf_f32", 2048, 4096), ("cw1_sb",)])
                if of < 32:
                    h_act(ctx, qt[:, 0:2048], wrow[:, 2048:4096], AF.Silu,
                          [("qtile_bf", (0, 2048))],
                          [KEY("wbuf_f32", 2048, 4096)])
                    h_dma(ctx, xsdram[128 * (of - 16):128 * (of - 15), :],
                          qt[:, 0:2048],
                          [("xsdram", (128 * (of - 16), 128 * (of - 15)))],
                          [("qtile_bf", (0, 2048))], queue="qgp")
                else:
                    h_act(ctx, qt[:, 0:2048], wrow[:, 2048:4096], AF.Silu,
                          [("qtile_bf", (0, 2048))],
                          [KEY("wbuf_f32", 2048, 4096)])
                    h_dma(ctx, bcscr[:, :], qt[:, 0:2048], [("bcscr",)],
                          [("qtile_bf", (0, 2048))], queue="qgp")
                    h_dma(ctx, pa_[0:64, 0:2048], bcscr[0:64, :],
                          [KEY("pool_bf_a", 0, 2048, 0, 64)], [("bcscr",)],
                          queue="qsync")
                    h_dma(ctx, pb_[0:64, 0:2048], bcscr[64:128, :],
                          [KEY("pool_bf_b", 0, 2048, 0, 64)], [("bcscr",)],
                          queue="qsync")
        else:
            for tt in range(16):
                h_mm(ctx, [(xb[:, xcols(g, 128 * tt, 128)],
                            wb[:, 32 * g:32 * (g + 1)]) for g in range(NDT)],
                     p_c[:, 0:32], ("p_c",),
                     [wbk] + [("xln_bf", (g * T + 128 * tt,
                                          g * T + 128 * (tt + 1)))
                              for g in range(NDT)])
                def f1(e):
                    return e.tensor_tensor(scr[:, 0:32], p_c[:, 0:32],
                                           mc[:, 0:32], ALU.add)
                h_v(ctx, f1, [KEY("scr_a", 0, 32)], [("p_c",), ("m1_sb",)])
                h_act(ctx, scr[:, 32:64], scr[:, 0:32], AF.Abs,
                      [KEY("scr_a", 32, 64)], [KEY("scr_a", 0, 32)])
                h_act(ctx, scr[:, 32:64], scr[:, 32:64], AF.Exp,
                      [KEY("scr_a", 32, 64)], [KEY("scr_a", 32, 64)],
                      scale=-1.0)
                h_act(ctx, scr[:, 32:64], scr[:, 32:64], AF.Ln,
                      [KEY("scr_a", 32, 64)],
                      [KEY("scr_a", 32, 64), ("cpool",)],
                      bias=ctx.c.col(ctx, 1.0))
                def f2(e, tt=tt):
                    return e.tensor_scalar(sm[:, 32 * tt:32 * (tt + 1)],
                                           scr[:, 0:32], 0.0, None, ALU.max)
                h_v(ctx, f2, [("smalltok", (32 * tt, 32 * (tt + 1)))],
                    [KEY("scr_a", 0, 32)])
                def f3(e, tt=tt):
                    return e.tensor_add(sm[:, 32 * tt:32 * (tt + 1)],
                                        sm[:, 32 * tt:32 * (tt + 1)],
                                        scr[:, 32:64])
                h_v(ctx, f3, [("smalltok", (32 * tt, 32 * (tt + 1)))],
                    [("smalltok", (32 * tt, 32 * (tt + 1))),
                     KEY("scr_a", 32, 64)])
    for tt in range(16):
        def ftr(e, tt=tt):
            return e.transpose(p_b[0:32, 0:128], sm[:, 32 * tt:32 * (tt + 1)],
                               idf[:, :])
        ctx.s.op("tensor", ftr,
                 reads=(("smalltok", (32 * tt, 32 * (tt + 1))), ("ident_f32",)),
                 writes=(("p_b",),))
        def fcp(e, tt=tt):
            return e.tensor_copy(wrow[0:32, 128 * tt:128 * (tt + 1)],
                                 p_b[0:32, 0:128])
        h_v(ctx, fcp, [KEY("wbuf_f32", 128 * tt, 128 * (tt + 1), 0, 32)],
            [("p_b",)])
    def fda(e):
        return e.tensor_scalar(wrow[0:32, 0:2048], wrow[0:32, 0:2048],
                               mc[0:32, 33:34], None, ALU.mult)
    h_v(ctx, fda, [KEY("wbuf_f32", 0, 2048, 0, 32)],
        [KEY("wbuf_f32", 0, 2048, 0, 32), ("m1_sb",)])
    for c in range(16):
        def fsc(e, c=c):
            return e.tensor_tensor_scan(
                wrow[0:32, 2048 + 128 * c:2048 + 128 * (c + 1)],
                wrow[0:32, 128 * c:128 * (c + 1)],
                wrow[0:32, 128 * c:128 * (c + 1)], 0.0, ALU.add, ALU.bypass)
        h_v(ctx, fsc, [KEY("wbuf_f32", 2048 + 128 * c, 2048 + 128 * (c + 1),
                           0, 32)],
            [KEY("wbuf_f32", 128 * c, 128 * (c + 1), 0, 32)])
    h_dma(ctx, bcscr[0:64, :], pb_[0:64, 0:2048],
          [("bcscr",)], [KEY("pool_bf_b", 0, 2048, 0, 64)], queue="qgp")
    h_dma(ctx, pb_[64:128, 0:2048], bcscr[0:64, :],
          [KEY("pool_bf_b", 0, 2048, 64, 128)], [("bcscr",)], queue="qsync")
    NH = 32
    for c in range(16):
        t0 = 128 * c
        PR = KEY("wbuf_f32", 2048 + t0, 2048 + t0 + 128, 0, 32)
        h_dma(ctx, bigb[:, 0:2048].rearrange("p (a c2) -> p a c2", a=16),
              xsdram[:, t0:t0 + 128].rearrange("(a p) c2 -> p a c2", p=128),
              [("bigb_bf", (0, 2048))], [("xsdram",)], queue="qscal")
        def ftp(e, c=c):
            return e.transpose(p_b[0:128, 0:32],
                               wrow[0:32, 2048 + t0:2048 + t0 + 128],
                               idf[0:32, 0:32])
        ctx.s.op("tensor", ftp, reads=(PR, ("ident_f32",)), writes=(("p_b",),))
        h_v(ctx, lambda e: e.tensor_copy(sm[:, 512:544], p_b[0:128, 0:32]),
            [("smalltok", (512, 544))], [("p_b",)])
        def fet(e, c=c):
            return e.activation(scr[0:32, 0:1],
                                wrow[0:32, 2048 + t0 + 127:2048 + t0 + 128],
                                AF.Exp)
        ctx.s.op("scalar", fet,
                 reads=(PR,), writes=(KEY("scr_a", 0, 1, 0, 32),))
        def fwret(e, c=c):
            return e.dma_start(out=etot_d[c:c + 1, :], in_=scr[0:32, 0:1])
        ctx.s.op("gpsimd", fwret, reads=(KEY("scr_a", 0, 1, 0, 32),),
                 writes=(("etot_d",),), queue="qgp")
        def fbce(e, c=c):
            b_ = bass.broadcast_tensor_aps(scr[:, 1:33],
                                           etot_d[c:c + 1, :])[1]
            return e.dma_start(out=scr[:, 1:33], in_=b_)
        ctx.s.op("gpsimd", fbce, reads=(("etot_d",),),
                 writes=(KEY("scr_a", 1, 33),), queue="qgp")
        def fsp(e, c=c):
            return e.tensor_scalar(
                wrow[0:32, t0:t0 + 128],
                wrow[0:32, 2048 + t0:2048 + t0 + 128],
                wrow[0:32, 2048 + t0 + 127:2048 + t0 + 128], None,
                ALU.subtract)
        h_v(ctx, fsp, [KEY("wbuf_f32", t0, t0 + 128, 0, 32)], [PR])
        def ftsp(e, c=c):
            return e.transpose(p_b[0:128, 0:32], wrow[0:32, t0:t0 + 128],
                               idf[0:32, 0:32])
        ctx.s.op("tensor", ftsp,
                 reads=(KEY("wbuf_f32", t0, t0 + 128, 0, 32), ("ident_f32",)),
                 writes=(("p_b",),))
        def fde(e, c=c):
            return e.activation(sm[:, 544:576], p_b[0:128, 0:32], AF.Exp,
                                scale=-1.0)
        ctx.s.op("scalar", fde, reads=(("p_b",),),
                 writes=(("smalltok", (544, 576)),))
        def fde2(e, c=c):
            return e.tensor_tensor(sm[:, 544:576], sm[:, 544:576],
                                   sm[:, 32 * c:32 * (c + 1)], ALU.mult)
        h_v(ctx, fde2, [("smalltok", (544, 576))],
            [("smalltok", (544, 576)), ("smalltok", (32 * c, 32 * (c + 1)))])
        def ftb(e, c=c):
            return e.transpose(p_bc.bitcast(BF16)[0:128, 0:64],
                               pa_[0:64, t0:t0 + 128], idb[0:64, 0:64])
        ctx.s.op("tensor", ftb,
                 reads=(KEY("pool_bf_a", t0, t0 + 128, 0, 64), ("ident_bf",)),
                 writes=(("p_bc",),))
        h_v(ctx, lambda e: e.tensor_copy(qt[:, 2112:2176],
                                         p_bc.bitcast(BF16)[0:128, 0:64]),
            [("qtile_bf", (2112, 2176))], [("p_bc",)])
        def fcb(e, c=c):
            return e.matmul(p_a[:, 0:128], pa_[0:64, t0:t0 + 128],
                            pb_[0:64, t0:t0 + 128], start=True, stop=True)
        ctx.s.op("tensor", fcb,
                 reads=(KEY("pool_bf_a", t0, t0 + 128, 0, 64),
                        KEY("pool_bf_b", t0, t0 + 128, 0, 64)),
                 writes=(("p_a",),))
        for h in range(NH):
            lane = 64 * (h % 2)
            hb2 = h // 2
            srow, scol = 64 * (h // 16), 64 * (h % 16)
            def ft1(e, h=h):
                return e.transpose(p_d[0:1, 0:128], sm[:, 512 + h:513 + h],
                                   idf[:, :])
            ctx.s.op("tensor", ft1,
                     reads=(("smalltok", (512, 544)), ("ident_f32",)),
                     writes=(("p_d",),))
            h_v(ctx, lambda e: e.tensor_copy(scr[0:1, 64:192], p_d[0:1, 0:128]),
                [KEY("scr_a", 64, 192, 0, 1)], [("p_d",)])
            def fk1(e):
                return e.matmul(p_st[:, 0:128], onesf[0:1, :],
                                scr[0:1, 64:192], start=True, stop=True)
            ctx.s.op("tensor", fk1,
                     reads=(("ones_f32",), KEY("scr_a", 64, 192, 0, 1)),
                     writes=(("p_stat", (0, 128)),))
            def fe1(e, h=h):
                return e.tensor_scalar(scr[:, 192:320], p_st[:, 0:128],
                                       sm[:, 512 + h:513 + h], 0.0,
                                       ALU.subtract, ALU.min)
            h_v(ctx, fe1, [KEY("scr_a", 192, 320)],
                [("p_stat", (0, 128)), ("smalltok", (512, 544))])
            h_act(ctx, scr[:, 192:320], scr[:, 192:320], AF.Exp,
                  [KEY("scr_a", 192, 320)], [KEY("scr_a", 192, 320)])
            def fw1(e):
                return e.tensor_tensor(scr[:, 320:448], scr[:, 192:320],
                                       p_a[:, 0:128], ALU.mult)
            h_v(ctx, fw1, [KEY("scr_a", 320, 448)],
                [KEY("scr_a", 192, 320), ("p_a",)])
            def fw2(e):
                return e.tensor_tensor(scr[:, 320:448], scr[:, 320:448],
                                       tri_bf[:, :], ALU.mult)
            h_v(ctx, fw2, [KEY("scr_a", 320, 448)],
                [KEY("scr_a", 320, 448), ("tri_bf",)])
            def fw3(e, c=c, h=h):
                return e.tensor_scalar(qt[:, 2368:2496], scr[:, 320:448],
                                       sm[:, 32 * c + h:32 * c + h + 1],
                                       None, ALU.mult)
            h_v(ctx, fw3, [("qtile_bf", (2368, 2496))],
                [KEY("scr_a", 320, 448),
                 ("smalltok", (32 * c, 32 * (c + 1)))])
            def fct(e, srow=srow):
                return e.activation(scr[srow:srow + 64, 448:576],
                                    p_st[srow:srow + 64, 0:128], AF.Exp)
            ctx.s.op("scalar", fct, reads=(("p_stat", (0, 128)),),
                     writes=(KEY("scr_a", 448, 576, srow, srow + 64),))
            def fct2(e, c=c, srow=srow):
                return e.tensor_tensor(qt[srow:srow + 64, 2240:2368],
                                       pb_[srow:srow + 64, t0:t0 + 128],
                                       scr[srow:srow + 64, 448:576], ALU.mult)
            h_v(ctx, fct2, [("qtile_bf", (2240, 2368), (srow, srow + 64))],
                [KEY("pool_bf_b", t0, t0 + 128, srow, srow + 64),
                 KEY("scr_a", 448, 576, srow, srow + 64)])
            def ftx(e, lane=lane, hb2=hb2):
                return e.transpose(
                    p_bc.bitcast(BF16)[0:128, 0:64],
                    bigb[lane:lane + 64, 128 * hb2:128 * hb2 + 128],
                    idb[lane:lane + 64, lane:lane + 64])
            ctx.s.op("tensor", ftx,
                     reads=(KEY("bigb_bf", 128 * hb2, 128 * hb2 + 128,
                                lane, lane + 64), ("ident_bf",)),
                     writes=(("p_bc",),))
            h_v(ctx, lambda e: e.tensor_copy(
                qt[:, 2048:2112], p_bc.bitcast(BF16)[0:128, 0:64]),
                [("qtile_bf", (2048, 2112))], [("p_bc",)])
            def fbh(e, h=h):
                return e.tensor_scalar(qt[:, 2176:2240], qt[:, 2112:2176],
                                       sm[:, 544 + h:545 + h], None, ALU.mult)
            h_v(ctx, fbh, [("qtile_bf", (2176, 2240))],
                [("qtile_bf", (2112, 2176)), ("smalltok", (544, 576))])
            def fy1(e, c=c, lane=lane):
                return e.matmul(p_c[lane:lane + 64, 0:128], qt[:, 2048:2112],
                                qt[:, 2368:2496], start=True, stop=(c == 0))
            ctx.s.op("tensor", fy1,
                     reads=(("qtile_bf", (2048, 2112)),
                            ("qtile_bf", (2368, 2496))),
                     writes=(("p_c", (0, 128), (lane, lane + 64)),))
            if c > 0:
                def fy2(e, srow=srow, scol=scol, lane=lane):
                    return e.matmul(p_c[lane:lane + 64, 0:128],
                                    msb[srow:srow + 64, scol:scol + 64],
                                    qt[srow:srow + 64, 2240:2368],
                                    start=False, stop=True)
                ctx.s.op("tensor", fy2,
                         reads=(KEY("msb_bf", scol, scol + 64, srow, srow + 64),
                                ("qtile_bf", (2240, 2368),
                                 (srow, srow + 64))),
                         writes=(("p_c", (0, 128), (lane, lane + 64)),))
            def fyev(e, lane=lane, hb2=hb2, h=h):
                return e.scalar_tensor_tensor(
                    qt[lane:lane + 64, 128 * hb2:128 * hb2 + 128],
                    bigb[lane:lane + 64, 128 * hb2:128 * hb2 + 128],
                    mc[lane:lane + 64, 64 + h:65 + h],
                    p_c[lane:lane + 64, 0:128], ALU.mult, ALU.add)
            h_v(ctx, fyev, [("qtile_bf", (128 * hb2, 128 * hb2 + 128),
                             (lane, lane + 64))],
                [KEY("bigb_bf", 128 * hb2, 128 * hb2 + 128,
                     lane, lane + 64), ("m1_sb",),
                 ("p_c", (0, 128), (lane, lane + 64))])
            def fsu(e, srow=srow):
                return e.matmul(p_d[srow:srow + 64, 0:64], qt[:, 2176:2240],
                                qt[:, 2048:2112], start=True, stop=True)
            ctx.s.op("tensor", fsu,
                     reads=(("qtile_bf", (2176, 2240)),
                            ("qtile_bf", (2048, 2112))),
                     writes=(("p_d",),))
            if c > 0:
                def fs1(e, srow=srow, scol=scol, h=h):
                    return e.tensor_scalar(scr[srow:srow + 64, 576:640],
                                           msb[srow:srow + 64,
                                               scol:scol + 64],
                                           scr[srow:srow + 64, 1 + h:2 + h],
                                           None, ALU.mult)
                h_v(ctx, fs1, [KEY("scr_a", 576, 640, srow, srow + 64)],
                    [KEY("msb_bf", scol, scol + 64, srow, srow + 64),
                     KEY("scr_a", 1, 33)])
                def fs2(e, srow=srow, scol=scol):
                    return e.tensor_tensor(msb[srow:srow + 64,
                                               scol:scol + 64],
                                           scr[srow:srow + 64, 576:640],
                                           p_d[srow:srow + 64, 0:64], ALU.add)
                h_v(ctx, fs2,
                    [KEY("msb_bf", scol, scol + 64, srow, srow + 64)],
                    [KEY("scr_a", 576, 640, srow, srow + 64), ("p_d",)])
            else:
                def fs3(e, srow=srow, scol=scol):
                    return e.tensor_copy(msb[srow:srow + 64, scol:scol + 64],
                                         p_d[srow:srow + 64, 0:64])
                h_v(ctx, fs3,
                    [KEY("msb_bf", scol, scol + 64, srow, srow + 64)],
                    [("p_d",)])
        h_dma(ctx, ydram[:, t0:t0 + 128].rearrange("(a p) c2 -> p a c2", p=128),
              qt[:, 0:2048].rearrange("p (a c2) -> p a c2", a=16),
              [("ydram",)], [("qtile_bf", (0, 2048))], queue="qgp")
    for cc in range(4):
        t0 = 512 * cc
        h_dma(ctx, bigb[:, 0:8192].rearrange("p (a c2) -> p a c2", a=16),
              ydram[:, t0:t0 + 512].rearrange("(a p) c2 -> p a c2", p=128),
              [("bigb_bf", (0, 8192))], [("ydram",)], queue="qscal")
        h_dma(ctx, bigb[:, 8192:16384].rearrange("p (a c2) -> p a c2", a=16),
              zscr[:, t0:t0 + 512].rearrange("(a p) c2 -> p a c2", p=128),
              [("bigb_bf", (8192, 16384))], [("zscr",)], queue="qsync")
        for ft in range(16):
            def fsl(e, ft=ft):
                return e.activation(scr[:, 0:512], bigb[:, 8192 + 512 * ft:
                                                        8192 + 512 * (ft + 1)],
                                    AF.Silu)
            ctx.s.op("scalar", fsl,
                     reads=(("bigb_bf", (8192 + 512 * ft,
                                         8192 + 512 * (ft + 1))),),
                     writes=(KEY("scr_a", 0, 512),))
            def fg(e, ft=ft):
                return e.tensor_tensor(bigb[:, 512 * ft:512 * (ft + 1)],
                                       bigb[:, 512 * ft:512 * (ft + 1)],
                                       scr[:, 0:512], ALU.mult)
            h_v(ctx, fg, [("bigb_bf", (512 * ft, 512 * (ft + 1)))],
                [("bigb_bf", (512 * ft, 512 * (ft + 1))),
                 KEY("scr_a", 0, 512)])
            def fsq(e, ft=ft):
                return e.activation(scr[:, 512:1024],
                                    bigb[:, 512 * ft:512 * (ft + 1)],
                                    AF.Square)
            ctx.s.op("scalar", fsq,
                     reads=(("bigb_bf", (512 * ft, 512 * (ft + 1))),),
                     writes=(KEY("scr_a", 512, 1024),))
            def fmm(e, ft=ft):
                return e.matmul(p_st[0:1, 0:512], onesf[:, 0:1],
                                scr[:, 512:1024], start=(ft == 0),
                                stop=(ft == 15))
            ctx.s.op("tensor", fmm,
                     reads=(("ones_f32",), KEY("scr_a", 512, 1024)),
                     writes=(("p_stat", (0, 512)),))
        rows = ctx.t["rowpool"]
        h_v(ctx, lambda e: e.tensor_scalar(rows[0:1, 0:512], p_st[0:1, 0:512],
                                           1.0 / 2048.0, 1e-5, ALU.mult,
                                           ALU.add),
            [("rows", (0, 512))], [("p_stat", (0, 512))])
        h_act(ctx, rows[0:1, 512:1024], rows[0:1, 0:512], AF.Sqrt,
              [("rows", (512, 1024))], [("rows", (0, 512))])
        h_v(ctx, lambda e: e.reciprocal(rows[0:1, 512:1024],
                                        rows[0:1, 512:1024]),
            [("rows", (512, 1024))], [("rows", (512, 1024))])
        h_mm(ctx, [(onesf[0:1, :], rows[0:1, 512:1024])], p_bc[:, 0:512],
             ("p_bc",), [("ones_f32",), ("rows", (512, 1024))])
        for ft in range(16):
            def fn1(e, ft=ft):
                return e.tensor_scalar(scr[:, 0:512],
                                       bigb[:, 512 * ft:512 * (ft + 1)],
                                       mc[:, 96 + ft:97 + ft], None, ALU.mult)
            h_v(ctx, fn1, [KEY("scr_a", 0, 512)],
                [("bigb_bf", (512 * ft, 512 * (ft + 1))), ("m1_sb",)])
            def fn2(e, ft=ft):
                return e.tensor_tensor(bigb[:, 512 * ft:512 * (ft + 1)],
                                       scr[:, 0:512], p_bc[:, 0:512], ALU.mult)
            h_v(ctx, fn2, [("bigb_bf", (512 * ft, 512 * (ft + 1)))],
                [KEY("scr_a", 0, 512), ("p_bc",)])
        for dd in range(NDT):
            wb = wb0 if dd % 2 == 0 else wb1
            wbk = ("wband0",) if dd % 2 == 0 else ("wband1",)
            load_band(ctx, wb[:, 0:2048], wbk, ctx.w["L1_attn_out_proj_w"],
                      128 * dd, 128 * (dd + 1), nrows=2048)
            pp = p_c if dd % 2 == 0 else p_d
            pk = ("p_c",) if dd % 2 == 0 else ("p_d",)
            h_mm(ctx, [(wb[:, 128 * g:128 * (g + 1)],
                        bigb[:, 512 * g:512 * (g + 1)]) for g in range(16)],
                 pp[:, 0:512], pk,
                 [wbk] + [("bigb_bf", (512 * g, 512 * (g + 1)))
                          for g in range(16)])
            def fac(e, dd=dd, t0=t0, pp=pp):
                return e.tensor_add(x[:, xcols(dd, t0, 512)], pp[:, 0:512],
                                    x[:, xcols(dd, t0, 512)])
            xk = ("x_res", (dd * T + t0, dd * T + t0 + 512))
            h_v(ctx, fac, [xk], [pk, xk])

    build_norm(ctx, x, "x_res", xb, "xln_bf", kind="rms")
    for of in range(8):
        wb = wb0 if of % 2 == 0 else wb1
        wbk = ("wband0",) if of % 2 == 0 else ("wband1",)
        load_band(ctx, wb[:, 0:1024], wbk, ctx.w["L1_attn_attn_w"],
                  128 * of, 128 * (of + 1))
        for cc in range(4):
            pp = p_a if cc % 2 == 0 else p_b
            pk = ("p_a",) if cc % 2 == 0 else ("p_b",)
            h_mm(ctx, [(wb[:, 128 * g:128 * (g + 1)],
                        xb[:, xcols(g, 512 * cc, 512)]) for g in range(NDT)],
                 pp[:, 0:512], pk,
                 [wbk] + [("xln_bf", (g * T + 512 * cc,
                                      g * T + 512 * (cc + 1)))
                          for g in range(NDT)])
            def fev(e, of=of, cc=cc, pp=pp):
                return e.tensor_copy(bigb[:, of * T + 512 * cc:
                                          of * T + 512 * (cc + 1)],
                                     pp[:, 0:512])
            h_v(ctx, fev, [("bigb_bf", (of * T + 512 * cc,
                                        of * T + 512 * (cc + 1)))], [pk])
    load_band(ctx, wb0[:, 0:512], ("wband0",), ctx.w["L1_attn_attn_w"],
              1024, 1088)
    for cc in range(4):
        h_mm(ctx, [(wb0[:, 64 * g:64 * (g + 1)],
                    xb[:, xcols(g, 512 * cc, 512)]) for g in range(NDT)],
             p_a[0:64, 0:512], ("p_a",),
             [("wband0",)] + [("xln_bf", (g * T + 512 * cc,
                                          g * T + 512 * (cc + 1)))
                              for g in range(NDT)])
        def fev2(e, cc=cc):
            return e.tensor_copy(pa_[0:64, 512 * cc:512 * (cc + 1)],
                                 p_a[0:64, 0:512])
        h_v(ctx, fev2, [KEY("pool_bf_a", 512 * cc, 512 * (cc + 1), 0, 64)],
            [("p_a",)])
    h_dma(ctx, bcscr[0:64, :], pa_[0:64, 0:2048],
          [("bcscr",)], [KEY("pool_bf_a", 0, 2048, 0, 64)], queue="qgp")
    h_dma(ctx, pa_[64:128, 0:2048], bcscr[0:64, :],
          [KEY("pool_bf_a", 0, 2048, 64, 128)], [("bcscr",)], queue="qsync")
    load_band(ctx, wb1[:, 0:512], ("wband1",), ctx.w["L1_attn_attn_w"],
              1088, 1152)
    for tt in range(16):
        h_mm(ctx, [(xb[:, xcols(g, 128 * tt, 128)],
                    wb1[:, 64 * g:64 * (g + 1)]) for g in range(NDT)],
             p_b[:, 0:64], ("p_b",),
             [("wband1",)] + [("xln_bf", (g * T + 128 * tt,
                                          g * T + 128 * (tt + 1)))
                              for g in range(NDT)])
        def fev3(e, tt=tt):
            return e.tensor_copy(pb_[:, 64 * tt:64 * (tt + 1)], p_b[:, 0:64])
        h_v(ctx, fev3, [KEY("pool_bf_b", 64 * tt, 64 * (tt + 1))], [("p_b",)])
    for h in range(16):
        lane = 64 * (h % 2)
        hb2 = h // 2
        for qc in range(4):
            nkt = 4 * qc + 4
            for kt in range(nkt):
                pp = p_a if kt % 2 == 0 else p_b
                pk = ("p_a",) if kt % 2 == 0 else ("p_b",)
                def fs_(e, lane=lane, hb2=hb2, qc=qc, kt=kt, pp=pp):
                    return e.matmul(pp[:, 0:512],
                                    pa_[lane:lane + 64,
                                        128 * kt:128 * (kt + 1)],
                                    bigb[lane:lane + 64,
                                         hb2 * T + 512 * qc:
                                         hb2 * T + 512 * (qc + 1)],
                                    start=True, stop=True)
                ctx.s.op("tensor", fs_,
                         reads=(KEY("pool_bf_a", 128 * kt, 128 * (kt + 1),
                                    lane, lane + 64),
                                KEY("bigb_bf", hb2 * T + 512 * qc,
                                    hb2 * T + 512 * (qc + 1),
                                    lane, lane + 64)),
                         writes=(pk,))
                def fe_(e, pp=pp):
                    return e.activation(ctx.t["ptile_bf"][:, 0:512],
                                        pp[:, 0:512], AF.Exp, scale=0.125)
                ctx.s.op("scalar", fe_, reads=(pk,), writes=(("ptile_bf",),))
                m = kt - 4 * qc
                if m >= 0:
                    def fmk(e, m=m):
                        return e.tensor_tensor(
                            ctx.t["ptile_bf"][:, 0:512],
                            ctx.t["ptile_bf"][:, 0:512],
                            cmask[:, 384 - 128 * m:896 - 128 * m], ALU.mult)
                    h_v(ctx, fmk, [("ptile_bf",)],
                        [("ptile_bf",), ("cmask_bf",)])
                def fnm(e, kt=kt, qc=qc, nkt=nkt):
                    return e.matmul(p_c[0:64, 0:512],
                                    pb_[:, 64 * kt:64 * (kt + 1)],
                                    ctx.t["ptile_bf"][:, 0:512],
                                    start=(kt == 0), stop=(kt == nkt - 1))
                ctx.s.op("tensor", fnm,
                         reads=(KEY("pool_bf_b", 64 * kt, 64 * (kt + 1)),
                                ("ptile_bf",)),
                         writes=(("p_c", (0, 512), (0, 64)),))
                def fdn(e, kt=kt, nkt=nkt):
                    return e.matmul(p_st[0:1, 0:512], onesb[:, 0:1],
                                    ctx.t["ptile_bf"][:, 0:512],
                                    start=(kt == 0), stop=(kt == nkt - 1))
                ctx.s.op("tensor", fdn,
                         reads=(("ones_bf",), ("ptile_bf",)),
                         writes=(("p_stat", (0, 512)),))
            rows = ctx.t["rowpool"]
            h_v(ctx, lambda e: e.reciprocal(rows[0:1, 0:512],
                                            p_st[0:1, 0:512]),
                [("rows", (0, 512))], [("p_stat", (0, 512))])
            h_mm(ctx, [(onesf[0:1, 0:64], rows[0:1, 0:512])],
                 p_bc[0:64, 0:512], ("p_bc",),
                 [("ones_f32",), ("rows", (0, 512))])
            h_v(ctx, lambda e: e.tensor_copy(scr[0:64, 0:512],
                                             p_c[0:64, 0:512]),
                [KEY("scr_a", 0, 512, 0, 64)], [("p_c", (0, 512), (0, 64))])
            def fo_(e, lane=lane, hb2=hb2, qc=qc):
                return e.tensor_tensor(
                    bigb[lane:lane + 64, hb2 * T + 512 * qc:
                         hb2 * T + 512 * (qc + 1)],
                    scr[0:64, 0:512], p_bc[0:64, 0:512], ALU.mult)
            h_v(ctx, fo_,
                [KEY("bigb_bf", hb2 * T + 512 * qc, hb2 * T + 512 * (qc + 1),
                     lane, lane + 64)],
                [KEY("scr_a", 0, 512, 0, 64), ("p_bc",)])
    for cc in range(4):
        t0 = 512 * cc
        for dd in range(NDT):
            wb = wb0 if dd % 2 == 0 else wb1
            wbk = ("wband0",) if dd % 2 == 0 else ("wband1",)
            load_band(ctx, wb[:, 0:1024], wbk, ctx.w["L1_attn_proj_w"],
                      128 * dd, 128 * (dd + 1))
            pp = p_c if dd % 2 == 0 else p_d
            pk = ("p_c",) if dd % 2 == 0 else ("p_d",)
            h_mm(ctx, [(wb[:, 128 * g:128 * (g + 1)],
                        bigb[:, g * T + t0:g * T + t0 + 512])
                       for g in range(NDT)],
                 pp[:, 0:512], pk,
                 [wbk] + [("bigb_bf", (g * T + t0, g * T + t0 + 512))
                          for g in range(NDT)])
            def fac2(e, dd=dd, t0=t0, pp=pp):
                return e.tensor_add(x[:, xcols(dd, t0, 512)], pp[:, 0:512],
                                    x[:, xcols(dd, t0, 512)])
            xk = ("x_res", (dd * T + t0, dd * T + t0 + 512))
            h_v(ctx, fac2, [xk], [pk, xk])

    build_norm(ctx, x, "x_res", xb, "xln_bf", kind="rms")
    def fmk1(e):
        return e.tensor_scalar(scr[:, 0:16], mc[:, 112:128], -1.0, 1.0,
                               ALU.mult, ALU.add)
    h_v(ctx, fmk1, [KEY("scr_a", 0, 16)], [("m1_sb",)])
    MU = float(np.sqrt(0.5))
    DEN = float(np.sqrt(1.0 / (4.0 * np.pi)) * np.sqrt(2.0))
    for cc in range(4):
        t0 = 512 * cc
        for g in range(NDT):
            dstb = pa_ if g < 4 else pb_
            dbk = "pool_bf_a" if g < 4 else "pool_bf_b"
            co = 512 * (g % 4)
            def fxk(e, g=g, dstb=dstb, co=co, t0=t0):
                return e.tensor_scalar(dstb[:, co:co + 512],
                                       xb[:, xcols(g, t0, 512)],
                                       scr[:, g:g + 1], None, ALU.mult)
            h_v(ctx, fxk, [(dbk, (co, co + 512))],
                [("xln_bf", (g * T + t0, g * T + t0 + 512)),
                 KEY("scr_a", 0, 16)])
            off = 0 if t0 > 0 else 1
            def fxk2(e, g=g, dstb=dstb, co=co, t0=t0, off=off):
                return e.scalar_tensor_tensor(
                    dstb[:, co + off:co + 512],
                    xb[:, xcols(g, t0 - 1 + off, 512 - off)],
                    mc[:, 112 + g:113 + g],
                    dstb[:, co + off:co + 512], ALU.mult, ALU.add)
            h_v(ctx, fxk2, [(dbk, (co, co + 512))],
                [("xln_bf", (g * T, (g + 1) * T)), ("m1_sb",),
                 (dbk, (co, co + 512))])
        for fj in range(32):
            wb = wb0 if fj % 2 == 0 else wb1
            wbk = ("wband0",) if fj % 2 == 0 else ("wband1",)
            load_band(ctx, wb[:, 0:1024], wbk, ctx.w["L1_attn_key_w"],
                      128 * fj, 128 * (fj + 1))
            pp = p_a if fj % 2 == 0 else p_b
            pk = ("p_a",) if fj % 2 == 0 else ("p_b",)
            pairs = []
            rks = [wbk]
            for g in range(NDT):
                srcb = pa_ if g < 4 else pb_
                pairs.append((wb[:, 128 * g:128 * (g + 1)],
                              srcb[:, 512 * (g % 4):512 * (g % 4 + 1)]))
                rks.append(("pool_bf_a" if g < 4 else "pool_bf_b",
                            (512 * (g % 4), 512 * (g % 4 + 1))))
            h_mm(ctx, pairs, pp[:, 0:512], pk, rks)
            def fer(e, pp=pp):
                return e.activation(scr[:, 512:1024], pp[:, 0:512], AF.Erf,
                                    scale=1.0 / DEN,
                                    bias=ctx.c.col(ctx, -MU / DEN))
            ctx.s.op("scalar", fer, reads=(pk, ("cpool",)),
                     writes=(KEY("scr_a", 512, 1024),))
            def fk5(e, fj=fj):
                return e.tensor_scalar(bigb[:, 512 * fj:512 * (fj + 1)],
                                       scr[:, 512:1024], 1.0, 0.5,
                                       ALU.add, ALU.mult)
            h_v(ctx, fk5, [("bigb_bf", (512 * fj, 512 * (fj + 1)))],
                [KEY("scr_a", 512, 1024)])
        for g in range(NDT):
            dstb = pa_ if g < 4 else pb_
            dbk = "pool_bf_a" if g < 4 else "pool_bf_b"
            co = 512 * (g % 4)
            def fxr(e, g=g, dstb=dstb, co=co, t0=t0):
                return e.tensor_scalar(dstb[:, co:co + 512],
                                       xb[:, xcols(g, t0, 512)],
                                       scr[:, 8 + g:9 + g], None, ALU.mult)
            h_v(ctx, fxr, [(dbk, (co, co + 512))],
                [("xln_bf", (g * T + t0, g * T + t0 + 512)),
                 KEY("scr_a", 0, 16)])
            off = 0 if t0 > 0 else 1
            def fxr2(e, g=g, dstb=dstb, co=co, t0=t0, off=off):
                return e.scalar_tensor_tensor(
                    dstb[:, co + off:co + 512],
                    xb[:, xcols(g, t0 - 1 + off, 512 - off)],
                    mc[:, 120 + g:121 + g],
                    dstb[:, co + off:co + 512], ALU.mult, ALU.add)
            h_v(ctx, fxr2, [(dbk, (co, co + 512))],
                [("xln_bf", (g * T, (g + 1) * T)), ("m1_sb",),
                 (dbk, (co, co + 512))])
        for dd in range(NDT):
            wbv = wb0 if dd % 2 == 0 else wb1
            wbvk = ("wband0",) if dd % 2 == 0 else ("wband1",)
            load_band(ctx, wbv[:, 0:4096], wbvk, ctx.w["L1_attn_value_w"],
                      128 * dd, 128 * (dd + 1), nrows=4096)
            pp = p_c if dd % 2 == 0 else p_d
            pk = ("p_c",) if dd % 2 == 0 else ("p_d",)
            h_mm(ctx, [(wbv[:, 128 * fj:128 * (fj + 1)],
                        bigb[:, 512 * fj:512 * (fj + 1)]) for fj in range(32)],
                 pp[:, 0:512], pk,
                 [wbvk] + [("bigb_bf", (512 * fj, 512 * (fj + 1)))
                           for fj in range(32)])
            wbr = ctx.t["wbandv"]
            load_band(ctx, wbr[:, 0:1024], ("wbandv",),
                      ctx.w["L1_attn_recept_w"], 128 * dd, 128 * (dd + 1))
            pairs = []
            rks = [("wbandv",)]
            for g in range(NDT):
                srcb = pa_ if g < 4 else pb_
                pairs.append((wbr[:, 128 * g:128 * (g + 1)],
                              srcb[:, 512 * (g % 4):512 * (g % 4 + 1)]))
                rks.append(("pool_bf_a" if g < 4 else "pool_bf_b",
                            (512 * (g % 4), 512 * (g % 4 + 1))))
            h_mm(ctx, pairs, p_st[:, 0:512], ("p_stat", (0, 512)), rks)
            def fsg(e):
                return e.activation(scr[:, 512:1024], p_st[:, 0:512],
                                    AF.Sigmoid)
            ctx.s.op("scalar", fsg, reads=(("p_stat", (0, 512)),),
                     writes=(KEY("scr_a", 512, 1024),))
            def fcv(e, pp=pp):
                return e.tensor_tensor(scr[:, 1024:1536], scr[:, 512:1024],
                                       pp[:, 0:512], ALU.mult)
            h_v(ctx, fcv, [KEY("scr_a", 1024, 1536)],
                [KEY("scr_a", 512, 1024), pk])
            def fac3(e, dd=dd, t0=t0):
                return e.tensor_add(x[:, xcols(dd, t0, 512)],
                                    scr[:, 1024:1536],
                                    x[:, xcols(dd, t0, 512)])
            xk_ = ("x_res", (dd * T + t0, dd * T + t0 + 512))
            h_v(ctx, fac3, [xk_], [KEY("scr_a", 1024, 1536), xk_])

    for k in range(NDT):
        h_dma(ctx, ctx.t["wbuf_f32"][:, 0:2048], xsave[:, k * T:(k + 1) * T],
              [KEY("wbuf_f32", 0, 2048)], [("xsave",)], queue="qsync")
        def ffin(e, k=k):
            return e.tensor_add(x[:, xcols(k)], x[:, xcols(k)],
                                ctx.t["wbuf_f32"][:, 0:2048])
        h_v(ctx, ffin, [("x_res", (k * T, (k + 1) * T))],
            [("x_res", (k * T, (k + 1) * T)), KEY("wbuf_f32", 0, 2048)])




P_CAP = 640
SC_CHUNKS = ((0, 384), (384, 256))


def build_ffn1_route(ctx, emit_counts):
    x = ctx.t["x_res"]
    xb = ctx.t["xln_bf"]
    build_norm(ctx, x, "x_res", xb, "xln_bf", kind="ln")
    wrow = ctx.t["wbuf_f32"]
    scr = ctx.t["scr_a"]
    sm = ctx.t["smalltok"]
    wb0 = ctx.t["wband0"]
    onesf = ctx.t["ones_f32"]
    idf = ctx.t["ident_f32"]
    iot = ctx.t["iota8_sb"]
    p_a, p_b = ctx.psum["p_a"], ctx.psum["p_b"]

    def KEY(name, c0, c1, p0=0, p1=128):
        return (name, (c0, c1), (p0, p1))

    load_band(ctx, wb0[:, 0:64], ("wband0",), ctx.w["L1_ffn_w_router"], 0, 8)
    for tt in range(16):
        c0 = 128 * tt
        h_mm(ctx, [(xb[:, xcols(g, c0, 128)], wb0[:, 8 * g:8 * (g + 1)])
                   for g in range(NDT)],
             p_a[:, 0:8], ("p_a",),
             [("wband0",)] + [("xln_bf", (g * T + c0, g * T + c0 + 128))
                              for g in range(NDT)])
        h_v(ctx, lambda e: e.tensor_reduce(scr[:, 16:17], p_a[:, 0:8],
                                           AX.X, ALU.max),
            [KEY("scr_a", 16, 17)], [("p_a",)])
        h_v(ctx, lambda e: e.tensor_scalar(scr[:, 0:8], p_a[:, 0:8],
                                           scr[:, 16:17], None, ALU.subtract),
            [KEY("scr_a", 0, 8)], [("p_a",), KEY("scr_a", 16, 17)])
        ctx.s.op("scalar",
                 lambda e: e.activation(scr[:, 0:8], scr[:, 0:8], AF.Exp,
                                        accum_out=scr[:, 17:18]),
                 reads=(KEY("scr_a", 0, 8),),
                 writes=(KEY("scr_a", 0, 8), KEY("scr_a", 17, 18)))
        h_v(ctx, lambda e: e.reciprocal(scr[:, 18:19], scr[:, 17:18]),
            [KEY("scr_a", 18, 19)], [KEY("scr_a", 17, 18)])
        def fpr(e, tt=tt):
            return e.tensor_scalar(sm[:, 8 * tt:8 * (tt + 1)], scr[:, 0:8],
                                   scr[:, 18:19], None, ALU.mult)
        h_v(ctx, fpr, [("smalltok", (8 * tt, 8 * (tt + 1)))],
            [KEY("scr_a", 0, 19)])
        W0 = 128 + 4 * tt
        def fm1(e, tt=tt, W0=W0):
            return e.tensor_reduce(sm[:, W0:W0 + 1],
                                   sm[:, 8 * tt:8 * (tt + 1)], AX.X, ALU.max)
        h_v(ctx, fm1, [("smalltok", (W0, W0 + 1))],
            [("smalltok", (8 * tt, 8 * (tt + 1)))])
        def fk1(e, tt=tt, W0=W0):
            return e.tensor_scalar(scr[:, 0:8], sm[:, 8 * tt:8 * (tt + 1)],
                                   sm[:, W0:W0 + 1], None, ALU.is_ge)
        h_v(ctx, fk1, [KEY("scr_a", 0, 8)],
            [("smalltok", (8 * tt, 8 * (tt + 1))), ("smalltok", (W0, W0 + 1))])
        def fs2(e, tt=tt):
            return e.scalar_tensor_tensor(scr[:, 8:16], scr[:, 0:8], -1e30,
                                          sm[:, 8 * tt:8 * (tt + 1)],
                                          ALU.mult, ALU.add)
        h_v(ctx, fs2, [KEY("scr_a", 8, 16)],
            [KEY("scr_a", 0, 8), ("smalltok", (8 * tt, 8 * (tt + 1)))])
        def fm2(e, W0=W0):
            return e.tensor_reduce(sm[:, W0 + 1:W0 + 2], scr[:, 8:16],
                                   AX.X, ALU.max)
        h_v(ctx, fm2, [("smalltok", (W0 + 1, W0 + 2))], [KEY("scr_a", 8, 16)])
        def fk2(e, W0=W0):
            return e.tensor_scalar(scr[:, 8:16], scr[:, 8:16],
                                   sm[:, W0 + 1:W0 + 2], None, ALU.is_ge)
        h_v(ctx, fk2, [KEY("scr_a", 8, 16)],
            [KEY("scr_a", 8, 16), ("smalltok", (W0 + 1, W0 + 2))])
        for j, mi in ((0, 0), (1, 8)):
            def fed(e, mi=mi):
                return e.tensor_tensor(scr[:, 24:32], scr[:, mi:mi + 8],
                                       iot[:, 0:8], ALU.mult)
            h_v(ctx, fed, [KEY("scr_a", 24, 32)],
                [KEY("scr_a", 0, 16), ("iota8_sb",)])
            def fer(e, W0=W0, j=j):
                return e.tensor_reduce(sm[:, W0 + 2 + j:W0 + 3 + j],
                                       scr[:, 24:32], AX.X, ALU.add)
            h_v(ctx, fer, [("smalltok", (W0 + 2 + j, W0 + 3 + j))],
                [KEY("scr_a", 24, 32)])
        for cch in range(2):
            def ftr(e, W0=W0, cch=cch):
                return e.transpose(p_b[0:1, 0:128],
                                   sm[:, W0 + 2 + cch:W0 + 3 + cch], idf[:, :])
            ctx.s.op("tensor", ftr,
                     reads=(("smalltok", (W0 + 2 + cch, W0 + 3 + cch)),
                            ("ident_f32",)),
                     writes=(("p_b",),))
            h_v(ctx, lambda e: e.tensor_copy(scr[0:1, 32:160],
                                             p_b[0:1, 0:128]),
                [KEY("scr_a", 32, 160, 0, 1)], [("p_b",)])
            for pb in (0, 32):
                ctx.s.op("tensor",
                         lambda e, pb=pb: e.matmul(
                             p_b[pb:pb + 8, 0:128], onesf[0:1, 0:8],
                             scr[0:1, 32:160], start=True, stop=True),
                         reads=(("ones_f32",), KEY("scr_a", 32, 160, 0, 1)),
                         writes=(("p_b",),))
                def foh(e, tt=tt, cch=cch, pb=pb):
                    return e.tensor_scalar(
                        wrow[pb:pb + 8, 256 * tt:256 * (tt + 1)].rearrange(
                            "p (a c2) -> p a c2", c2=2)[:, :, cch:cch + 1],
                        p_b[pb:pb + 8, 0:128], iot[pb:pb + 8, 8:9],
                        None, ALU.is_equal)
                h_v(ctx, foh,
                    [KEY("wbuf_f32", 256 * tt, 256 * (tt + 1), pb, pb + 8)],
                    [("p_b",), ("iota8_sb",)])
    h_v(ctx, lambda e: e.tensor_tensor_scan(
        wrow[32:40, 0:4096], wrow[32:40, 0:4096], wrow[32:40, 0:4096],
        0.0, ALU.add, ALU.bypass),
        [KEY("wbuf_f32", 0, 4096, 32, 40)],
        [KEY("wbuf_f32", 0, 4096, 32, 40)])
    if emit_counts:
        cout = ctx.nc.dram_tensor("counts_out", [8, 1], F32,
                                  kind="ExternalOutput").ap()
        h_dma(ctx, cout[:, :], wrow[32:40, 4095:4096], [("counts_out",)],
              [KEY("wbuf_f32", 0, 4096, 32, 40)], queue="qsync")


def build_ffn1_ffn(ctx):
    x = ctx.t["x_res"]
    xb = ctx.t["xln_bf"]
    bigb = ctx.t["bigb_bf"]
    qt = ctx.t["qtile_bf"]
    pa_, pb_ = ctx.t["pool_bf_a"], ctx.t["pool_bf_b"]
    wrow = ctx.t["wbuf_f32"]
    scr = ctx.t["scr_a"]
    sm = ctx.t["smalltok"]
    rows = ctx.t["rowpool"]
    wb0, wb1 = ctx.t["wband0"], ctx.t["wband1"]
    onesf = ctx.t["ones_f32"]
    idf, idb = ctx.t["ident_f32"], ctx.t["ident_bf"]
    iot = ctx.t["iota8_sb"]
    cnt = ctx.t["cnt_sb"]
    p_a, p_b, p_c, p_d = (ctx.psum[n] for n in ("p_a", "p_b", "p_c", "p_d"))
    p_st, p_bc = ctx.psum["p_stat"], ctx.psum["p_bc"]

    def KEY(name, c0, c1, p0=0, p1=128):
        return (name, (c0, c1), (p0, p1))

    if "w1bf" not in ctx.dram:
        w1bf = ctx.dscr("w1bf", (8, D, 4 * D), BF16).ap()
        w2bf = ctx.dscr("w2bf", (8, 4 * D, D), BF16).ap()
        for e_ in range(8):
            h_dma(ctx, w1bf[e_], ctx.w["L1_ffn_w1"][e_],
                  [("w1bf", (e_, e_ + 1))], queue="qgp")
            h_dma(ctx, w2bf[e_], ctx.w["L1_ffn_w2"][e_],
                  [("w2bf", (e_, e_ + 1))], queue="qgp")
        slotd = ctx.dscr("slotd", (1, 4096), F32).ap()
        sbuf_d = ctx.dscr("sbuf_d", (8 * P_CAP + 128, D), BF16).ap()
        yslotd = ctx.dscr("yslotd", (8 * P_CAP + 128, D), F32).ap()
    else:
        w1bf, w2bf = ctx.dram["w1bf"].ap(), ctx.dram["w2bf"].ap()
        slotd = ctx.dram["slotd"].ap()
        sbuf_d = ctx.dram["sbuf_d"].ap()
        yslotd = ctx.dram["yslotd"].ap()

    h_dma(ctx, cnt[0:8, 0:8], ctx.w["counts_in"][:, :], [("cnt_sb",)],
          queue="qsync")
    h_dma(ctx, cnt[0:8, 8:16], ctx.w["coremask_in"][:, :], [("cnt_sb",)],
          queue="qsync")
    h_v(ctx, lambda e: e.tensor_tensor(cnt[0:8, 16:24], cnt[0:8, 0:8],
                                       cnt[0:8, 8:16], ALU.mult),
        [("cnt_sb",)], [("cnt_sb",)])
    h_v(ctx, lambda e: e.tensor_reduce(cnt[0:8, 24:25], cnt[0:8, 16:24],
                                       AX.X, ALU.add),
        [("cnt_sb",)], [("cnt_sb",)])
    for ch in range(8):
        F0 = 512 * ch
        def fmv(e, F0=F0):
            return e.matmul(p_bc[0:8, 512:1024], idf[32:40, 32:40],
                            wrow[32:40, F0:F0 + 512], start=True, stop=True)
        ctx.s.op("tensor", fmv,
                 reads=(("ident_f32",), KEY("wbuf_f32", F0, F0 + 512, 32, 40)),
                 writes=(("p_bc", (512, 1024)),))
        def frr(e, F0=F0):
            return e.scalar_tensor_tensor(scr[0:8, 0:512],
                                          p_bc[0:8, 512:1024], 1.0,
                                          wrow[0:8, F0:F0 + 512],
                                          ALU.subtract, ALU.mult)
        h_v(ctx, frr, [KEY("scr_a", 0, 512, 0, 8)],
            [("p_bc", (512, 1024)), KEY("wbuf_f32", F0, F0 + 512, 0, 8)])
        ctx.s.op("tensor",
                 lambda e: e.matmul(p_st[0:1, 0:512], onesf[0:8, 0:1],
                                    scr[0:8, 0:512], start=True, stop=True),
                 reads=(("ones_f32",), KEY("scr_a", 0, 512, 0, 8)),
                 writes=(("p_stat", (0, 512)),))
        h_v(ctx, lambda e: e.tensor_copy(rows[0:1, 0:512], p_st[0:1, 0:512]),
            [("rows", (0, 512), (0, 1))], [("p_stat", (0, 512))])
        def fmp(e, F0=F0):
            return e.matmul(p_st[0:1, 512:1024], cnt[0:8, 24:25],
                            wrow[0:8, F0:F0 + 512], start=True, stop=True)
        ctx.s.op("tensor", fmp,
                 reads=(("cnt_sb",), KEY("wbuf_f32", F0, F0 + 512, 0, 8)),
                 writes=(("p_stat", (512, 1024)),))
        def fme(e, F0=F0):
            return e.matmul(p_bc[0:1, 0:512], iot[0:8, 8:9],
                            wrow[0:8, F0:F0 + 512], start=True, stop=True)
        ctx.s.op("tensor", fme,
                 reads=(("iota8_sb",), KEY("wbuf_f32", F0, F0 + 512, 0, 8)),
                 writes=(("p_bc", (0, 512)),))
        h_v(ctx, lambda e: e.tensor_tensor(rows[0:1, 512:1024],
                                           rows[0:1, 0:512],
                                           p_st[0:1, 512:1024], ALU.add),
            [("rows", (512, 1024), (0, 1))],
            [("rows", (0, 512), (0, 1)), ("p_stat", (512, 1024))])
        h_v(ctx, lambda e: e.tensor_scalar(rows[0:1, 512:1024],
                                           rows[0:1, 512:1024], 4096.0, None,
                                           ALU.is_lt),
            [("rows", (512, 1024), (0, 1))], [("rows", (512, 1024), (0, 1))])
        h_v(ctx, lambda e: e.tensor_scalar(rows[0:1, 1024:1536],
                                           rows[0:1, 0:512], float(P_CAP),
                                           None, ALU.is_lt),
            [("rows", (1024, 1536), (0, 1))], [("rows", (0, 512), (0, 1))])
        h_v(ctx, lambda e: e.tensor_tensor(rows[0:1, 512:1024],
                                           rows[0:1, 512:1024],
                                           rows[0:1, 1024:1536], ALU.mult),
            [("rows", (512, 1024), (0, 1))],
            [("rows", (512, 1024), (0, 1)), ("rows", (1024, 1536), (0, 1))])
        h_v(ctx, lambda e: e.scalar_tensor_tensor(rows[0:1, 0:512],
                                                  p_bc[0:1, 0:512],
                                                  float(P_CAP),
                                                  rows[0:1, 0:512],
                                                  ALU.mult, ALU.add),
            [("rows", (0, 512), (0, 1))],
            [("p_bc", (0, 512)), ("rows", (0, 512), (0, 1))])
        inv = ctx.t["invrow_sb"]
        h_v(ctx, lambda e: e.tensor_sub(rows[0:1, 0:512], rows[0:1, 0:512],
                                        inv[0:1, 0:512]),
            [("rows", (0, 512), (0, 1))],
            [("rows", (0, 512), (0, 1)), ("invrow_sb",)])
        h_v(ctx, lambda e: e.tensor_tensor(rows[0:1, 0:512],
                                           rows[0:1, 512:1024],
                                           rows[0:1, 0:512], ALU.mult),
            [("rows", (0, 512), (0, 1))],
            [("rows", (512, 1024), (0, 1)), ("rows", (0, 512), (0, 1))])
        h_v(ctx, lambda e: e.tensor_tensor(rows[0:1, 0:512], rows[0:1, 0:512],
                                           inv[0:1, 0:512], ALU.add),
            [("rows", (0, 512), (0, 1))],
            [("rows", (0, 512), (0, 1)), ("invrow_sb",)])
        h_dma(ctx, slotd[0:1, F0:F0 + 512], rows[0:1, 0:512], [("slotd",)],
              [("rows", (0, 512), (0, 1))], queue="qsync")
    h_v(ctx, lambda e: e.memset(rows[:, 0:1024], 0.0),
        [("rows", (0, 1024))], [("rows", (0, 1024))])
    h_dma(ctx, yslotd[8 * P_CAP:8 * P_CAP + 128, :], rows[:, 0:1024],
          [("yslotd",)], [("rows", (0, 1024))], queue="qsync")
    if FFN1_STOP < 1:
        return
    h_dma(ctx, ctx.t["idxf_sb"][:, :].rearrange("p (t2 c2) -> p t2 c2", c2=2),
          slotd[0, :].rearrange("(t2 p c2) -> p t2 c2", p=128, c2=2),
          [("idxf_sb",)], [("slotd",)], queue="qsync")
    h_v(ctx, lambda e: e.tensor_copy(ctx.t["idx_sb"][:, :],
                                     ctx.t["idxf_sb"][:, :]),
        [("idx_sb",)], [("idxf_sb",)])
    if FFN1_STOP < 2:
        return
    for tt in range(16):
        for g in range(NDT):
            def ftp(e, tt=tt, g=g):
                return e.transpose(p_d.bitcast(BF16)[0:128, 0:128],
                                   xb[:, xcols(g, 128 * tt, 128)], idb[:, :])
            ctx.s.op("tensor", ftp,
                     reads=(("xln_bf", (g * T + 128 * tt,
                                        g * T + 128 * (tt + 1))),
                            ("ident_bf",)),
                     writes=(("p_d",),))
            def fev(e, g=g):
                return e.tensor_copy(qt[:, 128 * g:128 * (g + 1)],
                                     p_d.bitcast(BF16)[0:128, 0:128])
            h_v(ctx, fev, [("qtile_bf", (128 * g, 128 * (g + 1)))], [("p_d",)])
        for cch in range(2):
            def fsct(e, tt=tt, cch=cch):
                return e.indirect_dma_start(
                    out=sbuf_d[:, :],
                    out_offset=IndirectOffsetOnAxis(
                        ap=ctx.t["idx_sb"][:, 2 * tt + cch:2 * tt + cch + 1],
                        axis=0),
                    in_=qt[:, 0:1024], in_offset=None)
            ctx.s.op("gpsimd", fsct,
                     reads=(("qtile_bf", (0, 1024)), ("idx_sb",)),
                     writes=(("sbuf_d",),), queue="qgp")

    def xs_ap(g, SC):
        if SC == 384:
            if g < 5:
                return pa_[:, 384 * g:384 * (g + 1)], \
                    ("pool_bf_a", (384 * g, 384 * (g + 1)))
            return pb_[:, 384 * (g - 5):384 * (g - 4)], \
                ("pool_bf_b", (384 * (g - 5), 384 * (g - 4)))
        return pa_[:, SC * g:SC * (g + 1)], ("pool_bf_a", (SC * g, SC * (g + 1)))

    if FFN1_STOP < 3:
        return
    GA = 0.044715
    SQ2PI = float(np.sqrt(2.0 / np.pi))
    for e_ in range(8 if FFN1_STOP >= 6 else 1):
        for off, SC in SC_CHUNKS:
            s0 = P_CAP * e_ + off
            for g in range(NDT):
                dst, dk = xs_ap(g, SC)
                h_dmaT(ctx, dst, sbuf_d[s0:s0 + SC, 128 * g:128 * (g + 1)],
                       [dk], [("sbuf_d",)], queue="qscal")
            if FFN1_STOP < 4:
                continue
            for fs in range(32):
                wb = wb0 if fs % 2 == 0 else wb1
                wbk = ("wband0",) if fs % 2 == 0 else ("wband1",)
                h_dma(ctx, wb[:, 0:1024].rearrange("p (g c) -> p g c", g=8),
                      w1bf[e_][:, 128 * fs:128 * (fs + 1)].rearrange(
                          "(g p) c -> p g c", p=128),
                      [wbk], [("w1bf", (e_, e_ + 1))],
                      queue="qsync" if fs % 2 == 0 else "qgp")
                pp = p_a if fs % 2 == 0 else p_b
                pk = ("p_a",) if fs % 2 == 0 else ("p_b",)
                h_mm(ctx, [(wb[:, 128 * g:128 * (g + 1)], xs_ap(g, SC)[0])
                           for g in range(NDT)],
                     pp[:, 0:SC], pk,
                     [wbk] + [xs_ap(g, SC)[1] for g in range(NDT)])
                ctx.s.op("scalar",
                         lambda e, pp=pp, SC=SC: e.activation(
                             scr[:, 0:SC], pp[:, 0:SC], AF.Square),
                         reads=(pk,), writes=(KEY("scr_a", 0, SC),))
                h_v(ctx, lambda e, SC=SC: e.tensor_scalar(
                    scr[:, 0:SC], scr[:, 0:SC], GA, 1.0, ALU.mult, ALU.add),
                    [KEY("scr_a", 0, SC)], [KEY("scr_a", 0, SC)])
                h_v(ctx, lambda e, pp=pp, SC=SC: e.tensor_tensor(
                    scr[:, 768:768 + SC], scr[:, 0:SC], pp[:, 0:SC],
                    ALU.mult),
                    [KEY("scr_a", 768, 768 + SC)],
                    [KEY("scr_a", 0, SC), pk])
                ctx.s.op("scalar",
                         lambda e, SC=SC: e.activation(
                             scr[:, 768:768 + SC], scr[:, 768:768 + SC],
                             AF.Tanh, scale=SQ2PI),
                         reads=(KEY("scr_a", 768, 768 + SC),),
                         writes=(KEY("scr_a", 768, 768 + SC),))
                def g5(e, pp=pp, fs=fs, SC=SC):
                    return e.scalar_tensor_tensor(
                        bigb[:, SC * fs:SC * (fs + 1)], scr[:, 768:768 + SC],
                        1.0, pp[:, 0:SC], ALU.add, ALU.mult)
                h_v(ctx, g5, [("bigb_bf", (SC * fs, SC * (fs + 1)))],
                    [KEY("scr_a", 768, 768 + SC), pk])
            if FFN1_STOP < 5:
                continue
            ngrp = SC // 128
            PTS = [(p_c, ("p_c",), 0), (p_d, ("p_d",), 0),
                   (p_st, ("p_stat",), 0), (p_bc, ("p_bc",), 0),
                   (p_st, ("p_stat",), 512), (p_bc, ("p_bc",), 512)]
            for fs in range(32):
                wb = wb0 if fs % 2 == 0 else wb1
                wbk = ("wband0",) if fs % 2 == 0 else ("wband1",)
                h_dma(ctx, wb[:, 0:1024], w2bf[e_][128 * fs:128 * (fs + 1), :],
                      [wbk], [("w2bf", (e_, e_ + 1))],
                      queue="qsync" if fs % 2 == 0 else "qgp")
                for gi in range(ngrp):
                    for dh in range(2):
                        pt, ptk, pc0 = PTS[2 * gi + dh]
                        def fy(e, fs=fs, gi=gi, pt=pt, pc0=pc0, wb=wb, SC=SC):
                            return e.matmul(
                                pt[0:128, pc0:pc0 + 512],
                                bigb[:, SC * fs + 128 * gi:
                                     SC * fs + 128 * (gi + 1)],
                                wb[:, 512 * dh:512 * (dh + 1)],
                                start=(fs == 0), stop=(fs == 31),
                                skip_group_check=True)
                        ctx.s.op("tensor", fy,
                                 reads=(wbk,
                                        ("bigb_bf", (SC * fs, SC * (fs + 1)))),
                                 writes=(ptk + ((pc0, pc0 + 512), (0, 128)),))
            for gi in range(ngrp):
                for dh in range(2):
                    pt, ptk, pc0 = PTS[2 * gi + dh]
                    h_v(ctx, lambda e, pt=pt, pc0=pc0, dh=dh: e.tensor_copy(
                        scr[:, 512 * dh:512 * (dh + 1)],
                        pt[0:128, pc0:pc0 + 512]),
                        [KEY("scr_a", 512 * dh, 512 * (dh + 1))],
                        [ptk + ((pc0, pc0 + 512), (0, 128))])
                def fst(e, gi=gi, s0=s0):
                    return e.dma_start(
                        out=yslotd[s0 + 128 * gi:s0 + 128 * (gi + 1), :],
                        in_=scr[:, 0:1024])
                ctx.s.op("sync", fst, reads=(KEY("scr_a", 0, 1024),),
                         writes=(("yslotd",),), queue="qsync")
    if FFN1_STOP < 7:
        return
    for tt in range(16):
        for cch in range(2):
            tgt = wrow[:, 1024 * cch:1024 * (cch + 1)]
            tk = KEY("wbuf_f32", 1024 * cch, 1024 * (cch + 1))
            h_v(ctx, lambda e, tgt=tgt: e.memset(tgt, 0.0), [tk], [tk])
            def fgb(e, tt=tt, cch=cch, tgt=tgt):
                return e.indirect_dma_start(
                    out=tgt, out_offset=None,
                    in_=yslotd[:, :],
                    in_offset=IndirectOffsetOnAxis(
                        ap=ctx.t["idx_sb"][:, 2 * tt + cch:2 * tt + cch + 1],
                        axis=0))
            ctx.s.op("gpsimd", fgb, reads=(("yslotd",), ("idx_sb",), tk),
                     writes=(tk,), queue="qgp")
        W0 = 128 + 4 * tt
        def fc1(e, W0=W0):
            return e.tensor_scalar(wrow[:, 0:1024], wrow[:, 0:1024],
                                   sm[:, W0:W0 + 1], None, ALU.mult)
        h_v(ctx, fc1, [KEY("wbuf_f32", 0, 1024)],
            [KEY("wbuf_f32", 0, 1024), ("smalltok", (W0, W0 + 2))])
        def fc2(e, W0=W0):
            return e.scalar_tensor_tensor(wrow[:, 0:1024], wrow[:, 1024:2048],
                                          sm[:, W0 + 1:W0 + 2],
                                          wrow[:, 0:1024], ALU.mult, ALU.add)
        h_v(ctx, fc2, [KEY("wbuf_f32", 0, 1024)],
            [KEY("wbuf_f32", 0, 2048), ("smalltok", (W0, W0 + 2))])
        for g in range(NDT):
            ctx.s.op("tensor",
                     lambda e, g=g: e.transpose(p_d[0:128, 0:128],
                                                wrow[:, 128 * g:128 * (g + 1)],
                                                idf[:, :]),
                     reads=(KEY("wbuf_f32", 0, 1024), ("ident_f32",)),
                     writes=(("p_d",),))
            xk = ("x_res", (g * T + 128 * tt, g * T + 128 * (tt + 1)))
            h_v(ctx, lambda e, g=g, tt=tt: e.scalar_tensor_tensor(
                x[:, xcols(g, 128 * tt, 128)], p_d[0:128, 0:128], 0.25,
                x[:, xcols(g, 128 * tt, 128)], ALU.mult, ALU.add),
                [xk], [("p_d",), xk])
    for g in range(NDT):
        h_v(ctx, lambda e, g=g: e.tensor_scalar(
            x[:, xcols(g)], x[:, xcols(g)],
            ctx.t["b1h_sb"][:, 8 + g:9 + g], None, ALU.add),
            [("x_res", (g * T, (g + 1) * T))],
            [("x_res", (g * T, (g + 1) * T)), ("b1h_sb",)])



STEP_SEQ = ["ffn0a", "attn0", "ffn0b",
            "ffn1a", "attn1", "ffn1b",
            "ffn2a", "attn2", "ffn2b",
            "ffn3a", "attn3", "ffn3b"]


def build_program(steps):
    nc = bass.Bass("TRN2", target_bir_lowering=False, debug=False)
    es = ExitStack()
    consts = ConstMgr()
    sched = Sched(nc)

    w = {}
    for name, shape, dtype in _input_specs(steps):
        w[name] = nc.dram_tensor(name, list(shape), dtype,
                                 kind="ExternalInput").ap()
    xout = nc.dram_tensor("out", [128, NDT * T], F32,
                          kind="ExternalOutput").ap()

    ctx = Ctx(nc, es, sched, consts, w)
    ctx.sb("x_res", (128, NDT * T), F32)
    ctx.sb("xln_bf", (128, NDT * T), BF16)
    ctx.sb("bigb_bf", (128, NDT * T), BF16)
    ctx.sb("scr_a", (128, 1536), F32)
    ctx.sb("wband0", (128, 4096), BF16)
    ctx.sb("wband1", (128, 4096), BF16)
    ctx.sb("wbuf_f32", (128, 4096), F32)
    ctx.sb("cols_sb", (128, 8), F32)
    ctx.sb("qtile_bf", (128, 2560), BF16)
    ctx.sb("rowpool", (128, 2048), F32)
    ctx.sb("ones_f32", (128, 128), F32)
    ctx.sb("ident_bf", (128, 128), BF16)
    ctx.sb("b1_sb", (128, 32), F32)
    ctx.sb("b2h_sb", (128, 8), F32)
    ctx.sb("cpool", (128, CPOOL_N), F32)
    if "attn1" in steps:
        ctx.sb("m1_sb", (128, 128), F32)
        ctx.sb("cw1_sb", (128, 96), F32)
        ctx.sb("tri_bf", (128, 128), BF16)
        ctx.sb("wbandv", (128, 1024), BF16)
        for nm, sh, dt_ in (("pool_bf_a", (128, 2048), BF16),
                            ("pool_bf_b", (128, 2048), BF16),
                            ("cmask_bf", (128, 896), BF16),
                            ("ptile_bf", (128, 512), BF16),
                            ("ones_bf", (128, 2), BF16),
                            ("msb_bf", (128, 1024), BF16),
                            ("smalltok", (128, 1024), F32),
                            ("ident_f32", (128, 128), F32)):
            if nm not in ctx.t:
                ctx.sb(nm, sh, dt_)
    if "attn3" in steps:
        ctx.sb("a3_sb", (128, 128), F32)
        ctx.sb("bias4_sb", (128, 32), F32)
        ctx.sb("convw_sb", (128, 32), F32)
        ctx.sb("bbc_sb", (128, 32), F32)
        ctx.sb("carry_sb", (128, 16), F32)
        for nm in ("pool_bf_a", "pool_bf_b"):
            if nm not in ctx.t:
                ctx.sb(nm, (128, 2048), BF16)
        if "smalltok" not in ctx.t:
            ctx.sb("smalltok", (128, 1024), F32)
        if "ident_f32" not in ctx.t:
            ctx.sb("ident_f32", (128, 128), F32)
    FFN3 = any(s.startswith("ffn3") for s in steps)
    if FFN3:
        for nm in ("pool_bf_a", "pool_bf_b"):
            if nm not in ctx.t:
                ctx.sb(nm, (128, 2048), BF16)
    FFN2 = any(s.startswith("ffn2") for s in steps)
    if FFN2:
        ctx.sb("sel16b", (16, 2048), BF16)
        ctx.sb("rsh_t", (128, 1), F32)
        if "smalltok" not in ctx.t:
            ctx.sb("smalltok", (128, 1024), F32)
        if "ident_f32" not in ctx.t:
            ctx.sb("ident_f32", (128, 128), F32)
        if "pool_bf_a" not in ctx.t:
            ctx.sb("pool_bf_a", (128, 2048), BF16)
    if "attn2" in steps:
        for nm in ("pool_bf_a", "pool_bf_b"):
            if nm not in ctx.t:
                ctx.sb(nm, (128, 2048), BF16)
        ctx.sb("cmask_bf", (128, 896), BF16)
        ctx.sb("ptile_bf", (128, 512), BF16)
        ctx.sb("ones_bf", (128, 2), BF16)
        if "msb_bf" not in ctx.t:
            ctx.sb("msb_bf", (128, 1024), BF16)
    if "attn0" in steps:
        if "smalltok" not in ctx.t:
            ctx.sb("smalltok", (128, 1024), F32)
        if "msb_bf" not in ctx.t:
            ctx.sb("msb_bf", (128, 1024), BF16)
        ctx.sb("sel16", (16, 2048), F32)
        if "ident_f32" not in ctx.t:
            ctx.sb("ident_f32", (128, 128), F32)
        ctx.sb("wtf1_bf", (64, 32), BF16)
        ctx.sb("bfd_t", (128, 16), F32)
        ctx.sb("bsd_t", (128, 16), F32)
        ctx.sb("btf2_t", (128, 1), F32)

    FFN1 = any(s.startswith("ffn1") for s in steps)
    if FFN1:
        ctx.sb("iota8_sb", (128, 16), F32)
        if "smalltok" not in ctx.t:
            ctx.sb("smalltok", (128, 1024), F32)
        if "ident_f32" not in ctx.t:
            ctx.sb("ident_f32", (128, 128), F32)
    if any(s in ("ffn1a", "ffn1b") for s in steps):
        ctx.sb("cnt_sb", (8, 32), F32)
        ctx.sb("idxf_sb", (128, 32), F32)
        ctx.sb("idx_sb", (128, 32), I32)
        ctx.sb("b1h_sb", (128, 16), F32)
        ctx.sb("invrow_sb", (1, 512), F32)
        for nm in ("pool_bf_a", "pool_bf_b"):
            if nm not in ctx.t:
                ctx.sb(nm, (128, 2048), BF16)

    ctx.ps("p_stat", 1024)
    ctx.ps("p_bc", 1024)
    ctx.ps("p_a", 512)
    ctx.ps("p_b", 512)
    ctx.ps("p_c", 512)
    ctx.ps("p_d", 512)

    with nc.Block() as block:
        sched.setup(es, block)
        h_dma(ctx, ctx.t["ones_f32"][:, :], w["ones_f32_in"][:, :],
              [("ones_f32",)])
        h_dma(ctx, ctx.t["cpool"][:, :], w["cpool_in"][:, :], [("cpool",)])
        h_dma(ctx, ctx.t["ident_bf"][:, :], w["ident_bf_in"][:, :],
              [("ident_bf",)], queue="qgp")
        h_dma(ctx, ctx.t["x_res"][:, :], w["xfm"][:, :], [("x_res",)])
        if any(s.startswith("ffn0") for s in steps):
            h_dma(ctx, ctx.t["b1_sb"][:, :], w["L0_ffn_b1r"][:, :],
                  [("b1_sb",)])
            h_dma(ctx, ctx.t["b2h_sb"][:, :], w["L0_ffn_b2r"][:, :],
                  [("b2h_sb",)])
        if any(s.startswith("ffn2") for s in steps):
            h_dma(ctx, ctx.t["sel16b"][:, :], w["sel16b_in"][:, :],
                  [("sel16b",)], queue="qgp")
            h_dma(ctx, ctx.t["rsh_t"][:, :], w["rsh_in"][:, :], [("rsh_t",)])
            h_v(ctx, lambda e: e.tensor_scalar_mul(
                ctx.t["rsh_t"][:, :], ctx.t["rsh_t"][:, :], 0.5),
                [("rsh_t",)], [("rsh_t",)])
            if "ident_f32_in" in w and "attn0" not in steps and "attn2" in steps:
                pass
        if "attn2" in steps:
            h_dma(ctx, ctx.t["cmask_bf"][:, :], w["cmask_in"][:, :],
                  [("cmask_bf",)], queue="qgp")
            h_dma(ctx, ctx.t["ones_bf"][:, :], w["ones_bf_in"][:, :],
                  [("ones_bf",)], queue="qgp")
        if "attn1" in steps:
            for nm in ("m1_sb", "cw1_sb"):
                h_dma(ctx, ctx.t[nm][:, :], w[nm + "_in"][:, :], [(nm,)])
            h_dma(ctx, ctx.t["tri_bf"][:, :], w["tri_in"][:, :], [("tri_bf",)],
                  queue="qgp")
            if "cmask_in" in w:
                h_dma(ctx, ctx.t["cmask_bf"][:, :], w["cmask_in"][:, :],
                      [("cmask_bf",)], queue="qgp")
            if "ones_bf_in" in w:
                h_dma(ctx, ctx.t["ones_bf"][:, :], w["ones_bf_in"][:, :],
                      [("ones_bf",)], queue="qgp")
        if "attn3" in steps:
            for nm in ("a3_sb", "bias4_sb", "convw_sb", "bbc_sb"):
                h_dma(ctx, ctx.t[nm][:, :], w[nm + "_in"][:, :], [(nm,)])
        if "ident_f32" in ctx.t and "attn0" not in steps:
            h_dma(ctx, ctx.t["ident_f32"][:, :], w["ident_f32_in"][:, :],
                  [("ident_f32",)])
        if any(s.startswith("ffn1") for s in steps):
            h_dma(ctx, ctx.t["iota8_sb"][:, :], w["iota8_in"][:, :],
                  [("iota8_sb",)])
        if any(s in ("ffn1a", "ffn1b") for s in steps):
            h_dma(ctx, ctx.t["b1h_sb"][:, 0:8], w["b1f_in"][:, :],
                  [("b1h_sb",)])
            h_dma(ctx, ctx.t["invrow_sb"][:, :], w["invrow_in"][:, :],
                  [("invrow_sb",)])
            h_v(ctx, lambda e: e.tensor_scalar_mul(
                ctx.t["b1h_sb"][:, 8:16], ctx.t["b1h_sb"][:, 0:8], 0.5),
                [("b1h_sb",)], [("b1h_sb",)])
        if "attn0" in steps and ATTN0_STOP > -1:
            for nm in ("sel16", "ident_f32", "bfd_t", "bsd_t", "btf2_t"):
                if nm == "sel16" and ATTN0_STOP == -0.5:
                    continue
                h_dma(ctx, ctx.t[nm][:, :], w[nm + "_in"][:, :], [(nm,)])
        sched.barrier_all()
        if any(s.startswith("ffn0") for s in steps):
            h_v(ctx, lambda e: e.tensor_scalar_mul(
                ctx.t["b2h_sb"][:, :], ctx.t["b2h_sb"][:, :], 0.5),
                [("b2h_sb",)], [("b2h_sb",)])
            build_l0ffn_prep(ctx)
        for st in steps:
            if st in ("ffn0a", "ffn0b"):
                build_ffn0(ctx)
            elif st == "attn0":
                build_attn0(ctx)
            elif st == "attn2":
                build_attn2(ctx)
            elif st in ("ffn2a", "ffn2b"):
                build_ffn2(ctx)
            elif st == "attn1":
                build_attn1(ctx)
            elif st == "attn3":
                build_attn3(ctx)
            elif st in ("ffn3a", "ffn3b"):
                build_ffn3(ctx, st == "ffn3a" or "ffn3a" not in steps)
            elif st in ("ffn1ar", "ffn1br"):
                build_ffn1_route(ctx, emit_counts=True)
            elif st in ("ffn1a", "ffn1b"):
                build_ffn1_route(ctx, emit_counts=False)
                build_ffn1_ffn(ctx)
            else:
                raise ValueError("step not implemented: " + st)
        h_dma(ctx, xout[:, :], ctx.t["x_res"][:, :],
              [("out",)], [("x_res",)])
        if DBG_DUMP == "wrows":
            dbg = nc.dram_tensor("dbg_wrows", [16, 2048], BF16,
                                 kind="ExternalOutput").ap()
            h_dma(ctx, dbg[:, :], ctx.t["pool_bf_a"][0:16, 0:2048],
                  [("dbgw",)], [("pool_bf_a",)])
            dbg2 = nc.dram_tensor("dbg_probs", [128, 256], F32,
                                  kind="ExternalOutput").ap()
            h_dma(ctx, dbg2[:, :], ctx.t["smalltok"][:, 0:256],
                  [("dbgp",)], [("smalltok",)])
        if DBG_DUMP == "bigb":
            dbg = nc.dram_tensor("dbg_bigb", [128, NDT * T], BF16,
                                 kind="ExternalOutput").ap()
            h_dma(ctx, dbg[:, :], ctx.t["bigb_bf"][:, :],
                  [("dbgb",)], [("bigb_bf",)])
        if DBG_DUMP and "w1qt" in ctx.dram:
            dq1 = nc.dram_tensor("dbg_q1", [D, 4 * D], BF16,
                                 kind="ExternalOutput").ap()
            dq2 = nc.dram_tensor("dbg_q2", [4 * D, D], BF16,
                                 kind="ExternalOutput").ap()
            h_dma(ctx, dq1[:, :], ctx.dram["w1qt"].ap()[:, :],
                  [("dbgq1",)], [("q1",)])
            h_dma(ctx, dq2[:, :], ctx.dram["w2qt"].ap()[:, :],
                  [("dbgq2",)], [("q2",)])
        sched.final_wait()
    es.close()

    assert ctx.sbuf_bytes < 208 * 1024, ctx.sbuf_bytes
    return nc, consts.pool(CPOOL_N)


def _input_specs(steps):
    specs = [
        ("xfm", (128, NDT * T), F32),
        ("ones_f32_in", (128, 128), F32),
        ("ident_bf_in", (128, 128), F32),
        ("cpool_in", (128, CPOOL_N), F32),
    ]
    if any(s.startswith("ffn0") for s in steps):
        specs += [
            ("L0_ffn_W1T", (D, 4 * D), F32),
            ("L0_ffn_W2T", (4 * D, D), F32),
            ("L0_ffn_b1r", (128, 32), F32),
            ("L0_ffn_b2r", (128, 8), F32),
        ]
    if "attn1" in steps:
        specs += [
            ("L1_attn_in_proj_w", (D, 4256), F32),
            ("L1_attn_out_proj_w", (2048, D), F32),
            ("L1_attn_attn_w", (D, 1152), F32),
            ("L1_attn_proj_w", (D, D), F32),
            ("L1_attn_key_w", (D, 4 * D), F32),
            ("L1_attn_recept_w", (D, D), F32),
            ("L1_attn_value_w", (4 * D, D), F32),
            ("m1_sb_in", (128, 128), F32), ("cw1_sb_in", (128, 96), F32),
            ("tri_in", (128, 128), F32),
        ]
        if "attn2" not in steps:
            specs += [("cmask_in", (128, 896), F32),
                      ("ones_bf_in", (128, 2), F32)]
        if not any(s in ("attn0", "attn2", "attn3") or s.startswith("ffn2")
                   for s in steps):
            specs += [("ident_f32_in", (128, 128), F32)]
    if "attn3" in steps:
        specs += [
            ("L3_attn_Win", (D, D), F32), ("L3_attn_Wg", (D, D), F32),
            ("L3_attn_Wout", (D, D), F32), ("L3_attn_Wdt", (D, D), F32),
            ("L3_attn_Wbc", (D, 32), F32),
            ("a3_sb_in", (128, 128), F32), ("bias4_sb_in", (128, 32), F32),
            ("convw_sb_in", (128, 32), F32), ("bbc_sb_in", (128, 32), F32),
        ]
        if not any(s in ("attn0", "attn2") or s.startswith("ffn2")
                   for s in steps):
            specs += [("ident_f32_in", (128, 128), F32)]
    if any(s.startswith("ffn3") for s in steps):
        specs += [
            ("L3_ffn_cprojT", (16 * D, D), F32),
            ("L3_ffn_bd1", (4, 8, 128, 128), F32),
            ("L3_ffn_bd2", (4, 8, 128, 512), F32),
        ]
    if any(s.startswith("ffn2") for s in steps):
        specs += [
            ("L2_ffn_selT", (D, 16), F32),
            ("L2_ffn_keys", (16, D, 256), F32),
            ("L2_ffn_values", (16, 256, D), F32),
            ("sel16b_in", (16, 2048), F32),
            ("rsh_in", (128, 1), F32),
        ]
        if not any("attn0" in s or "attn2" in s for s in steps):
            specs += [("ident_f32_in", (128, 128), F32)]
    if "attn2" in steps:
        specs += [
            ("L2_attn_wqT", (D, D), F32), ("L2_attn_wkT", (D, D), F32),
            ("L2_attn_wvT", (D, D), F32), ("L2_attn_woT", (D, D), F32),
            ("L2_attn_aqT", (D, 16), F32), ("L2_attn_akT", (D, 16), F32),
            ("L2_attn_avT", (D, 16), F32), ("L2_attn_aoT", (D, 16), F32),
            ("L2_attn_b3", (16, 3, D), F32), ("L2_attn_boT", (16, D), F32),
            ("cmask_in", (128, 896), F32), ("ones_bf_in", (128, 2), F32),
        ]
    if any(s.startswith("ffn1") for s in steps):
        specs += [("L1_ffn_w_router", (D, 8), F32), ("iota8_in", (128, 16), F32)]
        if not any(n == "ident_f32_in" for n, _, _ in specs) \
                and "attn0" not in steps:
            specs += [("ident_f32_in", (128, 128), F32)]
    if any(s in ("ffn1a", "ffn1b") for s in steps):
        specs += [
            ("L1_ffn_w1", (8, D, 4 * D), F32),
            ("L1_ffn_w2", (8, 4 * D, D), F32),
            ("b1f_in", (128, 8), F32),
            ("invrow_in", (1, 512), F32),
            ("counts_in", (8, 8), F32),
            ("coremask_in", (8, 8), F32),
        ]
    if "attn0" in steps:
        specs += [
            ("L0_attn_Wq", (D, D), F32), ("L0_attn_Wk", (D, D), F32),
            ("L0_attn_Wv", (D, D), F32), ("L0_attn_Wo", (D, D), F32),
            ("L0_attn_Wbfs", (D, 48), F32), ("L0_attn_Wtf1", (64, 32), F32),
            ("sel16_in", (16, 2048), F32), ("ident_f32_in", (128, 128), F32),
            ("bfd_t_in", (128, 16), F32), ("bsd_t_in", (128, 16), F32),
            ("btf1_t_in", (128, 512), F32), ("wtf2_t_in", (128, 512), F32),
            ("btf2_t_in", (128, 1), F32),
        ]
    return specs


def to_fm(x2d):
    return np.ascontiguousarray(
        x2d.reshape(T, NDT, 128).transpose(2, 1, 0).reshape(128, NDT * T))


def from_fm(xfm):
    return np.ascontiguousarray(
        xfm.reshape(128, NDT, T).transpose(2, 1, 0).reshape(T, D))


def retile_vec(v):
    n = v.shape[0] // 128
    return np.ascontiguousarray(v.reshape(n, 128).T)


def stage_inputs(steps, inputs, consts_pool, x_percore=None, counts=None):
    base = {
        "ones_f32_in": np.ones((128, 128), np.float32),
        "ident_bf_in": np.eye(128, dtype=np.float32),
        "cpool_in": consts_pool,
    }
    if any(s.startswith("ffn1") for s in steps):
        base["L1_ffn_w_router"] = np.ascontiguousarray(
            inputs["L1_ffn_w_router"])
        io = np.zeros((128, 16), np.float32)
        io[:, 0:8] = np.arange(8, dtype=np.float32)[None, :]
        io[:, 8] = (np.arange(128) % 8).astype(np.float32)
        base["iota8_in"] = io
        base["ident_f32_in"] = np.eye(128, dtype=np.float32)
    if any(s in ("ffn1a", "ffn1b") for s in steps):
        base["L1_ffn_w1"] = np.ascontiguousarray(inputs["L1_ffn_w1"])
        base["L1_ffn_w2"] = np.ascontiguousarray(inputs["L1_ffn_w2"])
        base["b1f_in"] = retile_vec(
            np.asarray(inputs["L1_ffn_bias"]).reshape(-1))
        base["invrow_in"] = (8 * 640 + (np.arange(512) // 2) % 128).astype(
            np.float32).reshape(1, 512)
        base["counts_in"] = np.ascontiguousarray(counts.T)
    if "attn1" in steps:
        for nm in ("in_proj_w", "out_proj_w", "attn_w", "proj_w", "key_w",
                   "recept_w", "value_w"):
            base["L1_attn_" + nm] = np.ascontiguousarray(
                inputs["L1_attn_" + nm])
        m1 = np.zeros((128, 128), np.float32)
        m1[:, 0:32] = np.tile(inputs["L1_attn_dt_bias"][None, :], (128, 1))
        m1[0:32, 32] = inputs["L1_attn_A_log"]
        m1[:, 64:96] = np.tile(inputs["L1_attn_D"][None, :], (128, 1))
        m1[:, 96:112] = retile_vec(inputs["L1_attn_mnorm_w"])
        m1[:, 112:120] = retile_vec(inputs["L1_attn_time_maa_k"])
        m1[:, 120:128] = retile_vec(inputs["L1_attn_time_maa_r"])
        base["m1_sb_in"] = m1
        cw1 = np.zeros((128, 96), np.float32)
        cwr = inputs["L1_attn_conv_w"].reshape(17, 128, 4).transpose(1, 0, 2)
        cw1[:, 0:68] = cwr.reshape(128, 68)
        cw1[:, 68:85] = inputs["L1_attn_conv_b"].reshape(17, 128).T
        base["cw1_sb_in"] = cw1
        jj, ii = np.meshgrid(np.arange(128), np.arange(128), indexing="ij")
        base["tri_in"] = (jj <= ii).astype(np.float32)
        if "cmask_in" not in base:
            cm = (np.arange(896)[None, :] >= (np.arange(128)[:, None] + 384))
            base["cmask_in"] = cm.astype(np.float32)
            base["ones_bf_in"] = np.ones((128, 2), np.float32)
        base["ident_f32_in"] = np.eye(128, dtype=np.float32)
    if "attn3" in steps:
        for nm in ("Win", "Wg", "Wout", "Wdt"):
            base["L3_attn_" + nm] = np.ascontiguousarray(inputs["L3_attn_" + nm])
        base["L3_attn_Wbc"] = np.ascontiguousarray(np.concatenate(
            [inputs["L3_attn_Wb"], inputs["L3_attn_Wc"]], axis=1))
        base["a3_sb_in"] = np.ascontiguousarray(
            inputs["L3_attn_A"].reshape(8, 128, 16).transpose(1, 0, 2)
            .reshape(128, 128))
        b4 = np.zeros((128, 32), np.float32)
        b4[:, 0:8] = retile_vec(inputs["L3_attn_b_in"])
        b4[:, 8:16] = retile_vec(inputs["L3_attn_bg"])
        b4[:, 16:24] = retile_vec(inputs["L3_attn_bdt"])
        b4[:, 24:32] = retile_vec(inputs["L3_attn_bout"])
        base["bias4_sb_in"] = b4
        cw = np.zeros((128, 32), np.float32)
        cwr = inputs["L3_attn_convw"].reshape(8, 128, 3).transpose(1, 0, 2)
        cw[:, 0:24] = cwr.reshape(128, 24)
        cw[:, 24:32] = retile_vec(inputs["L3_attn_convb"])
        base["convw_sb_in"] = cw
        bbc = np.zeros((128, 32), np.float32)
        bbc[:, 0:16] = np.tile(inputs["L3_attn_bb"][None, :], (128, 1))
        bbc[:, 16:32] = np.tile(inputs["L3_attn_bc"][None, :], (128, 1))
        base["bbc_sb_in"] = bbc
        base["ident_f32_in"] = np.eye(128, dtype=np.float32)
    if any(s.startswith("ffn3") for s in steps):
        base["L3_ffn_cprojT"] = np.ascontiguousarray(inputs["L3_ffn_w_cproj"].T)
        mk = inputs["L3_ffn_rail_maker"]
        up = inputs["L3_ffn_rail_up_proj"]
        bd1 = np.zeros((4, 8, 128, 128), np.float32)
        bd2 = np.zeros((4, 8, 128, 512), np.float32)
        for h in range(4):
            for rg in range(8):
                for j in range(8):
                    r = 8 * rg + j
                    bd1[h, rg, 16 * j:16 * (j + 1), 16 * j:16 * (j + 1)] = mk[h, r]
                    bd2[h, rg, 16 * j:16 * (j + 1), 64 * j:64 * (j + 1)] = \
                        up[h, r].T
        base["L3_ffn_bd1"] = bd1
        base["L3_ffn_bd2"] = bd2
    if any(s.startswith("ffn2") for s in steps):
        base["L2_ffn_selT"] = np.ascontiguousarray(inputs["L2_ffn_expert_sel"].T)
        base["L2_ffn_keys"] = np.ascontiguousarray(inputs["L2_ffn_keys"])
        base["L2_ffn_values"] = np.ascontiguousarray(inputs["L2_ffn_values"])
        sel = np.zeros((16, 2048), np.float32)
        for hh in range(16):
            sel[hh, 128 * hh:128 * (hh + 1)] = 1.0
        base["sel16b_in"] = sel
        base["rsh_in"] = np.tile(
            inputs["L2_ffn_route_scale"].astype(np.float32).reshape(1, 1),
            (128, 1))
        base["ident_f32_in"] = np.eye(128, dtype=np.float32)
    if "attn2" in steps:
        for s_, d_ in (("wq", "wqT"), ("wk", "wkT"), ("wv", "wvT"),
                       ("wo", "woT")):
            base["L2_attn_" + d_] = np.ascontiguousarray(
                inputs["L2_attn_" + s_].T)
        for s_, d_ in (("aq", "aqT"), ("ak", "akT"), ("av", "avT"),
                       ("ao", "aoT")):
            base["L2_attn_" + d_] = np.ascontiguousarray(
                inputs["L2_attn_" + s_].T)
        base["L2_attn_b3"] = np.ascontiguousarray(np.stack(
            [inputs["L2_attn_bq"].T, inputs["L2_attn_bk"].T,
             inputs["L2_attn_bv"].T], axis=1)).astype(np.float32)
        base["L2_attn_boT"] = np.ascontiguousarray(
            inputs["L2_attn_bo"].T).astype(np.float32)
        cm = (np.arange(896)[None, :] >= (np.arange(128)[:, None] + 384))
        base["cmask_in"] = cm.astype(np.float32)
        base["ones_bf_in"] = np.ones((128, 2), np.float32)
    if "attn0" in steps:
        for nm in ("L0_attn_Wq", "L0_attn_Wk", "L0_attn_Wv", "L0_attn_Wo",
                   "L0_attn_Wtf1"):
            base[nm] = np.ascontiguousarray(inputs[nm])
        base["L0_attn_Wbfs"] = np.ascontiguousarray(np.concatenate(
            [inputs["L0_attn_Wb"], inputs["L0_attn_Wfd"], inputs["L0_attn_Wsd"]],
            axis=1))
        sel = np.zeros((16, 2048), np.float32)
        for hh in range(16):
            sel[hh, 128 * hh:128 * (hh + 1)] = 1.0
        base["sel16_in"] = sel
        base["ident_f32_in"] = np.eye(128, dtype=np.float32)
        base["bfd_t_in"] = np.tile(inputs["L0_attn_bfd"][None, :], (128, 1)).astype(np.float32)
        base["bsd_t_in"] = np.tile(inputs["L0_attn_bsd"][None, :], (128, 1)).astype(np.float32)
        base["btf1_t_in"] = np.tile(inputs["L0_attn_btf1"][None, :], (128, 16)).astype(np.float32)
        base["wtf2_t_in"] = np.tile(inputs["L0_attn_Wtf2"][:, 0][None, :], (128, 16)).astype(np.float32)
        base["btf2_t_in"] = np.tile(inputs["L0_attn_btf2"][None, :], (128, 1)).astype(np.float32)
    if any(s.startswith("ffn0") for s in steps):
        base["L0_ffn_W1T"] = np.ascontiguousarray(inputs["L0_ffn_W1"].T)
        base["L0_ffn_W2T"] = np.ascontiguousarray(inputs["L0_ffn_W2"].T)
        base["L0_ffn_b1r"] = retile_vec(inputs["L0_ffn_b1"])
        base["L0_ffn_b2r"] = retile_vec(inputs["L0_ffn_b2"])
    in_maps = []
    for c in range(8):
        m = dict(base)
        if x_percore is not None:
            m["xfm"] = np.ascontiguousarray(x_percore[c])
        else:
            m["xfm"] = to_fm(np.asarray(inputs["L0_attn_x"][c]))
        if any(s in ("ffn1a", "ffn1b") for s in steps):
            cm = np.zeros((8, 8), np.float32)
            cm[:, 0:c] = 1.0
            m["coremask_in"] = cm
        in_maps.append(m)
    return in_maps


def run_steps(steps, inputs, x_percore=None, counts=None):
    nc, pool = build_program(steps)
    in_maps = stage_inputs(steps, inputs, pool, x_percore, counts)
    res = run_bass_kernel_spmd(nc, in_maps, core_ids=list(range(8)))
    return res.results


def kernel(**inputs):
    r1 = run_steps(["ffn0a", "attn0", "ffn0b", "ffn1ar"], inputs)
    x1 = [r["out"] for r in r1]
    c1 = np.stack([r["counts_out"][:, 0] for r in r1])
    r2 = run_steps(["ffn1a", "attn1", "ffn1br"], inputs,
                   x_percore=x1, counts=c1)
    x2 = [r["out"] for r in r2]
    c2 = np.stack([r["counts_out"][:, 0] for r in r2])
    r3 = run_steps(["ffn1b", "ffn2a", "attn2", "ffn2b",
                    "ffn3a", "attn3", "ffn3b"], inputs,
                   x_percore=x2, counts=c2)
    return np.stack([from_fm(r["out"]) for r in r3], axis=0)
